# version 06 (final)
# baseline (speedup 1.0000x reference)
import contextlib
import numpy as np
import concourse.bass as bass
import concourse.mybir as mybir
from concourse.bass_utils import run_bass_kernel_spmd

F32 = mybir.dt.float32
F16 = mybir.dt.float16
I16 = mybir.dt.int16
I8 = mybir.dt.int8
U16 = mybir.dt.uint16
I32 = mybir.dt.int32
AF = mybir.ActivationFunctionType
ALU = mybir.AluOpType
AX = mybir.AxisListType

NCORE = 8
SEQ = 2048
NTOK = 4096
D = 1024
NH = 16
HD = 64
DFF0 = 3072
DFF1 = 4096
NT = NTOK // 128
NEG = -1e30
SERIALIZE_DMA = True


class Bld:
    ENGINES = ("sync", "tensor", "vector", "scalar", "gpsimd")

    def __init__(self, nc, stack):
        self.nc = nc
        self.ops = {e: [] for e in self.ENGINES}
        self.cnt = {}
        self.floor = {e: {} for e in self.ENGINES}
        self.lastw = {}
        self.readers = {}
        self.semh = {}
        for e in self.ENGINES:
            for p in ("E", "D"):
                name = p + e
                self.semh[name] = stack.enter_context(nc.semaphore(name))
                self.cnt[name] = 0

    def emit(self, eng, fn, reads=(), writes=(), war=(), dma=False, tag=""):
        need = {}

        def _need(c):
            if c is not None:
                need[c[0]] = max(need.get(c[0], 0), c[1])

        for r in reads:
            _need(self.lastw.get(r))
            if isinstance(r, tuple) and r and r[0] == "pb":
                for s, c in self.readers.get(r, {}).items():
                    _need((s, c))
        for w in tuple(writes) + tuple(war):
            _need(self.lastw.get(w))
            for s, c in self.readers.get(w, {}).items():
                _need((s, c))
        fl = self.floor[eng]
        waits = sorted((s, c) for s, c in need.items() if fl.get(s, 0) < c)
        for s, c in waits:
            fl[s] = c
        if fn is None:
            self.ops[eng].append((None, waits, None, 0, tag))
            return None
        sem = ("D" + eng) if dma else ("E" + eng)
        delta = 16 if dma else 1
        if dma and SERIALIZE_DMA:
            prev = self.cnt[sem]
            if prev > 0 and fl.get(sem, 0) < prev:
                waits.append((sem, prev))
                fl[sem] = prev
        self.cnt[sem] += delta
        comp = (sem, self.cnt[sem])
        self.ops[eng].append((fn, waits, sem, delta, tag))
        for w in writes:
            self.lastw[w] = comp
            self.readers[w] = {}
        for r in reads:
            self.readers.setdefault(r, {})[comp[0]] = comp[1]
        return comp

    def barrier(self):
        snap = dict(self.cnt)
        for e in self.ENGINES:
            fl = self.floor[e]
            waits = sorted((s, c) for s, c in snap.items() if c > 0 and fl.get(s, 0) < c)
            for s, c in waits:
                fl[s] = c
            self.ops[e].append((None, waits, None, 0, "barrier"))

    def finalize(self):
        def mk(ename):
            def run(eng):
                for fn, waits, sem, delta, tag in self.ops[ename]:
                    for s, c in waits:
                        eng.wait_ge(self.semh[s], c)
                    if fn is not None:
                        fn(eng).then_inc(self.semh[sem], delta)
            return run
        with self.nc.Block() as block:
            block.sync(mk("sync"))
            block.tensor(mk("tensor"))
            block.vector(mk("vector"))
            block.scalar(mk("scalar"))
            block.gpsimd(mk("gpsimd"))


def _host_constants():
    c = {}
    c["c_ident"] = np.eye(128, dtype=np.float32)
    c["c_ident16"] = np.eye(128, dtype=np.float16)
    jj, ii = np.meshgrid(np.arange(128), np.arange(128), indexing="ij")
    c["c_slmask"] = (jj < ii).astype(np.float32)
    c["c_maskbias"] = np.where(jj <= ii, 0.0, NEG).astype(np.float32)
    c["c_ones_row"] = np.ones((1, 128), dtype=np.float32)
    c["c_misc"] = np.zeros((128, 4), dtype=np.float32)
    c["c_misc"][:, 0] = 1e-5
    c["c_ones_col"] = np.ones((128, 1), dtype=np.float32)
    c["c_ones_all"] = np.ones((128, 128), dtype=np.float32)
    pos = (np.arange(NTOK) % SEQ).astype(np.float64)
    inv = 1.0 / (10000.0 ** (np.arange(0, 16, 2, dtype=np.float64) / 16))
    fr = np.outer(pos, inv)
    c["c_ropecos"] = np.cos(fr).astype(np.float32)
    c["c_ropesin"] = np.sin(fr).astype(np.float32)
    return c


def _prep_inputs(inputs):
    x = np.ascontiguousarray(np.asarray(inputs["L0_attn_hidden_states"], np.float32))
    shards = x.reshape(NCORE, 2, SEQ, D).reshape(NCORE, NTOK, D)
    com = {}
    com["wqkvT"] = np.ascontiguousarray(np.asarray(inputs["L0_attn_Wqkv"]).T)
    com["bqkv"] = np.asarray(inputs["L0_attn_bqkv"]).reshape(1, 3 * D)
    com["wdT"] = np.ascontiguousarray(np.asarray(inputs["L0_attn_Wd"]).T)
    com["bd"] = np.ascontiguousarray(np.asarray(inputs["L0_attn_bd"]).reshape(8, 128).T)
    com["w0g"] = np.ascontiguousarray(np.asarray(inputs["L0_ffn_w_gate"]))
    com["w0u"] = np.ascontiguousarray(np.asarray(inputs["L0_ffn_w_up"]))
    com["w0d"] = np.ascontiguousarray(np.asarray(inputs["L0_ffn_w_down"]))
    com["wq1T"] = np.ascontiguousarray(np.asarray(inputs["L1_attn_w_q"]).reshape(D, D).T)
    com["wk1T"] = np.ascontiguousarray(np.asarray(inputs["L1_attn_w_k"]).reshape(D, D).T)
    com["wv1T"] = np.ascontiguousarray(np.asarray(inputs["L1_attn_w_v"]).reshape(D, D).T)
    com["wo1"] = np.ascontiguousarray(np.asarray(inputs["L1_attn_w_o"]))
    com["w1gT"] = np.ascontiguousarray(np.asarray(inputs["L1_ffn_w_gate"]).T)
    com["w1uT"] = np.ascontiguousarray(np.asarray(inputs["L1_ffn_w_up"]).T)
    com["w1dT"] = np.ascontiguousarray(np.asarray(inputs["L1_ffn_w_down"]).T)
    com.update(_host_constants())
    return shards, com


def build_nc(dbg=()):
    nc = bass.Bass("TRN2", target_bir_lowering=False, debug=False)
    st = contextlib.ExitStack()

    din = {}
    def dram_in(name, shape, dt=F32):
        din[name] = nc.dram_tensor(name, shape, dt, kind="ExternalInput").ap()
        return din[name]

    x_in = dram_in("x", [NTOK, D])
    wqkvT = dram_in("wqkvT", [D, 3 * D])
    bqkv = dram_in("bqkv", [1, 3 * D])
    wdT = dram_in("wdT", [D, D])
    bd_in = dram_in("bd", [128, 8])
    w0g = dram_in("w0g", [DFF0, D])
    w0u = dram_in("w0u", [DFF0, D])
    w0d = dram_in("w0d", [D, DFF0])
    wq1T = dram_in("wq1T", [D, D])
    wk1T = dram_in("wk1T", [D, D])
    wv1T = dram_in("wv1T", [D, D])
    wo1 = dram_in("wo1", [D, D])
    w1gT = dram_in("w1gT", [D, DFF1])
    w1uT = dram_in("w1uT", [D, DFF1])
    w1dT = dram_in("w1dT", [DFF1, D])
    c_ident = dram_in("c_ident", [128, 128])
    c_ident16 = dram_in("c_ident16", [128, 128], F16)
    c_slmask = dram_in("c_slmask", [128, 128])
    c_maskbias = dram_in("c_maskbias", [128, 128])
    c_ones_row = dram_in("c_ones_row", [1, 128])
    c_misc = dram_in("c_misc", [128, 4])
    c_ones_col = dram_in("c_ones_col", [128, 1])
    c_ones_all = dram_in("c_ones_all", [128, 128])
    c_ropecos = dram_in("c_ropecos", [NTOK, 8])
    c_ropesin = dram_in("c_ropesin", [NTOK, 8])

    y_out = nc.dram_tensor("y", [NTOK, D], F32, kind="ExternalOutput").ap()
    scr = {}
    def dram_scr(name, shape, dt):
        scr[name] = nc.dram_tensor(name, shape, dt).ap()
        return scr[name]

    s_qhi = dram_scr("s_qhi", [D, NTOK], F16)
    s_qlo = dram_scr("s_qlo", [D, NTOK], F16)
    s_khi = dram_scr("s_khi", [D, NTOK], F16)
    s_klo = dram_scr("s_klo", [D, NTOK], F16)
    s_v = dram_scr("s_v", [NTOK, D], F32)
    s_xqhi = dram_scr("s_xqhi", [D, NTOK], F16)
    s_xqlo = dram_scr("s_xqlo", [D, NTOK], F16)
    s_a = dram_scr("s_a", [DFF0, NTOK], F32)
    s_b = dram_scr("s_b", [DFF0, NTOK], F32)
    s_attn0 = dram_scr("s_attn0", [D, NTOK], F32)
    s_ffn0 = dram_scr("s_ffn0", [D, NTOK], F32)
    s_afmh = dram_scr("s_afmh", [D, NTOK], F16)
    s_afml = dram_scr("s_afml", [D, NTOK], F16)
    s_x1 = dram_scr("s_x1", [NTOK, D], F32)
    cc_in = dram_scr("cc_in", [1, 8], F32)
    cc_out = dram_scr("cc_out", [1, 8], F32)

    dbg_out = {}
    for name, shape, dt in dbg:
        dbg_out[name] = nc.dram_tensor("dbg_" + name, list(shape), dt,
                                       kind="ExternalOutput").ap()

    def sb(name, shape, dt=F32):
        return st.enter_context(nc.sbuf_tensor(name, shape, dt))

    ident = sb("ident", [128, 128])
    ident16 = sb("ident16", [128, 128], F16)
    slmask = sb("slmask", [128, 128])
    maskbias = sb("maskbias", [128, 128])
    ones_row = sb("ones_row", [1, 128])
    misc = sb("misc", [128, 4])
    ones_col = sb("ones_col", [128, 1])
    ones_all = sb("ones_all", [128, 128])
    ropec = sb("ropec", [128, NT * 8])
    ropes = sb("ropes", [128, NT * 8])
    stats_m = sb("stats_m", [128, NT])
    stats_r = sb("stats_r", [128, NT])
    cells = sb("cells", [128, 32])
    gmax = sb("gmax", [1, 32])
    stats1_m = sb("stats1_m", [128, NT])
    stats1_r = sb("stats1_r", [128, NT])
    rsff = sb("rsff", [128, NT])
    rso = sb("rso", [128, NT])

    pb = [st.enter_context(nc.psum_tensor("pbk%d" % i, [128, 512], F32)) for i in range(8)]

    b = Bld(nc, st)
    E = b.emit

    def ld(dst, src, key, eng="sync"):
        E(eng, lambda e, d=dst, s=src: e.dma_start(out=d, in_=s),
          writes=[key], dma=True, tag="ld:" + key)

    ld(ident[:, :], c_ident[:, :], "ident")
    ld(ident16[:, :], c_ident16[:, :], "ident16")
    ld(slmask[:, :], c_slmask[:, :], "slmask")
    ld(maskbias[:, :], c_maskbias[:, :], "maskbias")
    ld(ones_row[:, :], c_ones_row[:, :], "ones_row")
    ld(misc[:, :], c_misc[:, :], "misc")
    ld(ones_col[:, :], c_ones_col[:, :], "ones_col")
    ld(ones_all[:, :], c_ones_all[:, :], "ones_all")
    ld(ropec[:, :].rearrange("p (t f) -> p t f", f=8),
       c_ropecos.rearrange("(t p) f -> p t f", p=128), "ropec")
    ld(ropes[:, :].rearrange("p (t f) -> p t f", f=8),
       c_ropesin.rearrange("(t p) f -> p t f", p=128), "ropes")

    def bcast_cells(src_ap, src_keys, dst_ap, dst_key, n, bank=1):
        E("tensor", lambda e: e.matmul(pb[bank][:, 0:n], ones_row[:, :], src_ap,
                                       start=True, stop=True),
          reads=list(src_keys) + ["ones_row"], writes=[("pb", bank)], tag="bc:mm")
        E("vector", lambda e: e.tensor_copy(dst_ap, pb[bank][:, 0:n]),
          reads=[("pb", bank)], writes=[dst_key], tag="bc:ev")

    with nc.sbuf_tensor("p0x", [128, 2, D], F32) as p0x, \
         nc.sbuf_tensor("p0tmp", [128, 2, D], F32) as p0tmp, \
         nc.sbuf_tensor("p0dump", [128, D], F32) as p0dump, \
         nc.sbuf_tensor("p0cols", [128, 4 * NT], F32) as p0cols, \
         nc.sbuf_tensor("p0mx", [128, NT], F32) as p0mx:
        for t in range(NT):
            pbuf = p0x[:, t % 2, :]
            tbuf = p0tmp[:, t % 2, :]
            kx = ("p0x", t % 2)
            kt = ("p0t", t % 2)
            E("sync", lambda e, d=pbuf, s=x_in[t * 128:(t + 1) * 128, :]:
              e.dma_start(out=d[:, :], in_=s), writes=[kx], dma=True, tag="p0:dma")
            sm = p0cols[:, t:t + 1]
            sq = p0cols[:, NT + t:NT + t + 1]
            E("vector", lambda e, d=sm, s=pbuf: e.reduce_sum(d, s[:, :], axis=AX.X),
              reads=[kx], writes=[("sm", t)], tag="p0:sum")
            E("scalar", lambda e, d=sq, s=pbuf, dd=p0dump:
              e.activation(dd[:, :], s[:, :], AF.Square, accum_out=d),
              reads=[kx], writes=[("sq", t), "p0dump"], tag="p0:sq")
            mcol = stats_m[:, t:t + 1]
            rcol = stats_r[:, t:t + 1]
            vcol = p0cols[:, 2 * NT + t:2 * NT + t + 1]
            E("vector", lambda e, d=mcol, s=sm: e.tensor_scalar_mul(d, s, 1.0 / D),
              reads=[("sm", t)], writes=[("m", t)], tag="p0:m")
            E("vector", lambda e, d=vcol, s=mcol: e.tensor_tensor(d, s, s, ALU.mult),
              reads=[("m", t)], writes=[("v", t)], tag="p0:msq")
            E("vector", lambda e, d=vcol, s=sq, m2=vcol:
              e.scalar_tensor_tensor(d, s, 1.0 / D, m2, ALU.mult, ALU.subtract),
              reads=[("sq", t), ("v", t)], writes=[("v", t)], tag="p0:var")
            E("scalar", lambda e, d=vcol, s=vcol:
              e.activation(d, s, AF.Sqrt, bias=misc[:, 0:1], scale=1.0),
              reads=[("v", t), "misc"], writes=[("v", t)], tag="p0:sqrt")
            E("vector", lambda e, d=rcol, s=vcol: e.reciprocal(d, s),
              reads=[("v", t)], writes=[("r", t)], tag="p0:recip")
            E("vector", lambda e, d=tbuf, s=pbuf, m=mcol:
              e.tensor_scalar(d[:, :], s[:, :], m, None, ALU.subtract),
              reads=[kx, ("m", t)], writes=[kt], tag="p0:sub")
            tm = p0cols[:, 3 * NT + t:3 * NT + t + 1]
            E("vector", lambda e, d=tm, s=tbuf:
              e.reduce_max(d, s[:, :], axis=AX.X, apply_absolute_value=True),
              reads=[kt], writes=[("tm", t)], tag="p0:rmax")
            E("vector", lambda e, d=p0mx[:, t:t + 1], s=tm, r=rcol:
              e.tensor_tensor(d, s, r, ALU.mult),
              reads=[("tm", t), ("r", t)], writes=[("mx", t)], tag="p0:mx")
        mxc = p0cols[:, 0:1]
        E("vector", lambda e, d=mxc, s=p0mx: e.reduce_max(d, s[:, :], axis=AX.X),
          reads=[("mx", t) for t in range(NT)], writes=["mxc"], tag="p0:gmax1")
        E("tensor", lambda e: e.transpose(pb[0][0:1, 0:128], mxc, ident[:, :]),
          reads=["mxc", "ident"], writes=[("pb", 0)], tag="p0:tr")
        E("vector", lambda e: e.reduce_max(gmax[0:1, 0:1], pb[0][0:1, 0:128], axis=AX.X),
          reads=[("pb", 0)], writes=["gmax0"], tag="p0:gmax2")
        E("gpsimd", lambda e: e.dma_start(out=cc_in[0:1, 0:1], in_=gmax[0:1, 0:1]),
          reads=["gmax0"], writes=["cc_in"], dma=True, tag="r1:out")
        E("gpsimd", lambda e: e.collective_compute(
            "AllReduce", ALU.max, replica_groups=[list(range(NCORE))],
            ins=[cc_in[0:1, 0:1].opt()], outs=[cc_out[0:1, 0:1].opt()]),
          reads=["cc_in"], writes=["cc_out"], war=["cc_out"], tag="r1:cc")
        E("gpsimd", lambda e: e.dma_start(out=gmax[0:1, 1:2], in_=cc_out[0:1, 0:1]),
          reads=["cc_out"], writes=["gmax1"], dma=True, tag="r1:in")
        E("vector", lambda e: e.tensor_scalar(gmax[0:1, 2:3], gmax[0:1, 1:2],
                                              1.0 / 32767.0, 1e-12, ALU.mult, ALU.max),
          reads=["gmax1"], writes=["s1c"], tag="r1:s1")
        E("vector", lambda e: e.reciprocal(gmax[0:1, 3:4], gmax[0:1, 2:3]),
          reads=["s1c"], writes=["r1c"], tag="r1:recip")
        bcast_cells(gmax[0:1, 2:4], ["s1c", "r1c"], cells[:, 1:3], "cells_s1r1", 2)
        if "gmax" in dbg_out:
            E("sync", lambda e: e.dma_start(out=dbg_out["gmax"][:, :], in_=gmax[0:1, 0:4]),
              reads=["gmax0", "gmax1", "s1c", "r1c"], writes=["dbg"], dma=True)
        b.barrier()

    w0scope = contextlib.ExitStack()
    wqkv_hi = w0scope.enter_context(nc.sbuf_tensor("wqkv_hi", [128, 8, 3 * D], F16))
    wqkv_lo = w0scope.enter_context(nc.sbuf_tensor("wqkv_lo", [128, 8, 3 * D], F16))
    with nc.sbuf_tensor("w0tile", [128, 2, 3 * D], F32) as w0tile:
        for c in range(8):
            kw = ("w0t", c % 2)
            buf = w0tile[:, c % 2, :]
            E("sync", lambda e, d=buf, s=wqkvT[c * 128:(c + 1) * 128, :]:
              e.dma_start(out=d[:, :], in_=s), writes=[kw], dma=True, tag="w0:dma")
            E("vector", lambda e, d=wqkv_hi[:, c, :], s=buf: e.tensor_copy(d, s[:, :]),
              reads=[kw], writes=[("wqh", c)], tag="w0:hi")
            E("vector", lambda e, d=wqkv_lo[:, c, :], s=buf, h=wqkv_hi[:, c, :]:
              e.tensor_tensor(d, s[:, :], h, ALU.subtract),
              reads=[kw, ("wqh", c)], writes=[("wql", c)], tag="w0:lo")
        b.barrier()

    with nc.sbuf_tensor("p1x", [128, 2, D], F32) as p1x, \
         nc.sbuf_tensor("p1xn", [128, 4, D], F32) as p1xn, \
         nc.sbuf_tensor("xnfm", [128, 8, 512], F32) as xnfm, \
         nc.sbuf_tensor("xnfm_hi", [128, 8, 512], F16) as xnfm_hi, \
         nc.sbuf_tensor("xnfm_lo", [128, 8, 512], F16) as xnfm_lo, \
         nc.sbuf_tensor("xq16", [128, 512], I32) as xq16, \
         nc.sbuf_tensor("xqsh", [128, 512], I32) as xqsh, \
         nc.sbuf_tensor("xqcl", [128, 512], I32) as xqcl, \
         nc.sbuf_tensor("xqhi", [128, 512], F16) as xqhi, \
         nc.sbuf_tensor("xqlo", [128, 512], F16) as xqlo, \
         nc.sbuf_tensor("biasbc", [128, 3 * D], F32) as biasbc, \
         nc.sbuf_tensor("bqsb", [1, 3 * D], F32) as bqsb, \
         nc.sbuf_tensor("qkvst", [128, 4 * 512], F32) as qkvst, \
         nc.sbuf_tensor("ropet", [128, 4, 256], F32) as ropet, \
         nc.sbuf_tensor("sthi", [128, 2, 512], F16) as sthi, \
         nc.sbuf_tensor("stlo", [128, 2, 512], F16) as stlo:

        ld(bqsb[:, :], bqkv[:, :], "bqsb")
        for ot in range(6):
            bcast_cells(bqsb[0:1, ot * 512:(ot + 1) * 512], ["bqsb"],
                        biasbc[:, ot * 512:(ot + 1) * 512], ("biasbc", ot), 512, bank=7)

        for sup in range(8):
            for tt in range(4):
                t = sup * 4 + tt
                kx = ("p1x", t % 2)
                E("sync", lambda e, d=p1x[:, t % 2, :], s=x_in[t * 128:(t + 1) * 128, :]:
                  e.dma_start(out=d[:, :], in_=s), writes=[kx], dma=True, tag="p1:dmax")
                kxn = ("p1xn", tt)
                E("vector", lambda e, d=p1xn[:, tt, :], s=p1x[:, t % 2, :],
                  m=stats_m[:, t:t + 1], r=stats_r[:, t:t + 1]:
                  e.tensor_scalar(d[:, :], s[:, :], m, r, ALU.subtract, ALU.mult),
                  reads=[kx, ("m", t), ("r", t)], writes=[kxn], tag="p1:xn")
                for c in range(8):
                    bank = c % 2 + 2
                    E("tensor", lambda e, tt=tt, c=c, bank=bank:
                      e.transpose(pb[bank][:, 0:128],
                                  p1xn[:, tt, c * 128:(c + 1) * 128], ident[:, :]),
                      reads=[("p1xn", tt), "ident"], writes=[("pb", bank)], tag="p1:tr")
                    E("vector", lambda e, tt=tt, c=c, bank=bank:
                      e.tensor_copy(xnfm[:, c, tt * 128:(tt + 1) * 128], pb[bank][:, 0:128]),
                      reads=[("pb", bank)], writes=[("xnfm", c, tt)], tag="p1:ev")
            for c in range(8):
                rk = [("xnfm", c, tt) for tt in range(4)]
                E("vector", lambda e, c=c: e.tensor_copy(xnfm_hi[:, c, :], xnfm[:, c, :]),
                  reads=rk, writes=[("xnhi", c)], tag="p1:hi")
                E("vector", lambda e, c=c: e.tensor_tensor(
                    xnfm_lo[:, c, :], xnfm[:, c, :], xnfm_hi[:, c, :], ALU.subtract),
                  reads=rk + [("xnhi", c)], writes=[("xnlo", c)], tag="p1:lo")
                E("vector", lambda e, c=c: e.tensor_scalar(
                    xq16[:, :], xnfm[:, c, :], cells[:, 2:3], None, ALU.mult),
                  reads=rk + ["cells_s1r1"], writes=["xq16"], tag="p1:xq")
                E("vector", lambda e: e.tensor_scalar(
                    xqcl[:, :], xq16[:, :], -32768.0, 32767.0, ALU.max, ALU.min),
                  reads=["xq16"], writes=["xqcl"], tag="p1:xqc")
                E("vector", lambda e: e.tensor_scalar(
                    xqsh[:, :], xqcl[:, :], 1.0 / 256.0, None, ALU.mult),
                  reads=["xqcl"], writes=["xqsh"], tag="p1:xqs")
                E("vector", lambda e: e.tensor_scalar(
                    xqhi[:, :], xqsh[:, :], 256.0, None, ALU.mult),
                  reads=["xqsh"], writes=["xqhi"], tag="p1:xqh")
                E("vector", lambda e: e.scalar_tensor_tensor(
                    xqlo[:, :], xqsh[:, :], -256.0, xqcl[:, :], ALU.mult, ALU.add),
                  reads=["xqsh", "xqcl"], writes=["xqlo"], tag="p1:xql")
                E("gpsimd", lambda e, c=c, sup=sup: e.dma_start(
                    out=s_xqhi[c * 128:(c + 1) * 128, sup * 512:(sup + 1) * 512],
                    in_=xqhi[:, :]),
                  reads=["xqhi"], writes=[("s_xqhi", c, sup)], dma=True, tag="p1:stxqh")
                E("gpsimd", lambda e, c=c, sup=sup: e.dma_start(
                    out=s_xqlo[c * 128:(c + 1) * 128, sup * 512:(sup + 1) * 512],
                    in_=xqlo[:, :]),
                  reads=["xqlo"], writes=[("s_xqlo", c, sup)], dma=True, tag="p1:stxql")

            xn_r = [("xnhi", c) for c in range(8)] + [("xnlo", c) for c in range(8)]
            for ot in range(6):
                for tt in range(4):
                    bank = tt % 2 + 4
                    kst = ("qkvst", tt)
                    for c in range(8):
                        for p, (sa, sw) in enumerate((("hi", "hi"), ("hi", "lo"), ("lo", "hi"))):
                            act = xnfm_hi if sa == "hi" else xnfm_lo
                            wgt = wqkv_hi if sw == "hi" else wqkv_lo
                            first = (c == 0 and p == 0)
                            lastm = (c == 7 and p == 2)
                            E("tensor", lambda e, act=act, wgt=wgt, c=c, tt=tt, ot=ot,
                              bank=bank, first=first, lastm=lastm:
                              e.matmul(pb[bank][:, 0:512],
                                       act[:, c, tt * 128:(tt + 1) * 128],
                                       wgt[:, c, ot * 512:(ot + 1) * 512],
                                       start=first, stop=lastm),
                              reads=(xn_r + [("wqh", c), ("wql", c)]),
                              writes=[("pb", bank)] if lastm else (),
                              war=[("pb", bank)] if first else (),
                              tag="p1:mm")
                    E("vector", lambda e, tt=tt, ot=ot, bank=bank: e.tensor_tensor(
                        qkvst[:, tt * 512:(tt + 1) * 512], pb[bank][:, 0:512],
                        biasbc[:, ot * 512:(ot + 1) * 512], ALU.add),
                      reads=[("pb", bank), ("biasbc", ot)], writes=[kst], tag="p1:qkvev")
                kst_all = [("qkvst", tt) for tt in range(4)]
                if ot < 4:
                    qv = qkvst.rearrange("p (T h d) -> p T h d", T=4, d=64)
                    r0 = qv[:, :, :, 0:8]
                    r1 = qv[:, :, :, 8:16]
                    rcv = ropec[:, sup * 32:(sup + 1) * 32] \
                        .rearrange("p (T f) -> p T f", f=8).unsqueeze(2) \
                        .broadcast_to([128, 4, 8, 8])
                    rsv = ropes[:, sup * 32:(sup + 1) * 32] \
                        .rearrange("p (T f) -> p T f", f=8).unsqueeze(2) \
                        .broadcast_to([128, 4, 8, 8])
                    t0 = ropet[:, 0, :].rearrange("p (T h f) -> p T h f", T=4, f=8)
                    t1 = ropet[:, 1, :].rearrange("p (T h f) -> p T h f", T=4, f=8)
                    u0 = ropet[:, 2, :].rearrange("p (T h f) -> p T h f", T=4, f=8)
                    u1 = ropet[:, 3, :].rearrange("p (T h f) -> p T h f", T=4, f=8)
                    E("vector", lambda e, r0=r0, rcv=rcv, t0=t0: e.tensor_tensor(
                        t0, r0, rcv, ALU.mult), reads=kst_all + ["ropec"],
                      writes=["rt0"], tag="p1:ro0")
                    E("vector", lambda e, r1=r1, rcv=rcv, t1=t1: e.tensor_tensor(
                        t1, r1, rcv, ALU.mult), reads=kst_all + ["ropec"],
                      writes=["rt1"], tag="p1:ro1")
                    E("vector", lambda e, r1=r1, rsv=rsv, u0=u0: e.tensor_tensor(
                        u0, r1, rsv, ALU.mult), reads=kst_all + ["ropes"],
                      writes=["ru0"], tag="p1:ro2")
                    E("vector", lambda e, r0=r0, rsv=rsv, u1=u1: e.tensor_tensor(
                        u1, r0, rsv, ALU.mult), reads=kst_all + ["ropes"],
                      writes=["ru1"], tag="p1:ro3")
                    E("vector", lambda e, r0=r0, t0=t0, u0=u0: e.tensor_tensor(
                        r0, t0, u0, ALU.subtract), reads=["rt0", "ru0"],
                      writes=kst_all, tag="p1:ro4")
                    E("vector", lambda e, r1=r1, t1=t1, u1=u1: e.tensor_tensor(
                        r1, t1, u1, ALU.add), reads=["rt1", "ru1"],
                      writes=kst_all, tag="p1:ro5")
                    for osub in range(4):
                        par = osub % 2
                        khi = ("sthi", par)
                        klo = ("stlo", par)
                        bank2 = 6 + par
                        for tt in range(4):
                            E("tensor", lambda e, tt=tt, osub=osub, bank2=bank2:
                              e.transpose(pb[bank2][:, tt * 128:(tt + 1) * 128],
                                          qkvst[:, tt * 512 + osub * 128:
                                                tt * 512 + (osub + 1) * 128],
                                          ident[:, :]),
                              reads=kst_all + ["ident"],
                              writes=[("pb", bank2)] if tt == 3 else (),
                              war=[("pb", bank2)] if tt == 0 else (),
                              tag="p1:qktr")
                        E("vector", lambda e, bank2=bank2, par=par: e.tensor_copy(
                            sthi[:, par, :], pb[bank2][:, 0:512]),
                          reads=[("pb", bank2)], writes=[khi], tag="p1:sthi")
                        E("vector", lambda e, bank2=bank2, par=par: e.tensor_tensor(
                            stlo[:, par, :], pb[bank2][:, 0:512], sthi[:, par, :],
                            ALU.subtract),
                          reads=[("pb", bank2), khi], writes=[klo], tag="p1:stlo")
                        oc = ot * 4 + osub
                        if oc < 8:
                            dhi, dlo, row = s_qhi, s_qlo, oc * 128
                        else:
                            dhi, dlo, row = s_khi, s_klo, (oc - 8) * 128
                        E("gpsimd", lambda e, d=dhi, row=row, par=par, sup=sup: e.dma_start(
                            out=d[row:row + 128, sup * 512:(sup + 1) * 512],
                            in_=sthi[:, par, :]),
                          reads=[khi], writes=[("sqk", oc, sup, "h")], dma=True,
                          tag="p1:dmah")
                        E("gpsimd", lambda e, d=dlo, row=row, par=par, sup=sup: e.dma_start(
                            out=d[row:row + 128, sup * 512:(sup + 1) * 512],
                            in_=stlo[:, par, :]),
                          reads=[klo], writes=[("sqk", oc, sup, "l")], dma=True,
                          tag="p1:dmal")
                else:
                    for tt in range(4):
                        t = sup * 4 + tt
                        E("gpsimd", lambda e, tt=tt, t=t, ot=ot: e.dma_start(
                            out=s_v[t * 128:(t + 1) * 128,
                                    (ot - 4) * 512:(ot - 3) * 512],
                            in_=qkvst[:, tt * 512:(tt + 1) * 512]),
                          reads=[("qkvst", tt)], writes=[("s_v", t, ot)], dma=True,
                          tag="p1:dmav")
        b.barrier()


    b.barrier()
    w0scope.close()
    with nc.sbuf_tensor("qk4", [65, 4, SEQ], F16) as qk4, \
         nc.sbuf_tensor("vbuf", [128, 16, HD + 1], F32) as vbuf, \
         nc.sbuf_tensor("estrip", [128, 2, SEQ], F32) as estrip, \
         nc.sbuf_tensor("abuf", [128, 16, 65], F32) as abuf, \
         nc.sbuf_tensor("acol", [128, 2], F32) as acol, \
         nc.sbuf_tensor("dtile", [128, 128], F32) as dtile, \
         nc.sbuf_tensor("recs", [128, 1], F32) as recs, \
         nc.sbuf_tensor("stile", [128, 2, 128], F32) as stile, \
         nc.sbuf_tensor("ntile", [128, 2, 128], F32) as ntile, \
         nc.sbuf_tensor("btile", [128, 2, 64], F32) as btile, \
         nc.sbuf_tensor("afmst", [64, 2, SEQ], F16) as afmst:
        E("vector", lambda e: e.memset(vbuf[:, :, 64:65], 0.0),
          writes=["vzero"], tag="a0:vz")
        for sh in range(32):
            s, h = sh // 16, sh % 16
            for slot, src_t in ((0, s_qhi), (1, s_qlo), (2, s_khi), (3, s_klo)):
                E("sync", lambda e, slot=slot, src_t=src_t, s=s, h=h: e.dma_start(
                    out=qk4[0:64, slot, :],
                    in_=src_t[h * 64:(h + 1) * 64, s * SEQ:(s + 1) * SEQ]),
                  writes=[("qk", slot)], dma=True, tag="a0:ldqk")
            E("vector", lambda e: e.memset(qk4[64:65, 2, :], 1.0),
              writes=[("qka", 2)], war=[("qka", 2)], tag="a0:aug1")
            E("vector", lambda e: e.memset(qk4[64:65, 3, :], 0.0),
              writes=[("qka", 3)], war=[("qka", 3)], tag="a0:aug0")
            E("sync", lambda e, s=s, h=h: e.dma_start(
                out=vbuf[:, :, 0:64],
                in_=s_v[s * SEQ:(s + 1) * SEQ, h * 64:(h + 1) * 64]
                    .rearrange("(j p) c -> p j c", p=128)),
              writes=["vbuf"], dma=True, tag="a0:ldv")
            E("vector", lambda e: e.memset(abuf[:, :, 64:65], 1.0),
              war=[("A", j) for j in range(16)], writes=["Aones"], tag="a0:ones")
            for i in range(16):
                par = i % 2
                j0 = (i - 1) * 128 if i > 0 else 0
                w = 256 if i > 0 else 128
                E("tensor", lambda e, i=i, j0=j0, w=w: e.matmul(
                    pb[6][:, 0:w], qk4[0:64, 0, i * 128:(i + 1) * 128],
                    qk4[0:64, 2, j0:j0 + w], start=True, stop=True),
                  reads=[("qk", 0), ("qk", 2)], writes=[("pb", 6)], tag="a0:sbm")
                E("vector", lambda e, w=w: e.reduce_max(
                    acol[:, 0:1], pb[6][:, 0:w], axis=AX.X),
                  reads=[("pb", 6)], writes=["acol0"], tag="a0:sbr")
                E("vector", lambda e: e.tensor_scalar(
                    acol[:, 1:2], acol[:, 0:1], 320.0, -1.0, ALU.add, ALU.mult),
                  reads=["acol0"], writes=["acol1"], tag="a0:sba")
                E("tensor", lambda e: e.matmul(
                    pb[6][64:65, 0:128], acol[:, 1:2], ident[:, :],
                    start=True, stop=True),
                  reads=["acol1", "ident"], writes=[("pb", 6)], tag="a0:sbt")
                E("vector", lambda e, i=i: e.tensor_copy(
                    qk4[64:65, 0, i * 128:(i + 1) * 128], pb[6][64:65, 0:128]),
                  reads=[("pb", 6)], writes=[("qka", 0)], war=[("qka", 0)],
                  tag="a0:sbh")
                E("vector", lambda e, i=i: e.tensor_tensor(
                    qk4[64:65, 1, i * 128:(i + 1) * 128], pb[6][64:65, 0:128],
                    qk4[64:65, 0, i * 128:(i + 1) * 128], ALU.subtract),
                  reads=[("pb", 6), ("qka", 0)], writes=[("qka", 1)],
                  war=[("qka", 1)], tag="a0:sbl")
                for j in range(i + 1):
                    bank = j % 2
                    for p3, (aa, bb2) in enumerate(((0, 2), (1, 2), (0, 3))):
                        E("tensor", lambda e, j=j, i=i, bank=bank, p3=p3, aa=aa, bb2=bb2:
                          e.matmul(pb[bank][:, 0:128],
                                   qk4[:, bb2, j * 128:(j + 1) * 128],
                                   qk4[:, aa, i * 128:(i + 1) * 128],
                                   start=(p3 == 0), stop=(p3 == 2)),
                          reads=[("qk", aa), ("qk", bb2), ("qka", aa), ("qka", bb2)],
                          writes=[("pb", bank)] if p3 == 2 else (),
                          war=[("pb", bank)] if p3 == 0 else (), tag="a0:qk")
                    if j == i:
                        E("vector", lambda e, j=j, par=par, bank=bank: e.tensor_tensor(
                            estrip[:, par, j * 128:(j + 1) * 128], pb[bank][:, 0:128],
                            maskbias[:, :], ALU.add),
                          reads=[("pb", bank), "maskbias"], writes=[("em", par)],
                          tag="a0:mask")
                        E("scalar", lambda e, j=j, par=par: e.activation(
                            estrip[:, par, j * 128:(j + 1) * 128],
                            estrip[:, par, j * 128:(j + 1) * 128],
                            AF.Exp, bias=misc[:, 2:3], scale=0.125),
                          reads=[("em", par), "misc"], writes=[("es", par, j)],
                          tag="a0:expd")
                    else:
                        E("scalar", lambda e, j=j, par=par, bank=bank: e.activation(
                            estrip[:, par, j * 128:(j + 1) * 128], pb[bank][:, 0:128],
                            AF.Exp, bias=misc[:, 2:3], scale=0.125),
                          reads=[("pb", bank), "misc"], writes=[("es", par, j)],
                          tag="a0:exp")
                E("vector", lambda e, i=i, par=par: e.tensor_tensor(
                    dtile[:, :], estrip[:, par, i * 128:(i + 1) * 128], ident[:, :],
                    ALU.mult),
                  reads=[("es", par, i), "ident"], writes=["dtile"], tag="a0:dt")
                E("tensor", lambda e, i=i: e.matmul(
                    pb[2][:, 0:65], dtile[:, :], vbuf[:, i, :], start=True, stop=False),
                  reads=["dtile", "vbuf", "vzero"], war=[("pb", 2)], tag="a0:dv")
                for j in range(i):
                    E("tensor", lambda e, j=j, par=par: e.matmul(
                        pb[2][:, 0:65], estrip[:, par, j * 128:(j + 1) * 128],
                        abuf[:, j, :], start=False, stop=False),
                      reads=[("es", par, j), ("A", j), "Aones"], tag="a0:umm")
                E("tensor", lambda e, i=i, par=par: e.matmul(
                    pb[2][:, 64:65], estrip[:, par, i * 128:(i + 1) * 128],
                    ones_col[:, :], start=False, stop=True),
                  reads=[("es", par, i), "ones_col"],
                  writes=[("pb", 2)], tag="a0:dsum")
                E("vector", lambda e: e.reciprocal(recs[:, :], pb[2][:, 64:65]),
                  reads=[("pb", 2)], writes=["recs"], tag="a0:recs")
                E("vector", lambda e, i=i, par=par: e.scalar_tensor_tensor(
                    stile[:, 0, :], estrip[:, par, i * 128:(i + 1) * 128],
                    recs[:, :], slmask[:, :], ALU.mult, ALU.mult),
                  reads=[("es", par, i), "recs", "slmask"], writes=[("St", 0)],
                  tag="a0:nprep")
                E("tensor", lambda e: e.transpose(pb[3][:, 0:128], stile[:, 0, :],
                                                  ident[:, :]),
                  reads=[("St", 0), "ident"], writes=[("pb", 3)], tag="a0:ntr")
                E("vector", lambda e: e.tensor_copy(ntile[:, 0, :], pb[3][:, 0:128]),
                  reads=[("pb", 3)], writes=[("Nt", 0)], tag="a0:nev")
                E("scalar", lambda e: e.copy(btile[:, 0, :], pb[2][:, 0:64]),
                  reads=[("pb", 2)], writes=[("Bt", 0)], tag="a0:b0")
                for k in range(7):
                    sp = k % 2
                    np2 = (k + 1) % 2
                    E("tensor", lambda e, sp=sp: e.matmul(
                        pb[7][:, 0:64], stile[:, sp, :], btile[:, sp, :],
                        start=True, stop=True),
                      reads=[("St", sp), ("Bt", sp)], writes=[("pb", 7)],
                      tag="a0:bmm")
                    E("vector", lambda e, sp=sp, np2=np2: e.tensor_tensor(
                        btile[:, np2, :], btile[:, sp, :], pb[7][:, 0:64], ALU.add),
                      reads=[("pb", 7), ("Bt", sp)], writes=[("Bt", np2)],
                      tag="a0:badd")
                    if k < 6:
                        E("tensor", lambda e, sp=sp: e.matmul(
                            pb[3][:, 0:128], ntile[:, sp, :], stile[:, sp, :],
                            start=True, stop=True),
                          reads=[("Nt", sp), ("St", sp)], writes=[("pb", 3)],
                          tag="a0:smm")
                        E("vector", lambda e, np2=np2: e.tensor_copy(
                            stile[:, np2, :], pb[3][:, 0:128]),
                          reads=[("pb", 3)], writes=[("St", np2)], tag="a0:sev")
                        if k < 5:
                            E("tensor", lambda e, sp=sp: e.matmul(
                                pb[4][:, 0:128], stile[:, sp, :], ntile[:, sp, :],
                                start=True, stop=True),
                              reads=[("St", sp), ("Nt", sp)], writes=[("pb", 4)],
                              tag="a0:nmm")
                            E("vector", lambda e, np2=np2: e.tensor_copy(
                                ntile[:, np2, :], pb[4][:, 0:128]),
                              reads=[("pb", 4)], writes=[("Nt", np2)], tag="a0:nev2")
                E("vector", lambda e, i=i: e.tensor_scalar(
                    abuf[:, i, 0:64], btile[:, 1, :], recs[:, :], None, ALU.mult),
                  reads=[("Bt", 1), "recs"], writes=[("A", i)], tag="a0:afin")
            for i in range(16):
                E("tensor", lambda e, i=i: e.transpose(
                    pb[5][0:64, 0:128], abuf[:, i, 0:64], ident[:, :]),
                  reads=[("A", i), "ident"], writes=[("pb", 5)], tag="a0:atr")
                E("vector", lambda e, i=i: e.tensor_copy(
                    afmst[:, 0, i * 128:(i + 1) * 128], pb[5][0:64, 0:128]),
                  reads=[("pb", 5)], writes=[("afh", i)], tag="a0:ahev")
                E("vector", lambda e, i=i: e.tensor_tensor(
                    afmst[:, 1, i * 128:(i + 1) * 128], pb[5][0:64, 0:128],
                    afmst[:, 0, i * 128:(i + 1) * 128], ALU.subtract),
                  reads=[("pb", 5), ("afh", i)], writes=[("afl", i)], tag="a0:alev")
            E("gpsimd", lambda e, s=s, h=h: e.dma_start(
                out=s_afmh[h * 64:(h + 1) * 64, s * SEQ:(s + 1) * SEQ],
                in_=afmst[:, 0, :]),
              reads=[("afh", i) for i in range(16)], writes=[("safm", sh, 0)],
              dma=True, tag="a0:sth")
            E("gpsimd", lambda e, s=s, h=h: e.dma_start(
                out=s_afml[h * 64:(h + 1) * 64, s * SEQ:(s + 1) * SEQ],
                in_=afmst[:, 1, :]),
              reads=[("afl", i) for i in range(16)], writes=[("safm", sh, 1)],
              dma=True, tag="a0:stl")
        b.barrier()

    with nc.sbuf_tensor("wdsp", [128, 2, 8, D], F16) as wdsp, \
         nc.sbuf_tensor("wdt0", [128, 2, D], F32) as wdt0, \
         nc.sbuf_tensor("bdsb", [128, 8], F32) as bdsb, \
         nc.sbuf_tensor("aft", [128, 2, 8, 512], F16) as aft, \
         nc.sbuf_tensor("aost", [128, 512], F32) as aost:
        ld(bdsb[:, :], bd_in[:, :], "bdsb")
        for c in range(8):
            kw = ("wdt", c % 2)
            E("sync", lambda e, c=c: e.dma_start(
                out=wdt0[:, c % 2, :], in_=wdT[c * 128:(c + 1) * 128, :]),
              writes=[kw], war=[kw], dma=True, tag="ap0:ldw")
            E("vector", lambda e, c=c: e.tensor_copy(wdsp[:, 0, c, :], wdt0[:, c % 2, :]),
              reads=[kw], writes=[("wdh", c)], tag="ap0:hi")
            E("vector", lambda e, c=c: e.tensor_tensor(
                wdsp[:, 1, c, :], wdt0[:, c % 2, :], wdsp[:, 0, c, :], ALU.subtract),
              reads=[kw, ("wdh", c)], writes=[("wdl", c)], tag="ap0:lo")
        for s2 in range(2):
            for tt in range(4):
                col0 = s2 * SEQ + tt * 512
                for sl, srct in ((0, s_afmh), (1, s_afml)):
                    E("sync", lambda e, sl=sl, srct=srct, col0=col0: e.dma_start(
                        out=aft[:, sl, :, :],
                        in_=srct[:, col0:col0 + 512].rearrange("(c p) t -> p c t", p=128)),
                      reads=[("safm", sh2, sl) for sh2 in range(32)],
                      writes=[("aft", sl)], dma=True, tag="ap0:lda")
                for co in range(8):
                    bank = co % 2
                    for c in range(8):
                        for p3, (wa, ab) in enumerate(((0, 0), (0, 1), (1, 0))):
                            first = (c == 0 and p3 == 0)
                            lastm = (c == 7 and p3 == 2)
                            E("tensor", lambda e, co=co, c=c, wa=wa, ab=ab, bank=bank,
                              first=first, lastm=lastm:
                              e.matmul(pb[bank][:, 0:512],
                                       wdsp[:, wa, c, co * 128:(co + 1) * 128],
                                       aft[:, ab, c, :], start=first, stop=lastm),
                              reads=[("wdh", c), ("wdl", c), ("aft", 0), ("aft", 1)],
                              writes=[("pb", bank)] if lastm else (),
                              war=[("pb", bank)] if first else (), tag="ap0:mm")
                    E("vector", lambda e, co=co, bank=bank: e.tensor_scalar(
                        aost[:, :], pb[bank][:, 0:512], bdsb[:, co:co + 1], None,
                        ALU.add),
                      reads=[("pb", bank), "bdsb"], writes=["aost"], tag="ap0:ev")
                    E("gpsimd", lambda e, co=co, col0=col0: e.dma_start(
                        out=s_attn0[co * 128:(co + 1) * 128, col0:col0 + 512],
                        in_=aost[:, :]),
                      reads=["aost"], writes=[("sat", co, col0)], dma=True,
                      tag="ap0:st")
        b.barrier()

    def lpbq_prep(wsrc, nrows, whiT, wloT, kp):
        nrc = nrows // 128
        ncol = int(wsrc.shape[1])
        nblk = ncol // 32
        nchunk = ncol // 128
        with nc.sbuf_tensor("lq_w" + kp, [128, 2, ncol], F32) as lqw, \
             nc.sbuf_tensor("lq_m" + kp, [128, nblk], F32) as lqm, \
             nc.sbuf_tensor("lq_rs" + kp, [128, nblk], F32) as lqrs, \
             nc.sbuf_tensor("lq_t" + kp, [128, ncol], F32) as lqt, \
             nc.sbuf_tensor("lq_q" + kp, [128, ncol], I32) as lqq, \
             nc.sbuf_tensor("lq_dq" + kp, [128, 2, ncol], F32) as lqdq, \
             nc.sbuf_tensor("lq_st" + kp, [128, 128], F32) as lqst:
            for rc in range(nrc):
                par = rc % 2
                kw = (kp, "w", par)
                E("sync", lambda e, rc=rc, par=par: e.dma_start(
                    out=lqw[:, par, :], in_=wsrc[rc * 128:(rc + 1) * 128, :]),
                  writes=[kw], war=[kw], dma=True, tag="f0:lw")
                wv = lqw[:, par, :].rearrange("p (b q) -> p b q", q=32)
                E("vector", lambda e, wv=wv: e.tensor_reduce(
                    lqm[:, :], wv, AX.X, ALU.max, apply_absolute_value=True),
                  reads=[kw], writes=[(kp, "m")], tag="f0:absm")
                E("vector", lambda e: e.tensor_scalar(
                    lqrs[:, :], lqm[:, :], 1.0 / 7.0, 1e-12, ALU.mult, ALU.max),
                  reads=[(kp, "m")], writes=[(kp, "s")], tag="f0:s")
                E("vector", lambda e: e.reciprocal(lqm[:, :], lqrs[:, :]),
                  reads=[(kp, "s")], writes=[(kp, "m")], tag="f0:rs")
                mbc = lqm[:, :].unsqueeze(2).broadcast_to([128, nblk, 32])
                sbc = lqrs[:, :].unsqueeze(2).broadcast_to([128, nblk, 32])
                tv = lqt[:, :].rearrange("p (b q) -> p b q", q=32)
                E("vector", lambda e, wv=wv, mbc=mbc, tv=tv: e.tensor_tensor(
                    tv, wv, mbc, ALU.mult),
                  reads=[kw, (kp, "m")], writes=[(kp, "t")], tag="f0:t")
                E("vector", lambda e: e.tensor_scalar(
                    lqq[:, :], lqt[:, :], -8.0, 7.0, ALU.max, ALU.min),
                  reads=[(kp, "t")], writes=[(kp, "q")], tag="f0:q")
                qv = lqq[:, :].rearrange("p (b q) -> p b q", q=32)
                dqv = lqdq[:, par, :].rearrange("p (b q) -> p b q", q=32)
                kdq = (kp, "dq", par)
                E("vector", lambda e, qv=qv, sbc=sbc, dqv=dqv: e.tensor_tensor(
                    dqv, qv, sbc, ALU.mult),
                  reads=[(kp, "q"), (kp, "s")], writes=[kdq], war=[kdq], tag="f0:dq")
                for c in range(nchunk):
                    bank = 6 + c % 2
                    E("tensor", lambda e, c=c, par=par, bank=bank: e.transpose(
                        pb[bank][:, 0:128], lqdq[:, par, c * 128:(c + 1) * 128],
                        ident[:, :]),
                      reads=[kdq, "ident"], writes=[("pb", bank)], tag="f0:tr")
                    E("scalar", lambda e, bank=bank: e.copy(
                        lqst[:, :], pb[bank][:, 0:128]),
                      reads=[("pb", bank)], writes=[(kp, "st")], tag="f0:cp")
                    E("vector", lambda e, c=c, rc=rc, bank=bank: e.tensor_copy(
                        whiT[:, c, rc * 128:(rc + 1) * 128], pb[bank][:, 0:128]),
                      reads=[("pb", bank)], writes=[(kp, "hi", c, rc)], tag="f0:hi")
                    E("vector", lambda e, c=c, rc=rc: e.tensor_tensor(
                        wloT[:, c, rc * 128:(rc + 1) * 128], lqst[:, :],
                        whiT[:, c, rc * 128:(rc + 1) * 128], ALU.subtract),
                      reads=[(kp, "st"), (kp, "hi", c, rc)],
                      writes=[(kp, "lo", c, rc)], tag="f0:lo")

    b.barrier()

    with nc.sbuf_tensor("f0xq", [128, 2, 8, 512], F16) as f0xq, \
         nc.sbuf_tensor("f0raw", [128, 2, 512], F32) as f0raw, \
         nc.sbuf_tensor("f0mc", [128, 1], F32) as f0mc, \
         nc.sbuf_tensor("f0mx", [128, 2], F32) as f0mx:
        E("vector", lambda e: e.memset(f0mx[:, :], 0.0), writes=[("f0mx", 0), ("f0mx", 1)],
          tag="f0:mx0")
        for wi, (wsrc0, kpw, sdst) in enumerate(((w0u, "wu", s_b),
                                                 (w0g, "wg", s_a))):
            wscope = contextlib.ExitStack()
            wh = wscope.enter_context(nc.sbuf_tensor("f0wh%d" % wi, [128, 8, DFF0], F16))
            wl = wscope.enter_context(nc.sbuf_tensor("f0wl%d" % wi, [128, 8, DFF0], F16))
            lpbq_prep(wsrc0, DFF0, wh, wl, kpw)
            b.barrier()
            if wi == 0 and "wusp" in dbg_out:
                E("sync", lambda e, wh=wh: e.dma_start(
                    out=dbg_out["wusp"][0:128, :],
                    in_=wh[:, :, :].rearrange("p c o -> p (c o)")),
                  writes=["dbgwu1"], dma=True, tag="dbg:wu1")
                E("sync", lambda e, wl=wl: e.dma_start(
                    out=dbg_out["wusp"][128:256, :],
                    in_=wl[:, :, :].rearrange("p c o -> p (c o)")),
                  writes=["dbgwu2"], dma=True, tag="dbg:wu2")
            for sup in range(8):
                for sl, srct in ((0, s_xqhi), (1, s_xqlo)):
                    E("sync", lambda e, sl=sl, srct=srct, sup=sup: e.dma_start(
                        out=f0xq[:, sl, :, :],
                        in_=srct[:, sup * 512:(sup + 1) * 512]
                        .rearrange("(c p) t -> p c t", p=128)),
                      writes=[("f0xq", sl)], war=[("f0xq", sl)], dma=True,
                      tag="f0:ldxq")
                for mm2 in range(24):
                    bank = mm2 % 2
                    rpar = mm2 % 2
                    for c in range(8):
                        for p3, (wa, xb) in enumerate(((0, 0), (1, 0), (0, 1))):
                            first = (c == 0 and p3 == 0)
                            lastm = (c == 7 and p3 == 2)
                            ww = wh if wa == 0 else wl
                            E("tensor", lambda e, ww=ww, c=c, mm2=mm2, xb=xb,
                              bank=bank, first=first, lastm=lastm:
                              e.matmul(pb[bank][:, 0:512],
                                       ww[:, c, mm2 * 128:(mm2 + 1) * 128],
                                       f0xq[:, xb, c, :], start=first, stop=lastm),
                              reads=[("f0xq", 0), ("f0xq", 1)],
                              writes=[("pb", bank)] if lastm else (),
                              war=[("pb", bank)] if first else (), tag="f0:mm")
                    E("vector", lambda e, bank=bank, rpar=rpar: e.tensor_copy(
                        f0raw[:, rpar, :], pb[bank][:, 0:512]),
                      reads=[("pb", bank)], writes=[("f0raw", rpar)], tag="f0:ev")
                    E("vector", lambda e, rpar=rpar: e.reduce_max(
                        f0mc[:, :], f0raw[:, rpar, :], axis=AX.X,
                        apply_absolute_value=True),
                      reads=[("f0raw", rpar)], writes=["f0mc"], tag="f0:mx1")
                    E("vector", lambda e, wi=wi: e.tensor_tensor(
                        f0mx[:, wi:wi + 1], f0mx[:, wi:wi + 1], f0mc[:, :], ALU.max),
                      reads=["f0mc", ("f0mx", wi)], writes=[("f0mx", wi)],
                      tag="f0:mx2")
                    E("gpsimd", lambda e, sdst=sdst, mm2=mm2, sup=sup, rpar=rpar:
                      e.dma_start(out=sdst[mm2 * 128:(mm2 + 1) * 128,
                                           sup * 512:(sup + 1) * 512],
                                  in_=f0raw[:, rpar, :]),
                      reads=[("f0raw", rpar)], writes=[("f0s", wi, mm2, sup)],
                      dma=True, tag="f0:st")
            b.barrier()
            wscope.close()
        if "upraw" in dbg_out:
            E("sync", lambda e: e.dma_start(out=dbg_out["upraw"][:, :], in_=s_b[:, :]),
              reads=[("f0s", 0, mm3, sup3) for mm3 in range(24) for sup3 in range(8)],
              writes=["dbgupraw"], dma=True, tag="dbg:upraw")
        if "gateraw" in dbg_out:
            E("sync", lambda e: e.dma_start(out=dbg_out["gateraw"][:, :], in_=s_a[:, :]),
              reads=[("f0s", 1, mm3, sup3) for mm3 in range(24) for sup3 in range(8)],
              writes=["dbggateraw"], dma=True, tag="dbg:gateraw")
        E("tensor", lambda e: e.transpose(pb[0][0:1, 0:128], f0mx[:, 0:1], ident[:, :]),
          reads=[("f0mx", 0), "ident"], writes=[("pb", 0)], tag="r2:tr")
        E("vector", lambda e: e.reduce_max(gmax[0:1, 4:5], pb[0][0:1, 0:128], axis=AX.X),
          reads=[("pb", 0)], writes=["r2u"], tag="r2:mx1")
        E("tensor", lambda e: e.transpose(pb[1][0:1, 0:128], f0mx[:, 1:2], ident[:, :]),
          reads=[("f0mx", 1), "ident"], writes=[("pb", 1)], tag="r2:tr2")
        E("vector", lambda e: e.reduce_max(gmax[0:1, 5:6], pb[1][0:1, 0:128], axis=AX.X),
          reads=[("pb", 1)], writes=["r2g"], tag="r2:mx2")
        E("gpsimd", lambda e: e.dma_start(out=cc_in[0:1, 0:2], in_=gmax[0:1, 4:6]),
          reads=["r2u", "r2g"], writes=["cc_in"], dma=True, tag="r2:out")
        E("gpsimd", lambda e: e.collective_compute(
            "AllReduce", ALU.max, replica_groups=[list(range(NCORE))],
            ins=[cc_in[0:1, 0:2].opt()], outs=[cc_out[0:1, 0:2].opt()]),
          reads=["cc_in"], writes=["cc_out"], war=["cc_out"], tag="r2:cc")
        E("gpsimd", lambda e: e.dma_start(out=gmax[0:1, 6:8], in_=cc_out[0:1, 0:2]),
          reads=["cc_out"], writes=["r2m"], dma=True, tag="r2:in")
        E("vector", lambda e: e.tensor_tensor(
            gmax[0:1, 8:10], gmax[0:1, 6:8],
            gmax[0:1, 2:3].broadcast_to([1, 2]), ALU.mult),
          reads=["r2m", "s1c"], writes=["r2s"], tag="r2:mul")
        E("vector", lambda e: e.tensor_scalar(
            gmax[0:1, 10:12], gmax[0:1, 8:10], 1.0 / 32767.0, 1e-12, ALU.mult,
            ALU.max),
          reads=["r2s"], writes=["s2c"], tag="r2:s2")
        E("vector", lambda e: e.reciprocal(gmax[0:1, 12:14], gmax[0:1, 10:12]),
          reads=["s2c"], writes=["s2r"], tag="r2:rec")
        E("vector", lambda e: e.tensor_tensor(
            gmax[0:1, 14:16], gmax[0:1, 12:14],
            gmax[0:1, 2:3].broadcast_to([1, 2]), ALU.mult),
          reads=["s2r", "s1c"], writes=["s2sr"], tag="r2:ru")
        bcast_cells(gmax[0:1, 14:16], ["s2sr"], cells[:, 4:6], "cells_ru_rg", 2)
        bcast_cells(gmax[0:1, 10:12], ["s2c"], cells[:, 6:8], "cells_s2", 2)
        b.barrier()

    with nc.sbuf_tensor("e1g", [128, 2, 512], F32) as e1g, \
         nc.sbuf_tensor("e1i", [128, 512], I32) as e1i, \
         nc.sbuf_tensor("e1gate", [128, 512], F32) as e1gate, \
         nc.sbuf_tensor("e1sig", [128, 512], F32) as e1sig, \
         nc.sbuf_tensor("e1u", [128, 512], I32) as e1u, \
         nc.sbuf_tensor("e1p", [128, 2, 512], F32) as e1p, \
         nc.sbuf_tensor("e1mc", [128, 1], F32) as e1mc, \
         nc.sbuf_tensor("e1mx", [128, 1], F32) as e1mx:
        E("vector", lambda e: e.memset(e1mx[:, :], 0.0), writes=["e1mx"], tag="e1:mx0")
        for ti in range(192):
            mm2, sup = ti // 8, ti % 8
            par = ti % 2
            kg = ("e1g", par)
            E("sync", lambda e, mm2=mm2, sup=sup, par=par: e.dma_start(
                out=e1g[:, par, :], in_=s_a[mm2 * 128:(mm2 + 1) * 128,
                                            sup * 512:(sup + 1) * 512]),
              reads=[("f0s", 1, mm2, sup)], writes=[kg], war=[kg], dma=True,
              tag="e1:ld")
            E("vector", lambda e, par=par: e.tensor_scalar(
                e1i[:, :], e1g[:, par, :], cells[:, 5:6], None, ALU.mult),
              reads=[kg, "cells_ru_rg"], writes=["e1i"], tag="e1:q")
            E("vector", lambda e: e.tensor_scalar(
                e1i[:, :], e1i[:, :], -32768.0, 32767.0, ALU.max, ALU.min),
              reads=["e1i"], writes=["e1i"], tag="e1:cl")
            E("vector", lambda e: e.tensor_scalar(
                e1gate[:, :], e1i[:, :], cells[:, 7:8], None, ALU.mult),
              reads=["e1i", "cells_s2"], writes=["e1gate"], tag="e1:gate")
            E("scalar", lambda e: e.activation(
                e1sig[:, :], e1i[:, :], AF.Sigmoid, scale=cells[:, 7:8]),
              reads=["e1i", "cells_s2"], writes=["e1sig"], tag="e1:sig")
            E("vector", lambda e: e.tensor_scalar(
                e1u[:, :], e1sig[:, :], 65536.0, 65535.0, ALU.mult, ALU.min),
              reads=["e1sig"], writes=["e1u"], tag="e1:u16")
            E("vector", lambda e, par=par: e.scalar_tensor_tensor(
                e1p[:, par, :], e1gate[:, :], 1.0 / 65536.0, e1u[:, :],
                ALU.mult, ALU.mult),
              reads=["e1gate", "e1u"], writes=[("e1p", par)], tag="e1:prod")
            E("vector", lambda e, par=par: e.reduce_max(
                e1mc[:, :], e1p[:, par, :], axis=AX.X, apply_absolute_value=True),
              reads=[("e1p", par)], writes=["e1mc"], tag="e1:mx1")
            E("vector", lambda e: e.tensor_tensor(
                e1mx[:, :], e1mx[:, :], e1mc[:, :], ALU.max),
              reads=["e1mc", "e1mx"], writes=["e1mx"], tag="e1:mx2")
            E("gpsimd", lambda e, mm2=mm2, sup=sup, par=par: e.dma_start(
                out=s_a[mm2 * 128:(mm2 + 1) * 128, sup * 512:(sup + 1) * 512],
                in_=e1p[:, par, :]),
              reads=[("e1p", par)], writes=[("f0s", 1, mm2, sup)], dma=True,
              tag="e1:st")
        E("tensor", lambda e: e.transpose(pb[0][0:1, 0:128], e1mx[:, :], ident[:, :]),
          reads=["e1mx", "ident"], writes=[("pb", 0)], tag="r3:tr")
        E("vector", lambda e: e.reduce_max(gmax[0:1, 16:17], pb[0][0:1, 0:128],
                                           axis=AX.X),
          reads=[("pb", 0)], writes=["r3l"], tag="r3:mx")
        E("gpsimd", lambda e: e.dma_start(out=cc_in[0:1, 0:1], in_=gmax[0:1, 16:17]),
          reads=["r3l"], writes=["cc_in"], war=["cc_in"], dma=True, tag="r3:out")
        E("gpsimd", lambda e: e.collective_compute(
            "AllReduce", ALU.max, replica_groups=[list(range(NCORE))],
            ins=[cc_in[0:1, 0:1].opt()], outs=[cc_out[0:1, 0:1].opt()]),
          reads=["cc_in"], writes=["cc_out"], war=["cc_out"], tag="r3:cc")
        E("gpsimd", lambda e: e.dma_start(out=gmax[0:1, 17:18], in_=cc_out[0:1, 0:1]),
          reads=["cc_out"], writes=["r3m"], dma=True, tag="r3:in")
        E("vector", lambda e: e.tensor_scalar(
            gmax[0:1, 18:19], gmax[0:1, 17:18], 1.0 / 32767.0, 1e-12, ALU.mult,
            ALU.max),
          reads=["r3m"], writes=["s3c"], tag="r3:s3")
        E("vector", lambda e: e.reciprocal(gmax[0:1, 19:20], gmax[0:1, 18:19]),
          reads=["s3c"], writes=["s3r"], tag="r3:rec")
        bcast_cells(gmax[0:1, 19:20], ["s3r"], cells[:, 8:9], "cells_r3", 1)
        bcast_cells(gmax[0:1, 18:19], ["s3c"], cells[:, 9:10], "cells_s3", 1)
        b.barrier()

    with nc.sbuf_tensor("e2p", [128, 2, 512], F32) as e2p, \
         nc.sbuf_tensor("e2u", [128, 2, 512], F32) as e2u, \
         nc.sbuf_tensor("e2i", [128, 512], I32) as e2i, \
         nc.sbuf_tensor("e2up", [128, 512], F32) as e2up, \
         nc.sbuf_tensor("e2o", [128, 2, 512], F32) as e2o, \
         nc.sbuf_tensor("e2mc", [128, 1], F32) as e2mc, \
         nc.sbuf_tensor("e2mx", [128, 1], F32) as e2mx:
        E("vector", lambda e: e.memset(e2mx[:, :], 0.0), writes=["e2mx"], tag="e2:mx0")
        for ti in range(192):
            mm2, sup = ti // 8, ti % 8
            par = ti % 2
            kp2 = ("e2p", par)
            kup = ("e2ul", par)
            E("sync", lambda e, mm2=mm2, sup=sup, par=par: e.dma_start(
                out=e2p[:, par, :], in_=s_a[mm2 * 128:(mm2 + 1) * 128,
                                            sup * 512:(sup + 1) * 512]),
              reads=[("f0s", 1, mm2, sup)], writes=[kp2], war=[kp2], dma=True,
              tag="e2:ldp")
            E("sync", lambda e, mm2=mm2, sup=sup, par=par: e.dma_start(
                out=e2u[:, par, :], in_=s_b[mm2 * 128:(mm2 + 1) * 128,
                                            sup * 512:(sup + 1) * 512]),
              reads=[("f0s", 0, mm2, sup)], writes=[kup], war=[kup], dma=True,
              tag="e2:ldu")
            E("vector", lambda e, par=par: e.tensor_scalar(
                e2i[:, :], e2u[:, par, :], cells[:, 4:5], None, ALU.mult),
              reads=[kup, "cells_ru_rg"], writes=["e2i"], tag="e2:uq")
            E("vector", lambda e: e.tensor_scalar(
                e2i[:, :], e2i[:, :], -32768.0, 32767.0, ALU.max, ALU.min),
              reads=["e2i"], writes=["e2i"], tag="e2:ucl")
            E("vector", lambda e: e.tensor_scalar(
                e2up[:, :], e2i[:, :], cells[:, 6:7], None, ALU.mult),
              reads=["e2i", "cells_s2"], writes=["e2up"], tag="e2:up")
            E("vector", lambda e, par=par: e.tensor_scalar(
                e2i[:, :], e2p[:, par, :], cells[:, 8:9], None, ALU.mult),
              reads=[kp2, "cells_r3"], writes=["e2i"], tag="e2:q3")
            E("vector", lambda e: e.tensor_scalar(
                e2i[:, :], e2i[:, :], -32768.0, 32767.0, ALU.max, ALU.min),
              reads=["e2i"], writes=["e2i"], tag="e2:q3cl")
            E("vector", lambda e, par=par: e.scalar_tensor_tensor(
                e2o[:, par, :], e2i[:, :], cells[:, 9:10], e2up[:, :],
                ALU.mult, ALU.mult),
              reads=["e2i", "cells_s3", "e2up"], writes=[("e2o", par)], tag="e2:p2")
            E("vector", lambda e, par=par: e.reduce_max(
                e2mc[:, :], e2o[:, par, :], axis=AX.X, apply_absolute_value=True),
              reads=[("e2o", par)], writes=["e2mc"], tag="e2:mx1")
            E("vector", lambda e: e.tensor_tensor(
                e2mx[:, :], e2mx[:, :], e2mc[:, :], ALU.max),
              reads=["e2mc", "e2mx"], writes=["e2mx"], tag="e2:mx2")
            E("gpsimd", lambda e, mm2=mm2, sup=sup, par=par: e.dma_start(
                out=s_b[mm2 * 128:(mm2 + 1) * 128, sup * 512:(sup + 1) * 512],
                in_=e2o[:, par, :]),
              reads=[("e2o", par)], writes=[("f0s", 0, mm2, sup)], dma=True,
              tag="e2:st")
        E("tensor", lambda e: e.transpose(pb[0][0:1, 0:128], e2mx[:, :], ident[:, :]),
          reads=["e2mx", "ident"], writes=[("pb", 0)], tag="r4:tr")
        E("vector", lambda e: e.reduce_max(gmax[0:1, 20:21], pb[0][0:1, 0:128],
                                           axis=AX.X),
          reads=[("pb", 0)], writes=["r4l"], tag="r4:mx")
        E("gpsimd", lambda e: e.dma_start(out=cc_in[0:1, 0:1], in_=gmax[0:1, 20:21]),
          reads=["r4l"], writes=["cc_in"], war=["cc_in"], dma=True, tag="r4:out")
        E("gpsimd", lambda e: e.collective_compute(
            "AllReduce", ALU.max, replica_groups=[list(range(NCORE))],
            ins=[cc_in[0:1, 0:1].opt()], outs=[cc_out[0:1, 0:1].opt()]),
          reads=["cc_in"], writes=["cc_out"], war=["cc_out"], tag="r4:cc")
        E("gpsimd", lambda e: e.dma_start(out=gmax[0:1, 21:22], in_=cc_out[0:1, 0:1]),
          reads=["cc_out"], writes=["r4m"], dma=True, tag="r4:in")
        E("vector", lambda e: e.tensor_scalar(
            gmax[0:1, 22:23], gmax[0:1, 21:22], 1.0 / 32767.0, 1e-12, ALU.mult,
            ALU.max),
          reads=["r4m"], writes=["s4c"], tag="r4:s4")
        E("vector", lambda e: e.reciprocal(gmax[0:1, 23:24], gmax[0:1, 22:23]),
          reads=["s4c"], writes=["s4r"], tag="r4:rec")
        bcast_cells(gmax[0:1, 23:24], ["s4r"], cells[:, 10:11], "cells_r4", 1)
        bcast_cells(gmax[0:1, 22:23], ["s4c"], cells[:, 11:12], "cells_s4", 1)
        b.barrier()

    wdscope = contextlib.ExitStack()
    w0dsp_h = wdscope.enter_context(nc.sbuf_tensor("w0dsp_h", [128, 24, D], F16))
    w0dsp_l = wdscope.enter_context(nc.sbuf_tensor("w0dsp_l", [128, 24, D], F16))
    lpbq_prep(w0d, D, w0dsp_h, w0dsp_l, "wd")
    b.barrier()
    with nc.sbuf_tensor("e3p", [128, 2, 512], F32) as e3p, \
         nc.sbuf_tensor("e3i", [128, 512], I32) as e3i, \
         nc.sbuf_tensor("e3sh", [128, 512], I32) as e3sh, \
         nc.sbuf_tensor("e3hi", [128, 2, 512], F16) as e3hi, \
         nc.sbuf_tensor("e3lo", [128, 2, 512], F16) as e3lo, \
         nc.sbuf_tensor("e3o", [128, 512], F32) as e3o:
        for sup in range(8):
            for mm2 in range(24):
                par = mm2 % 2
                kp3 = ("e3p", par)
                E("sync", lambda e, mm2=mm2, sup=sup, par=par: e.dma_start(
                    out=e3p[:, par, :], in_=s_b[mm2 * 128:(mm2 + 1) * 128,
                                                sup * 512:(sup + 1) * 512]),
                  reads=[("f0s", 0, mm2, sup)], writes=[kp3], war=[kp3], dma=True,
                  tag="e3:ld")
                E("vector", lambda e, par=par: e.tensor_scalar(
                    e3i[:, :], e3p[:, par, :], cells[:, 10:11], None, ALU.mult),
                  reads=[kp3, "cells_r4"], writes=["e3i"], tag="e3:q")
                E("vector", lambda e: e.tensor_scalar(
                    e3i[:, :], e3i[:, :], -32768.0, 32767.0, ALU.max, ALU.min),
                  reads=["e3i"], writes=["e3i"], tag="e3:cl")
                E("vector", lambda e: e.tensor_scalar(
                    e3sh[:, :], e3i[:, :], 1.0 / 256.0, None, ALU.mult),
                  reads=["e3i"], writes=["e3sh"], tag="e3:sh")
                E("vector", lambda e, par=par: e.tensor_scalar(
                    e3hi[:, par, :], e3sh[:, :], 256.0, None, ALU.mult),
                  reads=["e3sh"], writes=[("e3hi", par)], tag="e3:hi")
                E("vector", lambda e, par=par: e.scalar_tensor_tensor(
                    e3lo[:, par, :], e3sh[:, :], -256.0, e3i[:, :], ALU.mult,
                    ALU.add),
                  reads=["e3sh", "e3i"], writes=[("e3lo", par)], tag="e3:lo")
                for hc in range(8):
                    for p3, (wa, ob) in enumerate(((0, 0), (1, 0), (0, 1))):
                        ww = w0dsp_h if wa == 0 else w0dsp_l
                        oo = e3hi if ob == 0 else e3lo
                        first = (mm2 == 0 and p3 == 0)
                        lastm = (mm2 == 23 and p3 == 2)
                        E("tensor", lambda e, ww=ww, oo=oo, hc=hc, mm2=mm2, par=par,
                          first=first, lastm=lastm:
                          e.matmul(pb[hc][:, 0:512],
                                   ww[:, mm2, hc * 128:(hc + 1) * 128],
                                   oo[:, par, :], start=first, stop=lastm),
                          reads=[("e3hi", par), ("e3lo", par)],
                          writes=[("pb", hc)] if lastm else (),
                          war=[("pb", hc)] if first else (), tag="e3:mm")
            for hc in range(8):
                E("vector", lambda e, hc=hc: e.tensor_scalar(
                    e3o[:, :], pb[hc][:, 0:512], cells[:, 11:12], None, ALU.mult),
                  reads=[("pb", hc), "cells_s4"], writes=["e3o"], tag="e3:ev")
                E("gpsimd", lambda e, hc=hc, sup=sup: e.dma_start(
                    out=s_ffn0[hc * 128:(hc + 1) * 128, sup * 512:(sup + 1) * 512],
                    in_=e3o[:, :]),
                  reads=["e3o"], writes=[("sff", hc, sup)], dma=True, tag="e3:st")
        b.barrier()
    wdscope.close()

    with nc.sbuf_tensor("p5x", [128, 2, D], F32) as p5x, \
         nc.sbuf_tensor("p5a", [128, 2, 8, 128], F32) as p5a, \
         nc.sbuf_tensor("p5f", [128, 2, 8, 128], F32) as p5f, \
         nc.sbuf_tensor("p5t", [128, 128], F32) as p5t, \
         nc.sbuf_tensor("p5x1", [128, 2, D], F32) as p5x1, \
         nc.sbuf_tensor("p5cols", [128, 3 * NT], F32) as p5cols, \
         nc.sbuf_tensor("p5dump", [128, D], F32) as p5dump:
        for t in range(NT):
            par = t % 2
            kx = ("p5x", par)
            ka = ("p5a", par)
            kf = ("p5f", par)
            kx1 = ("p5x1", par)
            E("sync", lambda e, t=t, par=par: e.dma_start(
                out=p5x[:, par, :], in_=x_in[t * 128:(t + 1) * 128, :]),
              writes=[kx], war=[kx], dma=True, tag="p5:ldx")
            E("sync", lambda e, t=t, par=par: e.dma_start(
                out=p5a[:, par, :, :],
                in_=s_attn0[:, t * 128:(t + 1) * 128]
                .rearrange("(c p) t2 -> p c t2", p=128)),
              writes=[ka], war=[ka], dma=True, tag="p5:lda")
            E("sync", lambda e, t=t, par=par: e.dma_start(
                out=p5f[:, par, :, :],
                in_=s_ffn0[:, t * 128:(t + 1) * 128]
                .rearrange("(c p) t2 -> p c t2", p=128)),
              writes=[kf], war=[kf], dma=True, tag="p5:ldf")
            for c in range(8):
                bank = c % 2
                bank2 = 2 + c % 2
                E("tensor", lambda e, c=c, par=par, bank=bank: e.transpose(
                    pb[bank][:, 0:128], p5a[:, par, c, :], ident[:, :]),
                  reads=[ka, "ident"], writes=[("pb", bank)], tag="p5:tra")
                E("tensor", lambda e, c=c, par=par, bank2=bank2: e.transpose(
                    pb[bank2][:, 0:128], p5f[:, par, c, :], ident[:, :]),
                  reads=[kf, "ident"], writes=[("pb", bank2)], tag="p5:trf")
                E("vector", lambda e, c=c, par=par, bank=bank: e.tensor_tensor(
                    p5t[:, :], pb[bank][:, 0:128],
                    p5x[:, par, c * 128:(c + 1) * 128], ALU.add),
                  reads=[("pb", bank), kx], writes=["p5t"], tag="p5:add1")
                E("vector", lambda e, c=c, par=par, bank2=bank2: e.tensor_tensor(
                    p5x1[:, par, c * 128:(c + 1) * 128], pb[bank2][:, 0:128],
                    p5t[:, :], ALU.add),
                  reads=[("pb", bank2), "p5t"], writes=[kx1], tag="p5:add2")
            E("gpsimd", lambda e, t=t, par=par: e.dma_start(
                out=s_x1[t * 128:(t + 1) * 128, :], in_=p5x1[:, par, :]),
              reads=[kx1], writes=[("sx1", t)], dma=True, tag="p5:st")
            sm = p5cols[:, t:t + 1]
            sq = p5cols[:, NT + t:NT + t + 1]
            E("vector", lambda e, sm=sm, par=par: e.reduce_sum(
                sm, p5x1[:, par, :], axis=AX.X),
              reads=[kx1], writes=[("sm1", t)], tag="p5:sum")
            E("scalar", lambda e, sq=sq, par=par: e.activation(
                p5dump[:, :], p5x1[:, par, :], AF.Square, accum_out=sq),
              reads=[kx1], writes=[("sq1", t), "p5dump"], tag="p5:sq")
            E("vector", lambda e, t=t, sm=sm: e.tensor_scalar_mul(
                stats1_m[:, t:t + 1], sm, 1.0 / D),
              reads=[("sm1", t)], writes=[("m1", t)], tag="p5:m")
            vcol = p5cols[:, 2 * NT + t:2 * NT + t + 1]
            E("vector", lambda e, t=t, vcol=vcol: e.tensor_tensor(
                vcol, stats1_m[:, t:t + 1], stats1_m[:, t:t + 1], ALU.mult),
              reads=[("m1", t)], writes=[("v1", t)], tag="p5:msq")
            E("vector", lambda e, t=t, vcol=vcol, sq=sq: e.scalar_tensor_tensor(
                vcol, sq, 1.0 / D, vcol, ALU.mult, ALU.subtract),
              reads=[("sq1", t), ("v1", t)], writes=[("v1", t)], tag="p5:var")
            E("scalar", lambda e, vcol=vcol: e.activation(
                vcol, vcol, AF.Sqrt, bias=misc[:, 0:1], scale=1.0),
              reads=[("v1", t), "misc"], writes=[("v1b", t)], tag="p5:sqrt")
            E("vector", lambda e, t=t, vcol=vcol: e.reciprocal(
                stats1_r[:, t:t + 1], vcol),
              reads=[("v1b", t)], writes=[("r1", t)], tag="p5:recip")
        b.barrier()


    b.barrier()
    p6scope = contextlib.ExitStack()
    w1sp = {}
    for nm, wsrc1 in (("q", wq1T), ("k", wk1T), ("v", wv1T)):
        w1sp[nm] = (p6scope.enter_context(nc.sbuf_tensor("w1%s_h" % nm, [128, 8, D], F16)),
                    p6scope.enter_context(nc.sbuf_tensor("w1%s_l" % nm, [128, 8, D], F16)))
        hi1, lo1 = w1sp[nm]
        with nc.sbuf_tensor("w1t" + nm, [128, 2, D], F32) as w1t:
            for c in range(8):
                kw = ("w1t" + nm, c % 2)
                E("sync", lambda e, c=c, wsrc1=wsrc1, w1t=w1t: e.dma_start(
                    out=w1t[:, c % 2, :], in_=wsrc1[c * 128:(c + 1) * 128, :]),
                  writes=[kw], war=[kw], dma=True, tag="w1:dma")
                E("vector", lambda e, c=c, hi1=hi1, w1t=w1t: e.tensor_copy(
                    hi1[:, c, :], w1t[:, c % 2, :]),
                  reads=[kw], writes=[("w1h" + nm, c)], tag="w1:hi")
                E("vector", lambda e, c=c, hi1=hi1, lo1=lo1, w1t=w1t: e.tensor_tensor(
                    lo1[:, c, :], w1t[:, c % 2, :], hi1[:, c, :], ALU.subtract),
                  reads=[kw, ("w1h" + nm, c)], writes=[("w1l" + nm, c)], tag="w1:lo")
    b.barrier()

    with nc.sbuf_tensor("p6x", [128, 2, D], F32) as p6x, \
         nc.sbuf_tensor("p6xn", [128, 4, D], F32) as p6xn, \
         nc.sbuf_tensor("p6mx", [128, 4], F32) as p6mx, \
         nc.sbuf_tensor("p6sc", [128, 4], F32) as p6sc, \
         nc.sbuf_tensor("p6xq", [128, 4, D], F16) as p6xq, \
         nc.sbuf_tensor("p6qi", [128, D], I32) as p6qi, \
         nc.sbuf_tensor("xnfm1", [128, 8, 512], F32) as xnfm1, \
         nc.sbuf_tensor("xnfm1_hi", [128, 8, 512], F16) as xnfm1_hi, \
         nc.sbuf_tensor("xnfm1_lo", [128, 8, 512], F16) as xnfm1_lo, \
         nc.sbuf_tensor("xqfm1", [128, 512], F16) as xqfm1, \
         nc.sbuf_tensor("qkst1", [128, 512], F32) as qkst1, \
         nc.sbuf_tensor("sthi1", [128, 2, 512], F16) as sthi1, \
         nc.sbuf_tensor("stlo1", [128, 2, 512], F16) as stlo1:
        for sup in range(8):
            for tt in range(4):
                t = sup * 4 + tt
                kx = ("p6x", t % 2)
                E("sync", lambda e, t=t: e.dma_start(
                    out=p6x[:, t % 2, :], in_=s_x1[t * 128:(t + 1) * 128, :]),
                  reads=[("sx1", t)], writes=[kx], war=[kx], dma=True, tag="p6:dmax")
                kxn = ("p6xn", tt)
                E("vector", lambda e, t=t, tt=tt: e.tensor_scalar(
                    p6xn[:, tt, :], p6x[:, t % 2, :], stats1_m[:, t:t + 1],
                    stats1_r[:, t:t + 1], ALU.subtract, ALU.mult),
                  reads=[kx, ("m1", t), ("r1", t)], writes=[kxn], tag="p6:xn")
                E("vector", lambda e, tt=tt: e.reduce_max(
                    p6mx[:, tt:tt + 1], p6xn[:, tt, :], axis=AX.X,
                    apply_absolute_value=True),
                  reads=[kxn], writes=[("p6m", tt)], tag="p6:mx")
                E("vector", lambda e, tt=tt: e.tensor_scalar(
                    p6mx[:, tt:tt + 1], p6mx[:, tt:tt + 1], 1e-5, None, ALU.max),
                  reads=[("p6m", tt)], writes=[("p6m", tt)], tag="p6:cl")
                E("vector", lambda e, t=t, tt=tt: e.tensor_scalar(
                    rsff[:, t:t + 1], p6mx[:, tt:tt + 1], 1.0 / 127.0, None, ALU.mult),
                  reads=[("p6m", tt)], writes=[("rs", t)], tag="p6:rs")
                E("vector", lambda e, tt=tt: e.reciprocal(
                    p6sc[:, tt:tt + 1], p6mx[:, tt:tt + 1]),
                  reads=[("p6m", tt)], writes=[("p6s", tt)], tag="p6:rec")
                E("vector", lambda e, tt=tt: e.tensor_scalar(
                    p6sc[:, tt:tt + 1], p6sc[:, tt:tt + 1], 127.0, None, ALU.mult),
                  reads=[("p6s", tt)], writes=[("p6s", tt)], tag="p6:sc")
                E("vector", lambda e, tt=tt: e.tensor_scalar(
                    p6qi[:, :], p6xn[:, tt, :], p6sc[:, tt:tt + 1], None, ALU.mult),
                  reads=[kxn, ("p6s", tt)], writes=["p6qi"], tag="p6:qi")
                E("vector", lambda e: e.tensor_scalar(
                    p6qi[:, :], p6qi[:, :], -128.0, 127.0, ALU.max, ALU.min),
                  reads=["p6qi"], writes=["p6qi"], tag="p6:qcl")
                E("vector", lambda e, tt=tt: e.tensor_copy(p6xq[:, tt, :], p6qi[:, :]),
                  reads=["p6qi"], writes=[("p6xq", tt)], tag="p6:qf")
                for c in range(8):
                    bank = c % 2 + 2
                    E("tensor", lambda e, tt=tt, c=c, bank=bank: e.transpose(
                        pb[bank][:, 0:128], p6xn[:, tt, c * 128:(c + 1) * 128],
                        ident[:, :]),
                      reads=[kxn, "ident"], writes=[("pb", bank)], tag="p6:tr")
                    E("vector", lambda e, tt=tt, c=c, bank=bank: e.tensor_copy(
                        xnfm1[:, c, tt * 128:(tt + 1) * 128], pb[bank][:, 0:128]),
                      reads=[("pb", bank)], writes=[("xnfm1", c, tt)], tag="p6:ev")
                    bank2 = c % 2 + 4
                    E("tensor", lambda e, tt=tt, c=c, bank2=bank2: e.transpose(
                        pb[bank2][:, :].bitcast(F16)[:, 0:128],
                        p6xq[:, tt, c * 128:(c + 1) * 128],
                        ident16[:, :]),
                      reads=[("p6xq", tt), "ident16"], writes=[("pb", bank2)],
                      tag="p6:trq")
                    E("vector", lambda e, tt=tt, c=c, bank2=bank2: e.tensor_copy(
                        xqfm1[:, 0:128], pb[bank2][:, :].bitcast(F16)[:, 0:128]),
                      reads=[("pb", bank2)], writes=["xqfm1"], tag="p6:evq")
                    E("gpsimd", lambda e, c=c, sup=sup, tt=tt: e.dma_start(
                        out=s_xqhi[c * 128:(c + 1) * 128,
                                   sup * 512 + tt * 128:sup * 512 + (tt + 1) * 128],
                        in_=xqfm1[:, 0:128]),
                      reads=["xqfm1"], writes=[("sxq1", c, sup, tt)], dma=True,
                      tag="p6:stq")
            for c in range(8):
                rk = [("xnfm1", c, tt) for tt in range(4)]
                E("vector", lambda e, c=c: e.tensor_copy(xnfm1_hi[:, c, :], xnfm1[:, c, :]),
                  reads=rk, writes=[("xn1hi", c)], tag="p6:hi")
                E("vector", lambda e, c=c: e.tensor_tensor(
                    xnfm1_lo[:, c, :], xnfm1[:, c, :], xnfm1_hi[:, c, :], ALU.subtract),
                  reads=rk + [("xn1hi", c)], writes=[("xn1lo", c)], tag="p6:lo")
            xn_r1 = [("xn1hi", c) for c in range(8)] + [("xn1lo", c) for c in range(8)]
            for nm, (dhi, dlo) in (("q", (s_qhi, s_qlo)), ("k", (s_khi, s_klo))):
                hi1, lo1 = w1sp[nm]
                for oc in range(8):
                    bank = oc % 2
                    for c in range(8):
                        for p3, (wa, xb) in enumerate(((0, 0), (0, 1), (1, 0))):
                            ww = hi1 if wa == 0 else lo1
                            act = xnfm1_hi if xb == 0 else xnfm1_lo
                            first = (c == 0 and p3 == 0)
                            lastm = (c == 7 and p3 == 2)
                            E("tensor", lambda e, ww=ww, act=act, c=c, oc=oc, bank=bank,
                              first=first, lastm=lastm:
                              e.matmul(pb[bank][:, 0:512],
                                       ww[:, c, oc * 128:(oc + 1) * 128],
                                       act[:, c, :], start=first, stop=lastm),
                              reads=xn_r1, writes=[("pb", bank)] if lastm else (),
                              war=[("pb", bank)] if first else (), tag="p6:qkmm")
                    E("vector", lambda e, bank=bank: e.tensor_copy(
                        sthi1[:, bank, :], pb[bank][:, 0:512]),
                      reads=[("pb", bank)], writes=[("st1h", bank)], tag="p6:sthi")
                    E("vector", lambda e, bank=bank: e.tensor_tensor(
                        stlo1[:, bank, :], pb[bank][:, 0:512], sthi1[:, bank, :],
                        ALU.subtract),
                      reads=[("pb", bank), ("st1h", bank)], writes=[("st1l", bank)],
                      tag="p6:stlo")
                    E("gpsimd", lambda e, d=dhi, oc=oc, sup=sup, bank=bank: e.dma_start(
                        out=d[oc * 128:(oc + 1) * 128, sup * 512:(sup + 1) * 512],
                        in_=sthi1[:, bank, :]),
                      reads=[("st1h", bank)], writes=[("sqk1", nm, oc, sup, "h")],
                      dma=True, tag="p6:dmah")
                    E("gpsimd", lambda e, d=dlo, oc=oc, sup=sup, bank=bank: e.dma_start(
                        out=d[oc * 128:(oc + 1) * 128, sup * 512:(sup + 1) * 512],
                        in_=stlo1[:, bank, :]),
                      reads=[("st1l", bank)], writes=[("sqk1", nm, oc, sup, "l")],
                      dma=True, tag="p6:dmal")
            hi1, lo1 = w1sp["v"]
            for tt in range(4):
                t = sup * 4 + tt
                for ot in range(2):
                    bank = 6 + ot % 2
                    for c in range(8):
                        for p3, (xb, wa) in enumerate(((0, 0), (0, 1), (1, 0))):
                            act = xnfm1_hi if xb == 0 else xnfm1_lo
                            ww = hi1 if wa == 0 else lo1
                            first = (c == 0 and p3 == 0)
                            lastm = (c == 7 and p3 == 2)
                            E("tensor", lambda e, act=act, ww=ww, c=c, tt=tt, ot=ot,
                              bank=bank, first=first, lastm=lastm:
                              e.matmul(pb[bank][:, 0:512],
                                       act[:, c, tt * 128:(tt + 1) * 128],
                                       ww[:, c, ot * 512:(ot + 1) * 512],
                                       start=first, stop=lastm),
                              reads=xn_r1, writes=[("pb", bank)] if lastm else (),
                              war=[("pb", bank)] if first else (), tag="p6:vmm")
                    E("vector", lambda e, bank=bank: e.tensor_copy(
                        qkst1[:, :], pb[bank][:, 0:512]),
                      reads=[("pb", bank)], writes=["qkst1"], tag="p6:vev")
                    E("gpsimd", lambda e, t=t, ot=ot: e.dma_start(
                        out=s_v[t * 128:(t + 1) * 128, ot * 512:(ot + 1) * 512],
                        in_=qkst1[:, :]),
                      reads=["qkst1"], writes=[("sv1", t, ot)], dma=True,
                      tag="p6:dmav")
        b.barrier()
    p6scope.close()

    with nc.sbuf_tensor("qk41", [65, 4, SEQ], F16) as qk41, \
         nc.sbuf_tensor("vbuf1", [128, 16, HD + 1], F32) as vbuf1, \
         nc.sbuf_tensor("estrip1", [128, 2, SEQ], F32) as estrip1, \
         nc.sbuf_tensor("acol1", [128, 2], F32) as acol1, \
         nc.sbuf_tensor("recrow", [128, 128], F32) as recrow, \
         nc.sbuf_tensor("ofm", [64, SEQ], F32) as ofm, \
         nc.sbuf_tensor("ofm_hi", [64, 2, SEQ], F16) as ofm_hi:
        E("vector", lambda e: e.memset(vbuf1[:, :, 64:65], 1.0),
          writes=["v1one"], tag="a1:vone")
        for sh in range(32):
            s, h = sh // 16, sh % 16
            for slot, src_t in ((0, s_qhi), (1, s_qlo), (2, s_khi), (3, s_klo)):
                E("sync", lambda e, slot=slot, src_t=src_t, s=s, h=h: e.dma_start(
                    out=qk41[0:64, slot, :],
                    in_=src_t[h * 64:(h + 1) * 64, s * SEQ:(s + 1) * SEQ]),
                  writes=[("qk1", slot)], dma=True, tag="a1:ldqk")
            E("vector", lambda e: e.memset(qk41[64:65, 2, :], 1.0),
              writes=[("qk1a", 2)], war=[("qk1a", 2)], tag="a1:aug1")
            E("vector", lambda e: e.memset(qk41[64:65, 3, :], 0.0),
              writes=[("qk1a", 3)], war=[("qk1a", 3)], tag="a1:aug0")
            E("sync", lambda e, s=s, h=h: e.dma_start(
                out=vbuf1[:, :, 0:64],
                in_=s_v[s * SEQ:(s + 1) * SEQ, h * 64:(h + 1) * 64]
                    .rearrange("(j p) c -> p j c", p=128)),
              writes=["vbuf1"], dma=True, tag="a1:ldv")
            for i in range(16):
                par = i % 2
                j0 = (i - 1) * 128 if i > 0 else 0
                w = 256 if i > 0 else 128
                E("tensor", lambda e, i=i, j0=j0, w=w: e.matmul(
                    pb[6][:, 0:w], qk41[0:64, 0, i * 128:(i + 1) * 128],
                    qk41[0:64, 2, j0:j0 + w], start=True, stop=True),
                  reads=[("qk1", 0), ("qk1", 2)], writes=[("pb", 6)], tag="a1:sbm")
                E("vector", lambda e, w=w: e.reduce_max(
                    acol1[:, 0:1], pb[6][:, 0:w], axis=AX.X),
                  reads=[("pb", 6)], writes=["a1c0"], tag="a1:sbr")
                E("vector", lambda e: e.tensor_scalar(
                    acol1[:, 1:2], acol1[:, 0:1], 320.0, -1.0, ALU.add, ALU.mult),
                  reads=["a1c0"], writes=["a1c1"], tag="a1:sba")
                E("tensor", lambda e: e.matmul(
                    pb[6][64:65, 0:128], acol1[:, 1:2], ident[:, :],
                    start=True, stop=True),
                  reads=["a1c1", "ident"], writes=[("pb", 6)], tag="a1:sbt")
                E("vector", lambda e, i=i: e.tensor_copy(
                    qk41[64:65, 0, i * 128:(i + 1) * 128], pb[6][64:65, 0:128]),
                  reads=[("pb", 6)], writes=[("qk1a", 0)], war=[("qk1a", 0)],
                  tag="a1:sbh")
                E("vector", lambda e, i=i: e.tensor_tensor(
                    qk41[64:65, 1, i * 128:(i + 1) * 128], pb[6][64:65, 0:128],
                    qk41[64:65, 0, i * 128:(i + 1) * 128], ALU.subtract),
                  reads=[("pb", 6), ("qk1a", 0)], writes=[("qk1a", 1)],
                  war=[("qk1a", 1)], tag="a1:sbl")
                for j in range(i + 1):
                    bank = j % 2
                    for p3, (aa, bb2) in enumerate(((0, 2), (1, 2), (0, 3))):
                        E("tensor", lambda e, j=j, i=i, bank=bank, p3=p3, aa=aa, bb2=bb2:
                          e.matmul(pb[bank][:, 0:128],
                                   qk41[:, bb2, j * 128:(j + 1) * 128],
                                   qk41[:, aa, i * 128:(i + 1) * 128],
                                   start=(p3 == 0), stop=(p3 == 2)),
                          reads=[("qk1", aa), ("qk1", bb2), ("qk1a", aa), ("qk1a", bb2)],
                          writes=[("pb", bank)] if p3 == 2 else (),
                          war=[("pb", bank)] if p3 == 0 else (), tag="a1:qk")
                    if j == i:
                        E("vector", lambda e, j=j, par=par, bank=bank: e.tensor_tensor(
                            estrip1[:, par, j * 128:(j + 1) * 128], pb[bank][:, 0:128],
                            maskbias[:, :], ALU.add),
                          reads=[("pb", bank), "maskbias"], writes=[("em1", par)],
                          tag="a1:mask")
                        E("scalar", lambda e, j=j, par=par: e.activation(
                            estrip1[:, par, j * 128:(j + 1) * 128],
                            estrip1[:, par, j * 128:(j + 1) * 128],
                            AF.Exp, bias=misc[:, 2:3], scale=0.125),
                          reads=[("em1", par), "misc"], writes=[("es1", par, j)],
                          tag="a1:expd")
                    else:
                        E("scalar", lambda e, j=j, par=par, bank=bank: e.activation(
                            estrip1[:, par, j * 128:(j + 1) * 128], pb[bank][:, 0:128],
                            AF.Exp, bias=misc[:, 2:3], scale=0.125),
                          reads=[("pb", bank), "misc"], writes=[("es1", par, j)],
                          tag="a1:exp")
                for j in range(i + 1):
                    E("tensor", lambda e, j=j, par=par, i=i: e.matmul(
                        pb[2][0:65, 0:128], vbuf1[:, j, :],
                        estrip1[:, par, j * 128:(j + 1) * 128],
                        start=(j == 0), stop=(j == i)),
                      reads=[("es1", par, j), "vbuf1", "v1one"],
                      writes=[("pb", 2)] if j == i else (),
                      war=[("pb", 2)] if j == 0 else (), tag="a1:pv")
                E("vector", lambda e: e.reciprocal(recrow[64:65, 0:128],
                                                   pb[2][64:65, 0:128]),
                  reads=[("pb", 2)], writes=["recr"], tag="a1:rec")
                E("tensor", lambda e: e.matmul(
                    pb[3][0:64, 0:128], ones_all[64:65, 0:64], recrow[64:65, 0:128],
                    start=True, stop=True),
                  reads=["recr", "ones_all"], writes=[("pb", 3)], tag="a1:rbc")
                E("scalar", lambda e: e.copy(recrow[0:64, 0:128], pb[3][0:64, 0:128]),
                  reads=[("pb", 3)], writes=["recb"], tag="a1:rbe")
                E("vector", lambda e, i=i: e.tensor_tensor(
                    ofm[:, i * 128:(i + 1) * 128], pb[2][0:64, 0:128],
                    recrow[0:64, 0:128], ALU.mult),
                  reads=[("pb", 2), "recb"], writes=[("ofm", i)], tag="a1:oev")
                E("vector", lambda e, i=i: e.tensor_copy(
                    ofm_hi[:, 0, i * 128:(i + 1) * 128],
                    ofm[:, i * 128:(i + 1) * 128]),
                  reads=[("ofm", i)], writes=[("ofh", i)], tag="a1:ohl")
                E("vector", lambda e, i=i: e.tensor_tensor(
                    ofm_hi[:, 1, i * 128:(i + 1) * 128],
                    ofm[:, i * 128:(i + 1) * 128],
                    ofm_hi[:, 0, i * 128:(i + 1) * 128], ALU.subtract),
                  reads=[("ofm", i), ("ofh", i)], writes=[("ofl", i)], tag="a1:oll")
            E("gpsimd", lambda e, s=s, h=h: e.dma_start(
                out=s_afmh[h * 64:(h + 1) * 64, s * SEQ:(s + 1) * SEQ],
                in_=ofm_hi[:, 0, :]),
              reads=[("ofh", i) for i in range(16)], writes=[("safm1", sh, 0)],
              dma=True, tag="a1:sth")
            E("gpsimd", lambda e, s=s, h=h: e.dma_start(
                out=s_afml[h * 64:(h + 1) * 64, s * SEQ:(s + 1) * SEQ],
                in_=ofm_hi[:, 1, :]),
              reads=[("ofl", i) for i in range(16)], writes=[("safm1", sh, 1)],
              dma=True, tag="a1:stl")
        b.barrier()

    with nc.sbuf_tensor("wosp", [128, 2, 8, D], F16) as wosp, \
         nc.sbuf_tensor("wot0", [128, 2, D], F32) as wot0, \
         nc.sbuf_tensor("aft1", [128, 2, 8, 512], F16) as aft1, \
         nc.sbuf_tensor("aost1", [128, 512], F32) as aost1:
        for c in range(8):
            kw = ("wot", c % 2)
            E("sync", lambda e, c=c: e.dma_start(
                out=wot0[:, c % 2, :], in_=wo1[c * 128:(c + 1) * 128, :]),
              writes=[kw], war=[kw], dma=True, tag="ap1:ldw")
            E("vector", lambda e, c=c: e.tensor_copy(wosp[:, 0, c, :], wot0[:, c % 2, :]),
              reads=[kw], writes=[("woh", c)], tag="ap1:hi")
            E("vector", lambda e, c=c: e.tensor_tensor(
                wosp[:, 1, c, :], wot0[:, c % 2, :], wosp[:, 0, c, :], ALU.subtract),
              reads=[kw, ("woh", c)], writes=[("wol", c)], tag="ap1:lo")
        for s2 in range(2):
            for tt in range(4):
                col0 = s2 * SEQ + tt * 512
                for sl, srct in ((0, s_afmh), (1, s_afml)):
                    E("sync", lambda e, sl=sl, srct=srct, col0=col0: e.dma_start(
                        out=aft1[:, sl, :, :],
                        in_=srct[:, col0:col0 + 512].rearrange("(c p) t -> p c t", p=128)),
                      reads=[("safm1", sh2, sl) for sh2 in range(32)],
                      writes=[("aft1", sl)], war=[("aft1", sl)], dma=True,
                      tag="ap1:lda")
                for co in range(8):
                    bank = co % 2
                    for c in range(8):
                        for p3, (wa, ab) in enumerate(((0, 0), (0, 1), (1, 0))):
                            first = (c == 0 and p3 == 0)
                            lastm = (c == 7 and p3 == 2)
                            E("tensor", lambda e, co=co, c=c, wa=wa, ab=ab, bank=bank,
                              first=first, lastm=lastm:
                              e.matmul(pb[bank][:, 0:512],
                                       wosp[:, wa, c, co * 128:(co + 1) * 128],
                                       aft1[:, ab, c, :], start=first, stop=lastm),
                              reads=[("woh", c), ("wol", c), ("aft1", 0), ("aft1", 1)],
                              writes=[("pb", bank)] if lastm else (),
                              war=[("pb", bank)] if first else (), tag="ap1:mm")
                    E("vector", lambda e, bank=bank: e.tensor_copy(
                        aost1[:, :], pb[bank][:, 0:512]),
                      reads=[("pb", bank)], writes=["aost1"], tag="ap1:ev")
                    E("gpsimd", lambda e, co=co, col0=col0: e.dma_start(
                        out=s_attn0[co * 128:(co + 1) * 128, col0:col0 + 512],
                        in_=aost1[:, :]),
                      reads=["aost1"], writes=[("sat1", co, col0)], dma=True,
                      tag="ap1:st")
        b.barrier()

    def w1_quant(wsrc1, wq_f16, kp, ncol, cc0):
        nrc2 = int(wsrc1.shape[0]) // 128
        with nc.sbuf_tensor("wq_t" + kp, [128, 2, ncol], F32) as wqt, \
             nc.sbuf_tensor("wq_s" + kp, [128, 33], F32) as wqs, \
             nc.sbuf_tensor("wq_i" + kp, [128, ncol], I32) as wqi:
            for rc in range(nrc2):
                kw = (kp, "w", rc % 2)
                E("sync", lambda e, rc=rc: e.dma_start(
                    out=wqt[:, rc % 2, :], in_=wsrc1[rc * 128:(rc + 1) * 128, :]),
                  writes=[kw], war=[kw], dma=True, tag="f1:lw")
                E("vector", lambda e, rc=rc: e.reduce_sum(
                    wqs[:, rc:rc + 1], wqt[:, rc % 2, :], axis=AX.X,
                    apply_absolute_value=True),
                  reads=[kw], writes=[(kp, "s", rc)], tag="f1:asum")
            E("vector", lambda e, nrc2=nrc2: e.reduce_sum(
                wqs[:, nrc2:nrc2 + 1], wqs[:, 0:nrc2], axis=AX.X),
              reads=[(kp, "s", rc2) for rc2 in range(nrc2)], writes=[(kp, "s2")],
              tag="f1:asum2")
            E("tensor", lambda e, nrc2=nrc2: e.transpose(
                pb[6][0:1, 0:128], wqs[:, nrc2:nrc2 + 1], ident[:, :]),
              reads=[(kp, "s2"), "ident"], writes=[("pb", 6)], tag="f1:str")
            E("vector", lambda e: e.reduce_sum(gmax[0:1, 24:25], pb[6][0:1, 0:128],
                                               axis=AX.X),
              reads=[("pb", 6)], writes=[(kp, "tot")], tag="f1:tot")
            E("vector", lambda e, nrc2=nrc2, ncol=ncol: e.tensor_scalar(
                gmax[0:1, 25:26], gmax[0:1, 24:25], 1.0 / (nrc2 * 128 * ncol), 1e-5,
                ALU.mult, ALU.max),
              reads=[(kp, "tot")], writes=[(kp, "mean")], tag="f1:mean")
            E("vector", lambda e: e.reciprocal(gmax[0:1, 26:27], gmax[0:1, 25:26]),
              reads=[(kp, "mean")], writes=[(kp, "sw")], tag="f1:sw")
            bcast_cells(gmax[0:1, 25:27], [(kp, "mean"), (kp, "sw")],
                        cells[:, cc0:cc0 + 2], (kp, "swc"), 2)
            for rc in range(nrc2):
                kw = (kp, "w", rc % 2)
                E("sync", lambda e, rc=rc: e.dma_start(
                    out=wqt[:, rc % 2, :], in_=wsrc1[rc * 128:(rc + 1) * 128, :]),
                  writes=[kw], war=[kw], dma=True, tag="f1:lw2")
                E("vector", lambda e, rc=rc: e.tensor_scalar(
                    wqi[:, :], wqt[:, rc % 2, :], cells[:, cc0 + 1:cc0 + 2], -1.0,
                    ALU.mult, ALU.max),
                  reads=[kw, (kp, "swc")], writes=[(kp, "i")], tag="f1:q1")
                E("vector", lambda e: e.tensor_scalar(
                    wqi[:, :], wqi[:, :], 1.0, None, ALU.min),
                  reads=[(kp, "i")], writes=[(kp, "i")], tag="f1:q2")
                E("vector", lambda e, rc=rc: e.tensor_copy(
                    wq_f16[:, rc, :], wqi[:, :]),
                  reads=[(kp, "i")], writes=[(kp, "f", rc)], tag="f1:qf")

    b.barrier()
    f1scope = contextlib.ExitStack()
    w1g_f = f1scope.enter_context(nc.sbuf_tensor("w1g_f", [128, 8, DFF1], F16))
    w1u_f = f1scope.enter_context(nc.sbuf_tensor("w1u_f", [128, 8, DFF1], F16))
    w1_quant(w1gT, w1g_f, "f1g", DFF1, 12)
    w1_quant(w1uT, w1u_f, "f1u", DFF1, 14)
    b.barrier()
    if "w1g" in dbg_out:
        E("sync", lambda e: e.dma_start(
            out=dbg_out["w1g"][:, :],
            in_=w1g_f[:, :, :].rearrange("p c o -> p (c o)")),
          writes=["dbgw1g"], dma=True, tag="dbg:w1g")
        E("sync", lambda e: e.dma_start(out=dbg_out["gm2"][:, :], in_=gmax[0:1, 0:32]),
          writes=["dbggm2"], dma=True, tag="dbg:gm2")
        b.barrier()

    dram_scr("s_guard1", [256, 1024], F32)
    s_oq = dram_scr("s_oq", [8, 128, 32, 512], F16)
    dram_scr("s_guard2", [256, 1024], F32)
    with nc.sbuf_tensor("f1xq", [128, 8, 128], F16) as f1xq, \
         nc.sbuf_tensor("f1gt", [128, 512], F32) as f1gt, \
         nc.sbuf_tensor("f1gu", [128, DFF1], F32) as f1gu, \
         nc.sbuf_tensor("f1up", [128, 512], F32) as f1up, \
         nc.sbuf_tensor("f1mc", [128, 1], F32) as f1mc, \
         nc.sbuf_tensor("f1mx", [128, 1], F32) as f1mx, \
         nc.sbuf_tensor("f1sc", [128, 4], F32) as f1sc, \
         nc.sbuf_tensor("f1qi", [128, 512], I32) as f1qi, \
         nc.sbuf_tensor("f1of", [128, 512], F16) as f1of, \
         nc.sbuf_tensor("f1ost", [128, 4, 128], F16) as f1ost:
        for t in range(NT):
            E("sync", lambda e, t=t: e.dma_start(
                out=f1xq[:, :, :],
                in_=s_xqhi[:, t * 128:(t + 1) * 128]
                    .rearrange("(c p) t2 -> p c t2", p=128)),
              reads=[("sxq1", c2, t // 4, t % 4) for c2 in range(8)],
              writes=["f1xq"], war=["f1xq"], dma=True, tag="f1:ldxq")
            E("vector", lambda e, t=t: e.tensor_tensor(
                f1sc[:, 0:1], rsff[:, t:t + 1], cells[:, 12:13], ALU.mult),
              reads=[("rs", t), ("f1g", "swc")], writes=["f1s0"], tag="f1:s0")
            E("vector", lambda e, t=t: e.tensor_tensor(
                f1sc[:, 1:2], rsff[:, t:t + 1], cells[:, 14:15], ALU.mult),
              reads=[("rs", t), ("f1u", "swc")], writes=["f1s1"], tag="f1:s1")
            E("vector", lambda e: e.memset(f1mx[:, :], 0.0),
              writes=["f1mx"], war=["f1mx"], tag="f1:mx0")
            for it in range(8):
                for wi2, wf in ((0, w1g_f), (1, w1u_f)):
                    bank = wi2
                    for c in range(8):
                        E("tensor", lambda e, wf=wf, c=c, it=it, bank=bank:
                          e.matmul(pb[bank][:, 0:512], f1xq[:, c, :],
                                   wf[:, c, it * 512:(it + 1) * 512],
                                   start=(c == 0), stop=(c == 7)),
                          reads=["f1xq"],
                          writes=[("pb", bank)] if c == 7 else (),
                          war=[("pb", bank)] if c == 0 else (), tag="f1:mm")
                E("scalar", lambda e: e.activation(
                    f1gt[:, :], pb[0][:, 0:512], AF.Silu, scale=f1sc[:, 0:1]),
                  reads=[("pb", 0), "f1s0"], writes=["f1gt"], tag="f1:silu")
                E("vector", lambda e: e.tensor_scalar(
                    f1up[:, :], pb[1][:, 0:512], f1sc[:, 1:2], None, ALU.mult),
                  reads=[("pb", 1), "f1s1"], writes=["f1up"], tag="f1:up")
                E("vector", lambda e, it=it: e.tensor_tensor(
                    f1gu[:, it * 512:(it + 1) * 512], f1gt[:, :], f1up[:, :],
                    ALU.mult),
                  reads=["f1gt", "f1up"], writes=[("f1gu", it)], tag="f1:gu")
                E("vector", lambda e, it=it: e.reduce_max(
                    f1mc[:, :], f1gu[:, it * 512:(it + 1) * 512], axis=AX.X,
                    apply_absolute_value=True),
                  reads=[("f1gu", it)], writes=["f1mc"], tag="f1:mx1")
                E("vector", lambda e: e.tensor_tensor(
                    f1mx[:, :], f1mx[:, :], f1mc[:, :], ALU.max),
                  reads=["f1mc", "f1mx"], writes=["f1mx"], tag="f1:mx2")
            E("vector", lambda e: e.tensor_scalar(
                f1sc[:, 2:3], f1mx[:, :], 1e-5, None, ALU.max),
              reads=["f1mx"], writes=["f1cl"], tag="f1:cl")
            E("vector", lambda e, t=t: e.tensor_scalar(
                rso[:, t:t + 1], f1sc[:, 2:3], 1.0 / 127.0, None, ALU.mult),
              reads=["f1cl"], writes=[("rso", t)], tag="f1:rso")
            E("vector", lambda e: e.reciprocal(f1sc[:, 3:4], f1sc[:, 2:3]),
              reads=["f1cl"], writes=["f1r"], tag="f1:rec")
            E("vector", lambda e: e.tensor_scalar(
                f1sc[:, 3:4], f1sc[:, 3:4], 127.0, None, ALU.mult),
              reads=["f1r"], writes=["f1r"], tag="f1:sc127")
            for it in range(8):
                E("vector", lambda e, it=it: e.tensor_scalar(
                    f1qi[:, :], f1gu[:, it * 512:(it + 1) * 512], f1sc[:, 3:4],
                    None, ALU.mult),
                  reads=[("f1gu", it), "f1r"], writes=["f1qi"], tag="f1:oq")
                E("vector", lambda e: e.tensor_scalar(
                    f1qi[:, :], f1qi[:, :], -128.0, 127.0, ALU.max, ALU.min),
                  reads=["f1qi"], writes=["f1qi"], tag="f1:ocl")
                E("vector", lambda e: e.tensor_copy(f1of[:, :], f1qi[:, :]),
                  reads=["f1qi"], writes=["f1of"], tag="f1:of")
                for sub in range(4):
                    bank = 2 + sub % 2
                    E("tensor", lambda e, sub=sub, bank=bank: e.transpose(
                        pb[bank][:, :].bitcast(F16)[:, 0:128],
                        f1of[:, sub * 128:(sub + 1) * 128],
                        ident16[:, :]),
                      reads=["f1of", "ident16"], writes=[("pb", bank)],
                      tag="f1:otr")
                    E("vector", lambda e, sub=sub, bank=bank: e.tensor_copy(
                        f1ost[:, sub, :], pb[bank][:, :].bitcast(F16)[:, 0:128]),
                      reads=[("pb", bank)], writes=[("f1os", sub)], tag="f1:oev")
                E("sync", lambda e, it=it, t=t: e.dma_start(
                    out=s_oq[t // 4, :, it * 4:(it + 1) * 4,
                             (t % 4) * 128:(t % 4 + 1) * 128],
                    in_=f1ost[:, :, :]),
                  reads=[("f1os", sub2) for sub2 in range(4)],
                  writes=[("soq", it, t)], dma=True, tag="f1:ost")
        b.barrier()
    f1scope.close()

    f1bscope = contextlib.ExitStack()
    w1d_f = f1bscope.enter_context(nc.sbuf_tensor("w1d_f", [128, 32, D], F16))
    rsbc = f1bscope.enter_context(nc.sbuf_tensor("rsbc", [128, NTOK], F32))
    w1_quant(w1dT, w1d_f, "f1d", D, 16)
    b.barrier()
    with nc.sbuf_tensor("rsr", [1, 128], F32) as rsr:
        for t in range(NT):
            E("tensor", lambda e, t=t: e.transpose(
                pb[6][0:1, 0:128], rso[:, t:t + 1], ident[:, :]),
              reads=[("rso", t), "ident"], writes=[("pb", 6)], tag="f1:rstr")
            E("vector", lambda e: e.tensor_scalar(
                rsr[:, :], pb[6][0:1, 0:128], cells[0:1, 16:17], None, ALU.mult),
              reads=[("pb", 6), ("f1d", "swc")], writes=["rsr"], tag="f1:rsev1")
            E("tensor", lambda e: e.matmul(
                pb[7][:, 0:128], ones_row[:, :], rsr[0:1, :], start=True, stop=True),
              reads=["rsr", "ones_row"], writes=[("pb", 7)], tag="f1:rsmm")
            E("vector", lambda e, t=t: e.tensor_copy(
                rsbc[:, t * 128:(t + 1) * 128], pb[7][:, 0:128]),
              reads=[("pb", 7)], writes=[("rsbc", t // 4)], tag="f1:rsev2")
    with nc.sbuf_tensor("f1pad", [128, 2048], F32) as f1pad, \
         nc.sbuf_tensor("f1oq", [128, 32, 512], F16) as f1oq, \
         nc.sbuf_tensor("f1fo", [128, 512], F32) as f1fo, \
         nc.sbuf_tensor("f1fo2", [128, 512], F32) as f1fo2:
        for sup in range(8):
            for qt in range(4):
                E("sync", lambda e, sup=sup, qt=qt: e.dma_start(
                    out=f1oq[:, qt * 8:(qt + 1) * 8, :],
                    in_=s_oq[sup, :, qt * 8:(qt + 1) * 8, :]),
                  reads=[("soq", it2, t2) for it2 in range(qt * 2, qt * 2 + 2)
                         for t2 in range(sup * 4, sup * 4 + 4)],
                  writes=[("f1oq", qt)], war=[("f1oq", qt)], dma=True,
                  tag="f1:ldoq")
            b.barrier()
            if "oqs" in dbg_out:
                E("gpsimd", lambda e, sup=sup: e.dma_start(
                    out=dbg_out["oqs"][sup, :, :, :], in_=f1oq[:, :, :]),
                  reads=[("f1oq", 0)], writes=[("dbgoqs", sup)], dma=True,
                  tag="dbg:oqs")
                b.barrier()
            for hc in range(8):
                bank = hc % 2
                for kc in range(32):
                    E("tensor", lambda e, kc=kc, hc=hc, bank=bank: e.matmul(
                        pb[bank][:, 0:512], w1d_f[:, kc, hc * 128:(hc + 1) * 128],
                        f1oq[:, kc, :], start=(kc == 0), stop=(kc == 31)),
                      reads=[("f1oq", kc // 8)],
                      writes=[("pb", bank)] if kc == 31 else (),
                      war=[("pb", bank)] if kc == 0 else (), tag="f1:dmm")
                if "fraw" in dbg_out:
                    E("scalar", lambda e, hc=hc, sup=sup, bank=bank: e.copy(
                        f1fo2[:, :], pb[bank][:, 0:512]),
                      reads=[("pb", bank)], writes=["f1fo2"], tag="f1:devr")
                    E("gpsimd", lambda e, hc=hc, sup=sup: e.dma_start(
                        out=dbg_out["fraw"][hc * 128:(hc + 1) * 128,
                                            sup * 512:(sup + 1) * 512],
                        in_=f1fo2[:, :]),
                      reads=["f1fo2"], writes=[("dbgfr", hc, sup)], dma=True,
                      tag="f1:devrst")
                E("vector", lambda e, hc=hc, sup=sup, bank=bank: e.scalar_tensor_tensor(
                    f1fo[:, :], pb[bank][:, 0:512], 1.0,
                    rsbc[:, sup * 512:(sup + 1) * 512], ALU.mult, ALU.mult),
                  reads=[("pb", bank), ("rsbc", sup)], writes=["f1fo"], tag="f1:dev")
                E("gpsimd", lambda e, hc=hc, sup=sup: e.dma_start(
                    out=s_ffn0[hc * 128:(hc + 1) * 128, sup * 512:(sup + 1) * 512],
                    in_=f1fo[:, :]),
                  reads=["f1fo"], writes=[("sff1", hc, sup)], dma=True, tag="f1:dst")
            b.barrier()
        b.barrier()
    f1bscope.close()

    with nc.sbuf_tensor("p9x", [128, 2, D], F32) as p9x, \
         nc.sbuf_tensor("p9a", [128, 2, 8, 128], F32) as p9a, \
         nc.sbuf_tensor("p9f", [128, 2, 8, 128], F32) as p9f, \
         nc.sbuf_tensor("p9t", [128, 128], F32) as p9t, \
         nc.sbuf_tensor("p9y", [128, 2, D], F32) as p9y:
        for t in range(NT):
            par = t % 2
            kx = ("p9x", par)
            ka = ("p9a", par)
            kf = ("p9f", par)
            ky = ("p9y", par)
            E("sync", lambda e, t=t, par=par: e.dma_start(
                out=p9x[:, par, :], in_=s_x1[t * 128:(t + 1) * 128, :]),
              reads=[("sx1", t)], writes=[kx], war=[kx], dma=True, tag="p9:ldx")
            E("sync", lambda e, t=t, par=par: e.dma_start(
                out=p9a[:, par, :, :],
                in_=s_attn0[:, t * 128:(t + 1) * 128]
                .rearrange("(c p) t2 -> p c t2", p=128)),
              reads=[("sat1", co, (t // 4) * 512) for co in range(8)],
              writes=[ka], war=[ka], dma=True, tag="p9:lda")
            E("sync", lambda e, t=t, par=par: e.dma_start(
                out=p9f[:, par, :, :],
                in_=s_ffn0[:, t * 128:(t + 1) * 128]
                .rearrange("(c p) t2 -> p c t2", p=128)),
              reads=[("sff1", hc2, t // 4) for hc2 in range(8)],
              writes=[kf], war=[kf], dma=True, tag="p9:ldf")
            for c in range(8):
                bank = c % 2
                bank2 = 2 + c % 2
                E("tensor", lambda e, c=c, par=par, bank=bank: e.transpose(
                    pb[bank][:, 0:128], p9a[:, par, c, :], ident[:, :]),
                  reads=[ka, "ident"], writes=[("pb", bank)], tag="p9:tra")
                E("tensor", lambda e, c=c, par=par, bank2=bank2: e.transpose(
                    pb[bank2][:, 0:128], p9f[:, par, c, :], ident[:, :]),
                  reads=[kf, "ident"], writes=[("pb", bank2)], tag="p9:trf")
                E("vector", lambda e, c=c, par=par, bank=bank: e.tensor_tensor(
                    p9t[:, :], pb[bank][:, 0:128],
                    p9x[:, par, c * 128:(c + 1) * 128], ALU.add),
                  reads=[("pb", bank), kx], writes=["p9t"], tag="p9:add1")
                E("vector", lambda e, c=c, par=par, bank2=bank2: e.tensor_tensor(
                    p9y[:, par, c * 128:(c + 1) * 128], pb[bank2][:, 0:128],
                    p9t[:, :], ALU.add),
                  reads=[("pb", bank2), "p9t"], writes=[ky], tag="p9:add2")
            E("sync", lambda e, t=t, par=par: e.dma_start(
                out=y_out[t * 128:(t + 1) * 128, :], in_=p9y[:, par, :]),
              reads=[ky], writes=[("y", t)], dma=True, tag="p9:sty")
        b.barrier()

    dbg_keys = [("y", t) for t in range(NT)]
    if "cols" in dbg_out:
        with nc.sbuf_tensor("dbgcols", [128, 4 * NT], F32) as dbgc:
            E("vector", lambda e: e.tensor_copy(dbgc[:, 0:NT], rsff[:, :]), tag="dbg")
            E("vector", lambda e: e.tensor_copy(dbgc[:, NT:2 * NT], rso[:, :]), tag="dbg")
            E("vector", lambda e: e.tensor_copy(dbgc[:, 2 * NT:3 * NT], stats1_m[:, :]), tag="dbg")
            E("vector", lambda e: e.tensor_copy(dbgc[:, 3 * NT:4 * NT], stats1_r[:, :]), tag="dbg")
            E("sync", lambda e: e.dma_start(out=dbg_out["cols"][:, :], in_=dbgc[:, :]),
              writes=["dbgcols"], dma=True, tag="dbg")
            dbg_keys.append("dbgcols")
    srcmap = {"xqhi": s_xqhi, "xqlo": s_xqlo, "qhi": s_qhi, "qlo": s_qlo,
              "khi": s_khi, "klo": s_klo, "v": s_v, "attn0": s_attn0,
              "ffn0": s_ffn0, "x1": s_x1, "sa": s_a, "sb": s_b}
    for name in dbg_out:
        if name in srcmap:
            E("sync", lambda e, d=dbg_out[name], s=srcmap[name]:
              e.dma_start(out=d[:, :], in_=s[:, :]),
              writes=["dbg" + name], dma=True, tag="tail:dbg")
            dbg_keys.append("dbg" + name)
    E("sync", None, reads=dbg_keys, tag="tail:wait")

    b.finalize()
    st.close()
    return nc, dbg_out


def kernel(**inputs):
    shards, com = _prep_inputs(inputs)
    nc, _ = build_nc()
    in_maps = []
    for c in range(NCORE):
        m = {"x": np.ascontiguousarray(shards[c])}
        m.update(com)
        in_maps.append(m)
    res = run_bass_kernel_spmd(nc, in_maps, core_ids=list(range(NCORE)))
    out = np.concatenate([r["y"] for r in res.results], axis=0)
    return out.reshape(16, SEQ, D).astype(np.float32)


# version 07
# speedup vs baseline: 1.0374x; 1.0374x over previous
import contextlib
import numpy as np
import concourse.bass as bass
import concourse.mybir as mybir
from concourse.bass_utils import run_bass_kernel_spmd

F32 = mybir.dt.float32
F16 = mybir.dt.float16
I16 = mybir.dt.int16
I8 = mybir.dt.int8
U16 = mybir.dt.uint16
I32 = mybir.dt.int32
AF = mybir.ActivationFunctionType
ALU = mybir.AluOpType
AX = mybir.AxisListType

NCORE = 8
SEQ = 2048
NTOK = 4096
D = 1024
NH = 16
HD = 64
DFF0 = 3072
DFF1 = 4096
NT = NTOK // 128
NEG = -1e30
SERIALIZE_DMA = False


class Bld:
    ENGINES = ("sync", "tensor", "vector", "scalar", "gpsimd")

    def __init__(self, nc, stack):
        self.nc = nc
        self.ops = {e: [] for e in self.ENGINES}
        self.cnt = {}
        self.floor = {e: {} for e in self.ENGINES}
        self.lastw = {}
        self.readers = {}
        self.semh = {}
        for e in self.ENGINES:
            for p in ("E", "D"):
                name = p + e
                self.semh[name] = stack.enter_context(nc.semaphore(name))
                self.cnt[name] = 0

    def emit(self, eng, fn, reads=(), writes=(), war=(), dma=False, tag=""):
        need = {}

        def _need(c):
            if c is not None:
                need[c[0]] = max(need.get(c[0], 0), c[1])

        for r in reads:
            _need(self.lastw.get(r))
            if isinstance(r, tuple) and r and r[0] == "pb":
                for s, c in self.readers.get(r, {}).items():
                    _need((s, c))
        for w in tuple(writes) + tuple(war):
            _need(self.lastw.get(w))
            for s, c in self.readers.get(w, {}).items():
                _need((s, c))
        fl = self.floor[eng]
        waits = sorted((s, c) for s, c in need.items() if fl.get(s, 0) < c)
        for s, c in waits:
            fl[s] = c
        if fn is None:
            self.ops[eng].append((None, waits, None, 0, tag))
            return None
        sem = ("D" + eng) if dma else ("E" + eng)
        delta = 16 if dma else 1
        if dma and SERIALIZE_DMA:
            prev = self.cnt[sem]
            if prev > 0 and fl.get(sem, 0) < prev:
                waits.append((sem, prev))
                fl[sem] = prev
        self.cnt[sem] += delta
        comp = (sem, self.cnt[sem])
        self.ops[eng].append((fn, waits, sem, delta, tag))
        for w in writes:
            self.lastw[w] = comp
            self.readers[w] = {}
        for r in reads:
            self.readers.setdefault(r, {})[comp[0]] = comp[1]
        return comp

    def barrier(self):
        snap = dict(self.cnt)
        for e in self.ENGINES:
            fl = self.floor[e]
            waits = sorted((s, c) for s, c in snap.items() if c > 0 and fl.get(s, 0) < c)
            for s, c in waits:
                fl[s] = c
            self.ops[e].append((None, waits, None, 0, "barrier"))

    def finalize(self):
        def mk(ename):
            def run(eng):
                for fn, waits, sem, delta, tag in self.ops[ename]:
                    for s, c in waits:
                        eng.wait_ge(self.semh[s], c)
                    if fn is not None:
                        fn(eng).then_inc(self.semh[sem], delta)
            return run
        with self.nc.Block() as block:
            block.sync(mk("sync"))
            block.tensor(mk("tensor"))
            block.vector(mk("vector"))
            block.scalar(mk("scalar"))
            block.gpsimd(mk("gpsimd"))


def _host_constants():
    c = {}
    c["c_ident"] = np.eye(128, dtype=np.float32)
    c["c_ident16"] = np.eye(128, dtype=np.float16)
    jj, ii = np.meshgrid(np.arange(128), np.arange(128), indexing="ij")
    c["c_slmask"] = (jj < ii).astype(np.float32)
    c["c_maskbias"] = np.where(jj <= ii, 0.0, NEG).astype(np.float32)
    c["c_ones_row"] = np.ones((1, 128), dtype=np.float32)
    c["c_misc"] = np.zeros((128, 4), dtype=np.float32)
    c["c_misc"][:, 0] = 1e-5
    c["c_ones_col"] = np.ones((128, 1), dtype=np.float32)
    c["c_ones_all"] = np.ones((128, 128), dtype=np.float32)
    pos = (np.arange(NTOK) % SEQ).astype(np.float64)
    inv = 1.0 / (10000.0 ** (np.arange(0, 16, 2, dtype=np.float64) / 16))
    fr = np.outer(pos, inv)
    c["c_ropecos"] = np.cos(fr).astype(np.float32)
    c["c_ropesin"] = np.sin(fr).astype(np.float32)
    return c


def _prep_inputs(inputs):
    x = np.ascontiguousarray(np.asarray(inputs["L0_attn_hidden_states"], np.float32))
    shards = x.reshape(NCORE, 2, SEQ, D).reshape(NCORE, NTOK, D)
    com = {}
    com["wqkvT"] = np.ascontiguousarray(np.asarray(inputs["L0_attn_Wqkv"]).T)
    com["bqkv"] = np.asarray(inputs["L0_attn_bqkv"]).reshape(1, 3 * D)
    com["wdT"] = np.ascontiguousarray(np.asarray(inputs["L0_attn_Wd"]).T)
    com["bd"] = np.ascontiguousarray(np.asarray(inputs["L0_attn_bd"]).reshape(8, 128).T)
    com["w0g"] = np.ascontiguousarray(np.asarray(inputs["L0_ffn_w_gate"]))
    com["w0u"] = np.ascontiguousarray(np.asarray(inputs["L0_ffn_w_up"]))
    com["w0d"] = np.ascontiguousarray(np.asarray(inputs["L0_ffn_w_down"]))
    com["wq1T"] = np.ascontiguousarray(np.asarray(inputs["L1_attn_w_q"]).reshape(D, D).T)
    com["wk1T"] = np.ascontiguousarray(np.asarray(inputs["L1_attn_w_k"]).reshape(D, D).T)
    com["wv1T"] = np.ascontiguousarray(np.asarray(inputs["L1_attn_w_v"]).reshape(D, D).T)
    com["wo1"] = np.ascontiguousarray(np.asarray(inputs["L1_attn_w_o"]))
    com["w1gT"] = np.ascontiguousarray(np.asarray(inputs["L1_ffn_w_gate"]).T)
    com["w1uT"] = np.ascontiguousarray(np.asarray(inputs["L1_ffn_w_up"]).T)
    com["w1dT"] = np.ascontiguousarray(np.asarray(inputs["L1_ffn_w_down"]).T)
    com.update(_host_constants())
    return shards, com


def build_nc(dbg=()):
    nc = bass.Bass("TRN2", target_bir_lowering=False, debug=False)
    st = contextlib.ExitStack()

    din = {}
    def dram_in(name, shape, dt=F32):
        din[name] = nc.dram_tensor(name, shape, dt, kind="ExternalInput").ap()
        return din[name]

    x_in = dram_in("x", [NTOK, D])
    wqkvT = dram_in("wqkvT", [D, 3 * D])
    bqkv = dram_in("bqkv", [1, 3 * D])
    wdT = dram_in("wdT", [D, D])
    bd_in = dram_in("bd", [128, 8])
    w0g = dram_in("w0g", [DFF0, D])
    w0u = dram_in("w0u", [DFF0, D])
    w0d = dram_in("w0d", [D, DFF0])
    wq1T = dram_in("wq1T", [D, D])
    wk1T = dram_in("wk1T", [D, D])
    wv1T = dram_in("wv1T", [D, D])
    wo1 = dram_in("wo1", [D, D])
    w1gT = dram_in("w1gT", [D, DFF1])
    w1uT = dram_in("w1uT", [D, DFF1])
    w1dT = dram_in("w1dT", [DFF1, D])
    c_ident = dram_in("c_ident", [128, 128])
    c_ident16 = dram_in("c_ident16", [128, 128], F16)
    c_slmask = dram_in("c_slmask", [128, 128])
    c_maskbias = dram_in("c_maskbias", [128, 128])
    c_ones_row = dram_in("c_ones_row", [1, 128])
    c_misc = dram_in("c_misc", [128, 4])
    c_ones_col = dram_in("c_ones_col", [128, 1])
    c_ones_all = dram_in("c_ones_all", [128, 128])
    c_ropecos = dram_in("c_ropecos", [NTOK, 8])
    c_ropesin = dram_in("c_ropesin", [NTOK, 8])

    y_out = nc.dram_tensor("y", [NTOK, D], F32, kind="ExternalOutput").ap()
    scr = {}
    def dram_scr(name, shape, dt):
        scr[name] = nc.dram_tensor(name, shape, dt).ap()
        return scr[name]

    s_qhi = dram_scr("s_qhi", [D, NTOK], F16)
    s_qlo = dram_scr("s_qlo", [D, NTOK], F16)
    s_khi = dram_scr("s_khi", [D, NTOK], F16)
    s_klo = dram_scr("s_klo", [D, NTOK], F16)
    s_v = dram_scr("s_v", [NTOK, D], F32)
    s_xqhi = dram_scr("s_xqhi", [D, NTOK], F16)
    s_xqlo = dram_scr("s_xqlo", [D, NTOK], F16)
    s_a = dram_scr("s_a", [DFF0, NTOK], F32)
    s_b = dram_scr("s_b", [DFF0, NTOK], F32)
    s_attn0 = dram_scr("s_attn0", [D, NTOK], F32)
    s_ffn0 = dram_scr("s_ffn0", [D, NTOK], F32)
    s_afmh = dram_scr("s_afmh", [D, NTOK], F16)
    s_afml = dram_scr("s_afml", [D, NTOK], F16)
    s_x1 = dram_scr("s_x1", [NTOK, D], F32)
    cc_in = dram_scr("cc_in", [1, 8], F32)
    cc_out = dram_scr("cc_out", [1, 8], F32)

    dbg_out = {}
    for name, shape, dt in dbg:
        dbg_out[name] = nc.dram_tensor("dbg_" + name, list(shape), dt,
                                       kind="ExternalOutput").ap()

    def sb(name, shape, dt=F32):
        return st.enter_context(nc.sbuf_tensor(name, shape, dt))

    ident = sb("ident", [128, 128])
    ident16 = sb("ident16", [128, 128], F16)
    slmask = sb("slmask", [128, 128])
    maskbias = sb("maskbias", [128, 128])
    ones_row = sb("ones_row", [1, 128])
    misc = sb("misc", [128, 4])
    ones_col = sb("ones_col", [128, 1])
    ones_all = sb("ones_all", [128, 128])
    ropec = sb("ropec", [128, NT * 8])
    ropes = sb("ropes", [128, NT * 8])
    stats_m = sb("stats_m", [128, NT])
    stats_r = sb("stats_r", [128, NT])
    cells = sb("cells", [128, 32])
    gmax = sb("gmax", [1, 32])
    stats1_m = sb("stats1_m", [128, NT])
    stats1_r = sb("stats1_r", [128, NT])
    rsff = sb("rsff", [128, NT])
    rso = sb("rso", [128, NT])

    pb = [st.enter_context(nc.psum_tensor("pbk%d" % i, [128, 512], F32)) for i in range(8)]

    b = Bld(nc, st)
    E = b.emit

    def ld(dst, src, key, eng="sync"):
        E(eng, lambda e, d=dst, s=src: e.dma_start(out=d, in_=s),
          writes=[key], dma=True, tag="ld:" + key)

    ld(ident[:, :], c_ident[:, :], "ident")
    ld(ident16[:, :], c_ident16[:, :], "ident16")
    ld(slmask[:, :], c_slmask[:, :], "slmask")
    ld(maskbias[:, :], c_maskbias[:, :], "maskbias")
    ld(ones_row[:, :], c_ones_row[:, :], "ones_row")
    ld(misc[:, :], c_misc[:, :], "misc")
    ld(ones_col[:, :], c_ones_col[:, :], "ones_col")
    ld(ones_all[:, :], c_ones_all[:, :], "ones_all")
    ld(ropec[:, :].rearrange("p (t f) -> p t f", f=8),
       c_ropecos.rearrange("(t p) f -> p t f", p=128), "ropec")
    ld(ropes[:, :].rearrange("p (t f) -> p t f", f=8),
       c_ropesin.rearrange("(t p) f -> p t f", p=128), "ropes")

    def bcast_cells(src_ap, src_keys, dst_ap, dst_key, n, bank=1):
        E("tensor", lambda e: e.matmul(pb[bank][:, 0:n], ones_row[:, :], src_ap,
                                       start=True, stop=True),
          reads=list(src_keys) + ["ones_row"], writes=[("pb", bank)], tag="bc:mm")
        E("vector", lambda e: e.tensor_copy(dst_ap, pb[bank][:, 0:n]),
          reads=[("pb", bank)], writes=[dst_key], tag="bc:ev")

    with nc.sbuf_tensor("p0x", [128, 2, D], F32) as p0x, \
         nc.sbuf_tensor("p0tmp", [128, 2, D], F32) as p0tmp, \
         nc.sbuf_tensor("p0dump", [128, D], F32) as p0dump, \
         nc.sbuf_tensor("p0cols", [128, 4 * NT], F32) as p0cols, \
         nc.sbuf_tensor("p0mx", [128, NT], F32) as p0mx:
        for t in range(NT):
            pbuf = p0x[:, t % 2, :]
            tbuf = p0tmp[:, t % 2, :]
            kx = ("p0x", t % 2)
            kt = ("p0t", t % 2)
            E("sync", lambda e, d=pbuf, s=x_in[t * 128:(t + 1) * 128, :]:
              e.dma_start(out=d[:, :], in_=s), writes=[kx], dma=True, tag="p0:dma")
            sm = p0cols[:, t:t + 1]
            sq = p0cols[:, NT + t:NT + t + 1]
            E("vector", lambda e, d=sm, s=pbuf: e.reduce_sum(d, s[:, :], axis=AX.X),
              reads=[kx], writes=[("sm", t)], tag="p0:sum")
            E("scalar", lambda e, d=sq, s=pbuf, dd=p0dump:
              e.activation(dd[:, :], s[:, :], AF.Square, accum_out=d),
              reads=[kx], writes=[("sq", t), "p0dump"], tag="p0:sq")
            mcol = stats_m[:, t:t + 1]
            rcol = stats_r[:, t:t + 1]
            vcol = p0cols[:, 2 * NT + t:2 * NT + t + 1]
            E("vector", lambda e, d=mcol, s=sm: e.tensor_scalar_mul(d, s, 1.0 / D),
              reads=[("sm", t)], writes=[("m", t)], tag="p0:m")
            E("vector", lambda e, d=vcol, s=mcol: e.tensor_tensor(d, s, s, ALU.mult),
              reads=[("m", t)], writes=[("v", t)], tag="p0:msq")
            E("vector", lambda e, d=vcol, s=sq, m2=vcol:
              e.scalar_tensor_tensor(d, s, 1.0 / D, m2, ALU.mult, ALU.subtract),
              reads=[("sq", t), ("v", t)], writes=[("v", t)], tag="p0:var")
            E("scalar", lambda e, d=vcol, s=vcol:
              e.activation(d, s, AF.Sqrt, bias=misc[:, 0:1], scale=1.0),
              reads=[("v", t), "misc"], writes=[("v", t)], tag="p0:sqrt")
            E("vector", lambda e, d=rcol, s=vcol: e.reciprocal(d, s),
              reads=[("v", t)], writes=[("r", t)], tag="p0:recip")
            E("vector", lambda e, d=tbuf, s=pbuf, m=mcol:
              e.tensor_scalar(d[:, :], s[:, :], m, None, ALU.subtract),
              reads=[kx, ("m", t)], writes=[kt], tag="p0:sub")
            tm = p0cols[:, 3 * NT + t:3 * NT + t + 1]
            E("vector", lambda e, d=tm, s=tbuf:
              e.reduce_max(d, s[:, :], axis=AX.X, apply_absolute_value=True),
              reads=[kt], writes=[("tm", t)], tag="p0:rmax")
            E("vector", lambda e, d=p0mx[:, t:t + 1], s=tm, r=rcol:
              e.tensor_tensor(d, s, r, ALU.mult),
              reads=[("tm", t), ("r", t)], writes=[("mx", t)], tag="p0:mx")
        mxc = p0cols[:, 0:1]
        E("vector", lambda e, d=mxc, s=p0mx: e.reduce_max(d, s[:, :], axis=AX.X),
          reads=[("mx", t) for t in range(NT)], writes=["mxc"], tag="p0:gmax1")
        E("tensor", lambda e: e.transpose(pb[0][0:1, 0:128], mxc, ident[:, :]),
          reads=["mxc", "ident"], writes=[("pb", 0)], tag="p0:tr")
        E("vector", lambda e: e.reduce_max(gmax[0:1, 0:1], pb[0][0:1, 0:128], axis=AX.X),
          reads=[("pb", 0)], writes=["gmax0"], tag="p0:gmax2")
        E("gpsimd", lambda e: e.dma_start(out=cc_in[0:1, 0:1], in_=gmax[0:1, 0:1]),
          reads=["gmax0"], writes=["cc_in"], dma=True, tag="r1:out")
        E("gpsimd", lambda e: e.collective_compute(
            "AllReduce", ALU.max, replica_groups=[list(range(NCORE))],
            ins=[cc_in[0:1, 0:1].opt()], outs=[cc_out[0:1, 0:1].opt()]),
          reads=["cc_in"], writes=["cc_out"], war=["cc_out"], tag="r1:cc")
        E("gpsimd", lambda e: e.dma_start(out=gmax[0:1, 1:2], in_=cc_out[0:1, 0:1]),
          reads=["cc_out"], writes=["gmax1"], dma=True, tag="r1:in")
        E("vector", lambda e: e.tensor_scalar(gmax[0:1, 2:3], gmax[0:1, 1:2],
                                              1.0 / 32767.0, 1e-12, ALU.mult, ALU.max),
          reads=["gmax1"], writes=["s1c"], tag="r1:s1")
        E("vector", lambda e: e.reciprocal(gmax[0:1, 3:4], gmax[0:1, 2:3]),
          reads=["s1c"], writes=["r1c"], tag="r1:recip")
        bcast_cells(gmax[0:1, 2:4], ["s1c", "r1c"], cells[:, 1:3], "cells_s1r1", 2)
        if "gmax" in dbg_out:
            E("sync", lambda e: e.dma_start(out=dbg_out["gmax"][:, :], in_=gmax[0:1, 0:4]),
              reads=["gmax0", "gmax1", "s1c", "r1c"], writes=["dbg"], dma=True)
        b.barrier()

    w0scope = contextlib.ExitStack()
    wqkv_hi = w0scope.enter_context(nc.sbuf_tensor("wqkv_hi", [128, 8, 3 * D], F16))
    wqkv_lo = w0scope.enter_context(nc.sbuf_tensor("wqkv_lo", [128, 8, 3 * D], F16))
    with nc.sbuf_tensor("w0tile", [128, 2, 3 * D], F32) as w0tile:
        for c in range(8):
            kw = ("w0t", c % 2)
            buf = w0tile[:, c % 2, :]
            E("sync", lambda e, d=buf, s=wqkvT[c * 128:(c + 1) * 128, :]:
              e.dma_start(out=d[:, :], in_=s), writes=[kw], dma=True, tag="w0:dma")
            E("vector", lambda e, d=wqkv_hi[:, c, :], s=buf: e.tensor_copy(d, s[:, :]),
              reads=[kw], writes=[("wqh", c)], tag="w0:hi")
            E("vector", lambda e, d=wqkv_lo[:, c, :], s=buf, h=wqkv_hi[:, c, :]:
              e.tensor_tensor(d, s[:, :], h, ALU.subtract),
              reads=[kw, ("wqh", c)], writes=[("wql", c)], tag="w0:lo")
        b.barrier()

    with nc.sbuf_tensor("p1x", [128, 2, D], F32) as p1x, \
         nc.sbuf_tensor("p1xn", [128, 4, D], F32) as p1xn, \
         nc.sbuf_tensor("xnfm", [128, 8, 512], F32) as xnfm, \
         nc.sbuf_tensor("xnfm_hi", [128, 8, 512], F16) as xnfm_hi, \
         nc.sbuf_tensor("xnfm_lo", [128, 8, 512], F16) as xnfm_lo, \
         nc.sbuf_tensor("xq16", [128, 512], I32) as xq16, \
         nc.sbuf_tensor("xqsh", [128, 512], I32) as xqsh, \
         nc.sbuf_tensor("xqcl", [128, 512], I32) as xqcl, \
         nc.sbuf_tensor("xqhi", [128, 512], F16) as xqhi, \
         nc.sbuf_tensor("xqlo", [128, 512], F16) as xqlo, \
         nc.sbuf_tensor("biasbc", [128, 3 * D], F32) as biasbc, \
         nc.sbuf_tensor("bqsb", [1, 3 * D], F32) as bqsb, \
         nc.sbuf_tensor("qkvst", [128, 4 * 512], F32) as qkvst, \
         nc.sbuf_tensor("ropet", [128, 4, 256], F32) as ropet, \
         nc.sbuf_tensor("sthi", [128, 2, 512], F16) as sthi, \
         nc.sbuf_tensor("stlo", [128, 2, 512], F16) as stlo:

        ld(bqsb[:, :], bqkv[:, :], "bqsb")
        for ot in range(6):
            bcast_cells(bqsb[0:1, ot * 512:(ot + 1) * 512], ["bqsb"],
                        biasbc[:, ot * 512:(ot + 1) * 512], ("biasbc", ot), 512, bank=7)

        for sup in range(8):
            for tt in range(4):
                t = sup * 4 + tt
                kx = ("p1x", t % 2)
                E("sync", lambda e, d=p1x[:, t % 2, :], s=x_in[t * 128:(t + 1) * 128, :]:
                  e.dma_start(out=d[:, :], in_=s), writes=[kx], dma=True, tag="p1:dmax")
                kxn = ("p1xn", tt)
                E("vector", lambda e, d=p1xn[:, tt, :], s=p1x[:, t % 2, :],
                  m=stats_m[:, t:t + 1], r=stats_r[:, t:t + 1]:
                  e.tensor_scalar(d[:, :], s[:, :], m, r, ALU.subtract, ALU.mult),
                  reads=[kx, ("m", t), ("r", t)], writes=[kxn], tag="p1:xn")
                for c in range(8):
                    bank = c % 2 + 2
                    E("tensor", lambda e, tt=tt, c=c, bank=bank:
                      e.transpose(pb[bank][:, 0:128],
                                  p1xn[:, tt, c * 128:(c + 1) * 128], ident[:, :]),
                      reads=[("p1xn", tt), "ident"], writes=[("pb", bank)], tag="p1:tr")
                    E("vector", lambda e, tt=tt, c=c, bank=bank:
                      e.tensor_copy(xnfm[:, c, tt * 128:(tt + 1) * 128], pb[bank][:, 0:128]),
                      reads=[("pb", bank)], writes=[("xnfm", c, tt)], tag="p1:ev")
            for c in range(8):
                rk = [("xnfm", c, tt) for tt in range(4)]
                E("vector", lambda e, c=c: e.tensor_copy(xnfm_hi[:, c, :], xnfm[:, c, :]),
                  reads=rk, writes=[("xnhi", c)], tag="p1:hi")
                E("vector", lambda e, c=c: e.tensor_tensor(
                    xnfm_lo[:, c, :], xnfm[:, c, :], xnfm_hi[:, c, :], ALU.subtract),
                  reads=rk + [("xnhi", c)], writes=[("xnlo", c)], tag="p1:lo")
                E("vector", lambda e, c=c: e.tensor_scalar(
                    xq16[:, :], xnfm[:, c, :], cells[:, 2:3], None, ALU.mult),
                  reads=rk + ["cells_s1r1"], writes=["xq16"], tag="p1:xq")
                E("vector", lambda e: e.tensor_scalar(
                    xqcl[:, :], xq16[:, :], -32768.0, 32767.0, ALU.max, ALU.min),
                  reads=["xq16"], writes=["xqcl"], tag="p1:xqc")
                E("vector", lambda e: e.tensor_scalar(
                    xqsh[:, :], xqcl[:, :], 1.0 / 256.0, None, ALU.mult),
                  reads=["xqcl"], writes=["xqsh"], tag="p1:xqs")
                E("vector", lambda e: e.tensor_scalar(
                    xqhi[:, :], xqsh[:, :], 256.0, None, ALU.mult),
                  reads=["xqsh"], writes=["xqhi"], tag="p1:xqh")
                E("vector", lambda e: e.scalar_tensor_tensor(
                    xqlo[:, :], xqsh[:, :], -256.0, xqcl[:, :], ALU.mult, ALU.add),
                  reads=["xqsh", "xqcl"], writes=["xqlo"], tag="p1:xql")
                E("gpsimd", lambda e, c=c, sup=sup: e.dma_start(
                    out=s_xqhi[c * 128:(c + 1) * 128, sup * 512:(sup + 1) * 512],
                    in_=xqhi[:, :]),
                  reads=["xqhi"], writes=[("s_xqhi", c, sup)], dma=True, tag="p1:stxqh")
                E("gpsimd", lambda e, c=c, sup=sup: e.dma_start(
                    out=s_xqlo[c * 128:(c + 1) * 128, sup * 512:(sup + 1) * 512],
                    in_=xqlo[:, :]),
                  reads=["xqlo"], writes=[("s_xqlo", c, sup)], dma=True, tag="p1:stxql")

            xn_r = [("xnhi", c) for c in range(8)] + [("xnlo", c) for c in range(8)]
            for ot in range(6):
                for tt in range(4):
                    bank = tt % 2 + 4
                    kst = ("qkvst", tt)
                    for c in range(8):
                        for p, (sa, sw) in enumerate((("hi", "hi"), ("hi", "lo"), ("lo", "hi"))):
                            act = xnfm_hi if sa == "hi" else xnfm_lo
                            wgt = wqkv_hi if sw == "hi" else wqkv_lo
                            first = (c == 0 and p == 0)
                            lastm = (c == 7 and p == 2)
                            E("tensor", lambda e, act=act, wgt=wgt, c=c, tt=tt, ot=ot,
                              bank=bank, first=first, lastm=lastm:
                              e.matmul(pb[bank][:, 0:512],
                                       act[:, c, tt * 128:(tt + 1) * 128],
                                       wgt[:, c, ot * 512:(ot + 1) * 512],
                                       start=first, stop=lastm),
                              reads=(xn_r + [("wqh", c), ("wql", c)]),
                              writes=[("pb", bank)] if lastm else (),
                              war=[("pb", bank)] if first else (),
                              tag="p1:mm")
                    E("vector", lambda e, tt=tt, ot=ot, bank=bank: e.tensor_tensor(
                        qkvst[:, tt * 512:(tt + 1) * 512], pb[bank][:, 0:512],
                        biasbc[:, ot * 512:(ot + 1) * 512], ALU.add),
                      reads=[("pb", bank), ("biasbc", ot)], writes=[kst], tag="p1:qkvev")
                kst_all = [("qkvst", tt) for tt in range(4)]
                if ot < 4:
                    qv = qkvst.rearrange("p (T h d) -> p T h d", T=4, d=64)
                    r0 = qv[:, :, :, 0:8]
                    r1 = qv[:, :, :, 8:16]
                    rcv = ropec[:, sup * 32:(sup + 1) * 32] \
                        .rearrange("p (T f) -> p T f", f=8).unsqueeze(2) \
                        .broadcast_to([128, 4, 8, 8])
                    rsv = ropes[:, sup * 32:(sup + 1) * 32] \
                        .rearrange("p (T f) -> p T f", f=8).unsqueeze(2) \
                        .broadcast_to([128, 4, 8, 8])
                    t0 = ropet[:, 0, :].rearrange("p (T h f) -> p T h f", T=4, f=8)
                    t1 = ropet[:, 1, :].rearrange("p (T h f) -> p T h f", T=4, f=8)
                    u0 = ropet[:, 2, :].rearrange("p (T h f) -> p T h f", T=4, f=8)
                    u1 = ropet[:, 3, :].rearrange("p (T h f) -> p T h f", T=4, f=8)
                    E("vector", lambda e, r0=r0, rcv=rcv, t0=t0: e.tensor_tensor(
                        t0, r0, rcv, ALU.mult), reads=kst_all + ["ropec"],
                      writes=["rt0"], tag="p1:ro0")
                    E("vector", lambda e, r1=r1, rcv=rcv, t1=t1: e.tensor_tensor(
                        t1, r1, rcv, ALU.mult), reads=kst_all + ["ropec"],
                      writes=["rt1"], tag="p1:ro1")
                    E("vector", lambda e, r1=r1, rsv=rsv, u0=u0: e.tensor_tensor(
                        u0, r1, rsv, ALU.mult), reads=kst_all + ["ropes"],
                      writes=["ru0"], tag="p1:ro2")
                    E("vector", lambda e, r0=r0, rsv=rsv, u1=u1: e.tensor_tensor(
                        u1, r0, rsv, ALU.mult), reads=kst_all + ["ropes"],
                      writes=["ru1"], tag="p1:ro3")
                    E("vector", lambda e, r0=r0, t0=t0, u0=u0: e.tensor_tensor(
                        r0, t0, u0, ALU.subtract), reads=["rt0", "ru0"],
                      writes=kst_all, tag="p1:ro4")
                    E("vector", lambda e, r1=r1, t1=t1, u1=u1: e.tensor_tensor(
                        r1, t1, u1, ALU.add), reads=["rt1", "ru1"],
                      writes=kst_all, tag="p1:ro5")
                    for osub in range(4):
                        par = osub % 2
                        khi = ("sthi", par)
                        klo = ("stlo", par)
                        bank2 = 6 + par
                        for tt in range(4):
                            E("tensor", lambda e, tt=tt, osub=osub, bank2=bank2:
                              e.transpose(pb[bank2][:, tt * 128:(tt + 1) * 128],
                                          qkvst[:, tt * 512 + osub * 128:
                                                tt * 512 + (osub + 1) * 128],
                                          ident[:, :]),
                              reads=kst_all + ["ident"],
                              writes=[("pb", bank2)] if tt == 3 else (),
                              war=[("pb", bank2)] if tt == 0 else (),
                              tag="p1:qktr")
                        E("vector", lambda e, bank2=bank2, par=par: e.tensor_copy(
                            sthi[:, par, :], pb[bank2][:, 0:512]),
                          reads=[("pb", bank2)], writes=[khi], tag="p1:sthi")
                        E("vector", lambda e, bank2=bank2, par=par: e.tensor_tensor(
                            stlo[:, par, :], pb[bank2][:, 0:512], sthi[:, par, :],
                            ALU.subtract),
                          reads=[("pb", bank2), khi], writes=[klo], tag="p1:stlo")
                        oc = ot * 4 + osub
                        if oc < 8:
                            dhi, dlo, row = s_qhi, s_qlo, oc * 128
                        else:
                            dhi, dlo, row = s_khi, s_klo, (oc - 8) * 128
                        E("gpsimd", lambda e, d=dhi, row=row, par=par, sup=sup: e.dma_start(
                            out=d[row:row + 128, sup * 512:(sup + 1) * 512],
                            in_=sthi[:, par, :]),
                          reads=[khi], writes=[("sqk", oc, sup, "h")], dma=True,
                          tag="p1:dmah")
                        E("gpsimd", lambda e, d=dlo, row=row, par=par, sup=sup: e.dma_start(
                            out=d[row:row + 128, sup * 512:(sup + 1) * 512],
                            in_=stlo[:, par, :]),
                          reads=[klo], writes=[("sqk", oc, sup, "l")], dma=True,
                          tag="p1:dmal")
                else:
                    for tt in range(4):
                        t = sup * 4 + tt
                        E("gpsimd", lambda e, tt=tt, t=t, ot=ot: e.dma_start(
                            out=s_v[t * 128:(t + 1) * 128,
                                    (ot - 4) * 512:(ot - 3) * 512],
                            in_=qkvst[:, tt * 512:(tt + 1) * 512]),
                          reads=[("qkvst", tt)], writes=[("s_v", t, ot)], dma=True,
                          tag="p1:dmav")
        b.barrier()


    b.barrier()
    w0scope.close()
    with nc.sbuf_tensor("qk4", [65, 4, SEQ], F16) as qk4, \
         nc.sbuf_tensor("vbuf", [128, 16, HD + 1], F32) as vbuf, \
         nc.sbuf_tensor("estrip", [128, 2, SEQ], F32) as estrip, \
         nc.sbuf_tensor("abuf", [128, 16, 65], F32) as abuf, \
         nc.sbuf_tensor("acol", [128, 2], F32) as acol, \
         nc.sbuf_tensor("dtile", [128, 128], F32) as dtile, \
         nc.sbuf_tensor("recs", [128, 1], F32) as recs, \
         nc.sbuf_tensor("stile", [128, 2, 128], F32) as stile, \
         nc.sbuf_tensor("ntile", [128, 2, 128], F32) as ntile, \
         nc.sbuf_tensor("btile", [128, 2, 64], F32) as btile, \
         nc.sbuf_tensor("afmst", [64, 2, SEQ], F16) as afmst:
        E("vector", lambda e: e.memset(vbuf[:, :, 64:65], 0.0),
          writes=["vzero"], tag="a0:vz")
        for sh in range(32):
            s, h = sh // 16, sh % 16
            for slot, src_t in ((0, s_qhi), (1, s_qlo), (2, s_khi), (3, s_klo)):
                E("sync", lambda e, slot=slot, src_t=src_t, s=s, h=h: e.dma_start(
                    out=qk4[0:64, slot, :],
                    in_=src_t[h * 64:(h + 1) * 64, s * SEQ:(s + 1) * SEQ]),
                  writes=[("qk", slot)], dma=True, tag="a0:ldqk")
            E("vector", lambda e: e.memset(qk4[64:65, 2, :], 1.0),
              writes=[("qka", 2)], war=[("qka", 2)], tag="a0:aug1")
            E("vector", lambda e: e.memset(qk4[64:65, 3, :], 0.0),
              writes=[("qka", 3)], war=[("qka", 3)], tag="a0:aug0")
            E("sync", lambda e, s=s, h=h: e.dma_start(
                out=vbuf[:, :, 0:64],
                in_=s_v[s * SEQ:(s + 1) * SEQ, h * 64:(h + 1) * 64]
                    .rearrange("(j p) c -> p j c", p=128)),
              writes=["vbuf"], dma=True, tag="a0:ldv")
            E("vector", lambda e: e.memset(abuf[:, :, 64:65], 1.0),
              war=[("A", j) for j in range(16)], writes=["Aones"], tag="a0:ones")
            for i in range(16):
                par = i % 2
                j0 = (i - 1) * 128 if i > 0 else 0
                w = 256 if i > 0 else 128
                E("tensor", lambda e, i=i, j0=j0, w=w: e.matmul(
                    pb[6][:, 0:w], qk4[0:64, 0, i * 128:(i + 1) * 128],
                    qk4[0:64, 2, j0:j0 + w], start=True, stop=True),
                  reads=[("qk", 0), ("qk", 2)], writes=[("pb", 6)], tag="a0:sbm")
                E("vector", lambda e, w=w: e.reduce_max(
                    acol[:, 0:1], pb[6][:, 0:w], axis=AX.X),
                  reads=[("pb", 6)], writes=["acol0"], tag="a0:sbr")
                E("vector", lambda e: e.tensor_scalar(
                    acol[:, 1:2], acol[:, 0:1], 320.0, -1.0, ALU.add, ALU.mult),
                  reads=["acol0"], writes=["acol1"], tag="a0:sba")
                E("tensor", lambda e: e.matmul(
                    pb[6][64:65, 0:128], acol[:, 1:2], ident[:, :],
                    start=True, stop=True),
                  reads=["acol1", "ident"], writes=[("pb", 6)], tag="a0:sbt")
                E("vector", lambda e, i=i: e.tensor_copy(
                    qk4[64:65, 0, i * 128:(i + 1) * 128], pb[6][64:65, 0:128]),
                  reads=[("pb", 6)], writes=[("qka", 0)], war=[("qka", 0)],
                  tag="a0:sbh")
                E("vector", lambda e, i=i: e.tensor_tensor(
                    qk4[64:65, 1, i * 128:(i + 1) * 128], pb[6][64:65, 0:128],
                    qk4[64:65, 0, i * 128:(i + 1) * 128], ALU.subtract),
                  reads=[("pb", 6), ("qka", 0)], writes=[("qka", 1)],
                  war=[("qka", 1)], tag="a0:sbl")
                for j in range(i + 1):
                    bank = j % 2
                    for p3, (aa, bb2) in enumerate(((0, 2), (1, 2), (0, 3))):
                        E("tensor", lambda e, j=j, i=i, bank=bank, p3=p3, aa=aa, bb2=bb2:
                          e.matmul(pb[bank][:, 0:128],
                                   qk4[:, bb2, j * 128:(j + 1) * 128],
                                   qk4[:, aa, i * 128:(i + 1) * 128],
                                   start=(p3 == 0), stop=(p3 == 2)),
                          reads=[("qk", aa), ("qk", bb2), ("qka", aa), ("qka", bb2)],
                          writes=[("pb", bank)] if p3 == 2 else (),
                          war=[("pb", bank)] if p3 == 0 else (), tag="a0:qk")
                    if j == i:
                        E("vector", lambda e, j=j, par=par, bank=bank: e.tensor_tensor(
                            estrip[:, par, j * 128:(j + 1) * 128], pb[bank][:, 0:128],
                            maskbias[:, :], ALU.add),
                          reads=[("pb", bank), "maskbias"], writes=[("em", par)],
                          tag="a0:mask")
                        E("scalar", lambda e, j=j, par=par: e.activation(
                            estrip[:, par, j * 128:(j + 1) * 128],
                            estrip[:, par, j * 128:(j + 1) * 128],
                            AF.Exp, bias=misc[:, 2:3], scale=0.125),
                          reads=[("em", par), "misc"], writes=[("es", par, j)],
                          tag="a0:expd")
                    else:
                        E("scalar", lambda e, j=j, par=par, bank=bank: e.activation(
                            estrip[:, par, j * 128:(j + 1) * 128], pb[bank][:, 0:128],
                            AF.Exp, bias=misc[:, 2:3], scale=0.125),
                          reads=[("pb", bank), "misc"], writes=[("es", par, j)],
                          tag="a0:exp")
                E("vector", lambda e, i=i, par=par: e.tensor_tensor(
                    dtile[:, :], estrip[:, par, i * 128:(i + 1) * 128], ident[:, :],
                    ALU.mult),
                  reads=[("es", par, i), "ident"], writes=["dtile"], tag="a0:dt")
                E("tensor", lambda e, i=i: e.matmul(
                    pb[2][:, 0:65], dtile[:, :], vbuf[:, i, :], start=True, stop=False),
                  reads=["dtile", "vbuf", "vzero"], war=[("pb", 2)], tag="a0:dv")
                for j in range(i):
                    E("tensor", lambda e, j=j, par=par: e.matmul(
                        pb[2][:, 0:65], estrip[:, par, j * 128:(j + 1) * 128],
                        abuf[:, j, :], start=False, stop=False),
                      reads=[("es", par, j), ("A", j), "Aones"], tag="a0:umm")
                E("tensor", lambda e, i=i, par=par: e.matmul(
                    pb[2][:, 64:65], estrip[:, par, i * 128:(i + 1) * 128],
                    ones_col[:, :], start=False, stop=True),
                  reads=[("es", par, i), "ones_col"],
                  writes=[("pb", 2)], tag="a0:dsum")
                E("vector", lambda e: e.reciprocal(recs[:, :], pb[2][:, 64:65]),
                  reads=[("pb", 2)], writes=["recs"], tag="a0:recs")
                E("vector", lambda e, i=i, par=par: e.scalar_tensor_tensor(
                    stile[:, 0, :], estrip[:, par, i * 128:(i + 1) * 128],
                    recs[:, :], slmask[:, :], ALU.mult, ALU.mult),
                  reads=[("es", par, i), "recs", "slmask"], writes=[("St", 0)],
                  tag="a0:nprep")
                E("tensor", lambda e: e.transpose(pb[3][:, 0:128], stile[:, 0, :],
                                                  ident[:, :]),
                  reads=[("St", 0), "ident"], writes=[("pb", 3)], tag="a0:ntr")
                E("vector", lambda e: e.tensor_copy(ntile[:, 0, :], pb[3][:, 0:128]),
                  reads=[("pb", 3)], writes=[("Nt", 0)], tag="a0:nev")
                E("scalar", lambda e: e.copy(btile[:, 0, :], pb[2][:, 0:64]),
                  reads=[("pb", 2)], writes=[("Bt", 0)], tag="a0:b0")
                for k in range(7):
                    sp = k % 2
                    np2 = (k + 1) % 2
                    E("tensor", lambda e, sp=sp: e.matmul(
                        pb[7][:, 0:64], stile[:, sp, :], btile[:, sp, :],
                        start=True, stop=True),
                      reads=[("St", sp), ("Bt", sp)], writes=[("pb", 7)],
                      tag="a0:bmm")
                    E("vector", lambda e, sp=sp, np2=np2: e.tensor_tensor(
                        btile[:, np2, :], btile[:, sp, :], pb[7][:, 0:64], ALU.add),
                      reads=[("pb", 7), ("Bt", sp)], writes=[("Bt", np2)],
                      tag="a0:badd")
                    if k < 6:
                        E("tensor", lambda e, sp=sp: e.matmul(
                            pb[3][:, 0:128], ntile[:, sp, :], stile[:, sp, :],
                            start=True, stop=True),
                          reads=[("Nt", sp), ("St", sp)], writes=[("pb", 3)],
                          tag="a0:smm")
                        E("vector", lambda e, np2=np2: e.tensor_copy(
                            stile[:, np2, :], pb[3][:, 0:128]),
                          reads=[("pb", 3)], writes=[("St", np2)], tag="a0:sev")
                        if k < 5:
                            E("tensor", lambda e, sp=sp: e.matmul(
                                pb[4][:, 0:128], stile[:, sp, :], ntile[:, sp, :],
                                start=True, stop=True),
                              reads=[("St", sp), ("Nt", sp)], writes=[("pb", 4)],
                              tag="a0:nmm")
                            E("vector", lambda e, np2=np2: e.tensor_copy(
                                ntile[:, np2, :], pb[4][:, 0:128]),
                              reads=[("pb", 4)], writes=[("Nt", np2)], tag="a0:nev2")
                E("vector", lambda e, i=i: e.tensor_scalar(
                    abuf[:, i, 0:64], btile[:, 1, :], recs[:, :], None, ALU.mult),
                  reads=[("Bt", 1), "recs"], writes=[("A", i)], tag="a0:afin")
            for i in range(16):
                E("tensor", lambda e, i=i: e.transpose(
                    pb[5][0:64, 0:128], abuf[:, i, 0:64], ident[:, :]),
                  reads=[("A", i), "ident"], writes=[("pb", 5)], tag="a0:atr")
                E("vector", lambda e, i=i: e.tensor_copy(
                    afmst[:, 0, i * 128:(i + 1) * 128], pb[5][0:64, 0:128]),
                  reads=[("pb", 5)], writes=[("afh", i)], tag="a0:ahev")
                E("vector", lambda e, i=i: e.tensor_tensor(
                    afmst[:, 1, i * 128:(i + 1) * 128], pb[5][0:64, 0:128],
                    afmst[:, 0, i * 128:(i + 1) * 128], ALU.subtract),
                  reads=[("pb", 5), ("afh", i)], writes=[("afl", i)], tag="a0:alev")
            E("gpsimd", lambda e, s=s, h=h: e.dma_start(
                out=s_afmh[h * 64:(h + 1) * 64, s * SEQ:(s + 1) * SEQ],
                in_=afmst[:, 0, :]),
              reads=[("afh", i) for i in range(16)], writes=[("safm", sh, 0)],
              dma=True, tag="a0:sth")
            E("gpsimd", lambda e, s=s, h=h: e.dma_start(
                out=s_afml[h * 64:(h + 1) * 64, s * SEQ:(s + 1) * SEQ],
                in_=afmst[:, 1, :]),
              reads=[("afl", i) for i in range(16)], writes=[("safm", sh, 1)],
              dma=True, tag="a0:stl")
        b.barrier()

    with nc.sbuf_tensor("wdsp", [128, 2, 8, D], F16) as wdsp, \
         nc.sbuf_tensor("wdt0", [128, 2, D], F32) as wdt0, \
         nc.sbuf_tensor("bdsb", [128, 8], F32) as bdsb, \
         nc.sbuf_tensor("aft", [128, 2, 8, 512], F16) as aft, \
         nc.sbuf_tensor("aost", [128, 512], F32) as aost:
        ld(bdsb[:, :], bd_in[:, :], "bdsb")
        for c in range(8):
            kw = ("wdt", c % 2)
            E("sync", lambda e, c=c: e.dma_start(
                out=wdt0[:, c % 2, :], in_=wdT[c * 128:(c + 1) * 128, :]),
              writes=[kw], war=[kw], dma=True, tag="ap0:ldw")
            E("vector", lambda e, c=c: e.tensor_copy(wdsp[:, 0, c, :], wdt0[:, c % 2, :]),
              reads=[kw], writes=[("wdh", c)], tag="ap0:hi")
            E("vector", lambda e, c=c: e.tensor_tensor(
                wdsp[:, 1, c, :], wdt0[:, c % 2, :], wdsp[:, 0, c, :], ALU.subtract),
              reads=[kw, ("wdh", c)], writes=[("wdl", c)], tag="ap0:lo")
        for s2 in range(2):
            for tt in range(4):
                col0 = s2 * SEQ + tt * 512
                for sl, srct in ((0, s_afmh), (1, s_afml)):
                    E("sync", lambda e, sl=sl, srct=srct, col0=col0: e.dma_start(
                        out=aft[:, sl, :, :],
                        in_=srct[:, col0:col0 + 512].rearrange("(c p) t -> p c t", p=128)),
                      reads=[("safm", sh2, sl) for sh2 in range(32)],
                      writes=[("aft", sl)], dma=True, tag="ap0:lda")
                for co in range(8):
                    bank = co % 2
                    for c in range(8):
                        for p3, (wa, ab) in enumerate(((0, 0), (0, 1), (1, 0))):
                            first = (c == 0 and p3 == 0)
                            lastm = (c == 7 and p3 == 2)
                            E("tensor", lambda e, co=co, c=c, wa=wa, ab=ab, bank=bank,
                              first=first, lastm=lastm:
                              e.matmul(pb[bank][:, 0:512],
                                       wdsp[:, wa, c, co * 128:(co + 1) * 128],
                                       aft[:, ab, c, :], start=first, stop=lastm),
                              reads=[("wdh", c), ("wdl", c), ("aft", 0), ("aft", 1)],
                              writes=[("pb", bank)] if lastm else (),
                              war=[("pb", bank)] if first else (), tag="ap0:mm")
                    E("vector", lambda e, co=co, bank=bank: e.tensor_scalar(
                        aost[:, :], pb[bank][:, 0:512], bdsb[:, co:co + 1], None,
                        ALU.add),
                      reads=[("pb", bank), "bdsb"], writes=["aost"], tag="ap0:ev")
                    E("gpsimd", lambda e, co=co, col0=col0: e.dma_start(
                        out=s_attn0[co * 128:(co + 1) * 128, col0:col0 + 512],
                        in_=aost[:, :]),
                      reads=["aost"], writes=[("sat", co, col0)], dma=True,
                      tag="ap0:st")
        b.barrier()

    def lpbq_prep(wsrc, nrows, whiT, wloT, kp):
        nrc = nrows // 128
        ncol = int(wsrc.shape[1])
        nblk = ncol // 32
        nchunk = ncol // 128
        with nc.sbuf_tensor("lq_w" + kp, [128, 2, ncol], F32) as lqw, \
             nc.sbuf_tensor("lq_m" + kp, [128, nblk], F32) as lqm, \
             nc.sbuf_tensor("lq_rs" + kp, [128, nblk], F32) as lqrs, \
             nc.sbuf_tensor("lq_t" + kp, [128, ncol], F32) as lqt, \
             nc.sbuf_tensor("lq_q" + kp, [128, ncol], I32) as lqq, \
             nc.sbuf_tensor("lq_dq" + kp, [128, 2, ncol], F32) as lqdq, \
             nc.sbuf_tensor("lq_st" + kp, [128, 128], F32) as lqst:
            for rc in range(nrc):
                par = rc % 2
                kw = (kp, "w", par)
                E("sync", lambda e, rc=rc, par=par: e.dma_start(
                    out=lqw[:, par, :], in_=wsrc[rc * 128:(rc + 1) * 128, :]),
                  writes=[kw], war=[kw], dma=True, tag="f0:lw")
                wv = lqw[:, par, :].rearrange("p (b q) -> p b q", q=32)
                E("vector", lambda e, wv=wv: e.tensor_reduce(
                    lqm[:, :], wv, AX.X, ALU.max, apply_absolute_value=True),
                  reads=[kw], writes=[(kp, "m")], tag="f0:absm")
                E("vector", lambda e: e.tensor_scalar(
                    lqrs[:, :], lqm[:, :], 1.0 / 7.0, 1e-12, ALU.mult, ALU.max),
                  reads=[(kp, "m")], writes=[(kp, "s")], tag="f0:s")
                E("vector", lambda e: e.reciprocal(lqm[:, :], lqrs[:, :]),
                  reads=[(kp, "s")], writes=[(kp, "m")], tag="f0:rs")
                mbc = lqm[:, :].unsqueeze(2).broadcast_to([128, nblk, 32])
                sbc = lqrs[:, :].unsqueeze(2).broadcast_to([128, nblk, 32])
                tv = lqt[:, :].rearrange("p (b q) -> p b q", q=32)
                E("vector", lambda e, wv=wv, mbc=mbc, tv=tv: e.tensor_tensor(
                    tv, wv, mbc, ALU.mult),
                  reads=[kw, (kp, "m")], writes=[(kp, "t")], tag="f0:t")
                E("vector", lambda e: e.tensor_scalar(
                    lqq[:, :], lqt[:, :], -8.0, 7.0, ALU.max, ALU.min),
                  reads=[(kp, "t")], writes=[(kp, "q")], tag="f0:q")
                qv = lqq[:, :].rearrange("p (b q) -> p b q", q=32)
                dqv = lqdq[:, par, :].rearrange("p (b q) -> p b q", q=32)
                kdq = (kp, "dq", par)
                E("vector", lambda e, qv=qv, sbc=sbc, dqv=dqv: e.tensor_tensor(
                    dqv, qv, sbc, ALU.mult),
                  reads=[(kp, "q"), (kp, "s")], writes=[kdq], war=[kdq], tag="f0:dq")
                for c in range(nchunk):
                    bank = 6 + c % 2
                    E("tensor", lambda e, c=c, par=par, bank=bank: e.transpose(
                        pb[bank][:, 0:128], lqdq[:, par, c * 128:(c + 1) * 128],
                        ident[:, :]),
                      reads=[kdq, "ident"], writes=[("pb", bank)], tag="f0:tr")
                    E("scalar", lambda e, bank=bank: e.copy(
                        lqst[:, :], pb[bank][:, 0:128]),
                      reads=[("pb", bank)], writes=[(kp, "st")], tag="f0:cp")
                    E("vector", lambda e, c=c, rc=rc, bank=bank: e.tensor_copy(
                        whiT[:, c, rc * 128:(rc + 1) * 128], pb[bank][:, 0:128]),
                      reads=[("pb", bank)], writes=[(kp, "hi", c, rc)], tag="f0:hi")
                    E("vector", lambda e, c=c, rc=rc: e.tensor_tensor(
                        wloT[:, c, rc * 128:(rc + 1) * 128], lqst[:, :],
                        whiT[:, c, rc * 128:(rc + 1) * 128], ALU.subtract),
                      reads=[(kp, "st"), (kp, "hi", c, rc)],
                      writes=[(kp, "lo", c, rc)], tag="f0:lo")

    b.barrier()

    with nc.sbuf_tensor("f0xq", [128, 2, 8, 512], F16) as f0xq, \
         nc.sbuf_tensor("f0raw", [128, 2, 512], F32) as f0raw, \
         nc.sbuf_tensor("f0mc", [128, 1], F32) as f0mc, \
         nc.sbuf_tensor("f0mx", [128, 2], F32) as f0mx:
        E("vector", lambda e: e.memset(f0mx[:, :], 0.0), writes=[("f0mx", 0), ("f0mx", 1)],
          tag="f0:mx0")
        for wi, (wsrc0, kpw, sdst) in enumerate(((w0u, "wu", s_b),
                                                 (w0g, "wg", s_a))):
            wscope = contextlib.ExitStack()
            wh = wscope.enter_context(nc.sbuf_tensor("f0wh%d" % wi, [128, 8, DFF0], F16))
            wl = wscope.enter_context(nc.sbuf_tensor("f0wl%d" % wi, [128, 8, DFF0], F16))
            lpbq_prep(wsrc0, DFF0, wh, wl, kpw)
            b.barrier()
            if wi == 0 and "wusp" in dbg_out:
                E("sync", lambda e, wh=wh: e.dma_start(
                    out=dbg_out["wusp"][0:128, :],
                    in_=wh[:, :, :].rearrange("p c o -> p (c o)")),
                  writes=["dbgwu1"], dma=True, tag="dbg:wu1")
                E("sync", lambda e, wl=wl: e.dma_start(
                    out=dbg_out["wusp"][128:256, :],
                    in_=wl[:, :, :].rearrange("p c o -> p (c o)")),
                  writes=["dbgwu2"], dma=True, tag="dbg:wu2")
            for sup in range(8):
                for sl, srct in ((0, s_xqhi), (1, s_xqlo)):
                    E("sync", lambda e, sl=sl, srct=srct, sup=sup: e.dma_start(
                        out=f0xq[:, sl, :, :],
                        in_=srct[:, sup * 512:(sup + 1) * 512]
                        .rearrange("(c p) t -> p c t", p=128)),
                      writes=[("f0xq", sl)], war=[("f0xq", sl)], dma=True,
                      tag="f0:ldxq")
                for mm2 in range(24):
                    bank = mm2 % 2
                    rpar = mm2 % 2
                    for c in range(8):
                        for p3, (wa, xb) in enumerate(((0, 0), (1, 0), (0, 1))):
                            first = (c == 0 and p3 == 0)
                            lastm = (c == 7 and p3 == 2)
                            ww = wh if wa == 0 else wl
                            E("tensor", lambda e, ww=ww, c=c, mm2=mm2, xb=xb,
                              bank=bank, first=first, lastm=lastm:
                              e.matmul(pb[bank][:, 0:512],
                                       ww[:, c, mm2 * 128:(mm2 + 1) * 128],
                                       f0xq[:, xb, c, :], start=first, stop=lastm),
                              reads=[("f0xq", 0), ("f0xq", 1)],
                              writes=[("pb", bank)] if lastm else (),
                              war=[("pb", bank)] if first else (), tag="f0:mm")
                    E("vector", lambda e, bank=bank, rpar=rpar: e.tensor_copy(
                        f0raw[:, rpar, :], pb[bank][:, 0:512]),
                      reads=[("pb", bank)], writes=[("f0raw", rpar)], tag="f0:ev")
                    E("vector", lambda e, rpar=rpar: e.reduce_max(
                        f0mc[:, :], f0raw[:, rpar, :], axis=AX.X,
                        apply_absolute_value=True),
                      reads=[("f0raw", rpar)], writes=["f0mc"], tag="f0:mx1")
                    E("vector", lambda e, wi=wi: e.tensor_tensor(
                        f0mx[:, wi:wi + 1], f0mx[:, wi:wi + 1], f0mc[:, :], ALU.max),
                      reads=["f0mc", ("f0mx", wi)], writes=[("f0mx", wi)],
                      tag="f0:mx2")
                    E("gpsimd", lambda e, sdst=sdst, mm2=mm2, sup=sup, rpar=rpar:
                      e.dma_start(out=sdst[mm2 * 128:(mm2 + 1) * 128,
                                           sup * 512:(sup + 1) * 512],
                                  in_=f0raw[:, rpar, :]),
                      reads=[("f0raw", rpar)], writes=[("f0s", wi, mm2, sup)],
                      dma=True, tag="f0:st")
            b.barrier()
            wscope.close()
        if "upraw" in dbg_out:
            E("sync", lambda e: e.dma_start(out=dbg_out["upraw"][:, :], in_=s_b[:, :]),
              reads=[("f0s", 0, mm3, sup3) for mm3 in range(24) for sup3 in range(8)],
              writes=["dbgupraw"], dma=True, tag="dbg:upraw")
        if "gateraw" in dbg_out:
            E("sync", lambda e: e.dma_start(out=dbg_out["gateraw"][:, :], in_=s_a[:, :]),
              reads=[("f0s", 1, mm3, sup3) for mm3 in range(24) for sup3 in range(8)],
              writes=["dbggateraw"], dma=True, tag="dbg:gateraw")
        E("tensor", lambda e: e.transpose(pb[0][0:1, 0:128], f0mx[:, 0:1], ident[:, :]),
          reads=[("f0mx", 0), "ident"], writes=[("pb", 0)], tag="r2:tr")
        E("vector", lambda e: e.reduce_max(gmax[0:1, 4:5], pb[0][0:1, 0:128], axis=AX.X),
          reads=[("pb", 0)], writes=["r2u"], tag="r2:mx1")
        E("tensor", lambda e: e.transpose(pb[1][0:1, 0:128], f0mx[:, 1:2], ident[:, :]),
          reads=[("f0mx", 1), "ident"], writes=[("pb", 1)], tag="r2:tr2")
        E("vector", lambda e: e.reduce_max(gmax[0:1, 5:6], pb[1][0:1, 0:128], axis=AX.X),
          reads=[("pb", 1)], writes=["r2g"], tag="r2:mx2")
        E("gpsimd", lambda e: e.dma_start(out=cc_in[0:1, 0:2], in_=gmax[0:1, 4:6]),
          reads=["r2u", "r2g"], writes=["cc_in"], dma=True, tag="r2:out")
        E("gpsimd", lambda e: e.collective_compute(
            "AllReduce", ALU.max, replica_groups=[list(range(NCORE))],
            ins=[cc_in[0:1, 0:2].opt()], outs=[cc_out[0:1, 0:2].opt()]),
          reads=["cc_in"], writes=["cc_out"], war=["cc_out"], tag="r2:cc")
        E("gpsimd", lambda e: e.dma_start(out=gmax[0:1, 6:8], in_=cc_out[0:1, 0:2]),
          reads=["cc_out"], writes=["r2m"], dma=True, tag="r2:in")
        E("vector", lambda e: e.tensor_tensor(
            gmax[0:1, 8:10], gmax[0:1, 6:8],
            gmax[0:1, 2:3].broadcast_to([1, 2]), ALU.mult),
          reads=["r2m", "s1c"], writes=["r2s"], tag="r2:mul")
        E("vector", lambda e: e.tensor_scalar(
            gmax[0:1, 10:12], gmax[0:1, 8:10], 1.0 / 32767.0, 1e-12, ALU.mult,
            ALU.max),
          reads=["r2s"], writes=["s2c"], tag="r2:s2")
        E("vector", lambda e: e.reciprocal(gmax[0:1, 12:14], gmax[0:1, 10:12]),
          reads=["s2c"], writes=["s2r"], tag="r2:rec")
        E("vector", lambda e: e.tensor_tensor(
            gmax[0:1, 14:16], gmax[0:1, 12:14],
            gmax[0:1, 2:3].broadcast_to([1, 2]), ALU.mult),
          reads=["s2r", "s1c"], writes=["s2sr"], tag="r2:ru")
        bcast_cells(gmax[0:1, 14:16], ["s2sr"], cells[:, 4:6], "cells_ru_rg", 2)
        bcast_cells(gmax[0:1, 10:12], ["s2c"], cells[:, 6:8], "cells_s2", 2)
        b.barrier()

    with nc.sbuf_tensor("e1g", [128, 2, 512], F32) as e1g, \
         nc.sbuf_tensor("e1i", [128, 512], I32) as e1i, \
         nc.sbuf_tensor("e1gate", [128, 512], F32) as e1gate, \
         nc.sbuf_tensor("e1sig", [128, 512], F32) as e1sig, \
         nc.sbuf_tensor("e1u", [128, 512], I32) as e1u, \
         nc.sbuf_tensor("e1p", [128, 2, 512], F32) as e1p, \
         nc.sbuf_tensor("e1mc", [128, 1], F32) as e1mc, \
         nc.sbuf_tensor("e1mx", [128, 1], F32) as e1mx:
        E("vector", lambda e: e.memset(e1mx[:, :], 0.0), writes=["e1mx"], tag="e1:mx0")
        for ti in range(192):
            mm2, sup = ti // 8, ti % 8
            par = ti % 2
            kg = ("e1g", par)
            E("sync", lambda e, mm2=mm2, sup=sup, par=par: e.dma_start(
                out=e1g[:, par, :], in_=s_a[mm2 * 128:(mm2 + 1) * 128,
                                            sup * 512:(sup + 1) * 512]),
              reads=[("f0s", 1, mm2, sup)], writes=[kg], war=[kg], dma=True,
              tag="e1:ld")
            E("vector", lambda e, par=par: e.tensor_scalar(
                e1i[:, :], e1g[:, par, :], cells[:, 5:6], None, ALU.mult),
              reads=[kg, "cells_ru_rg"], writes=["e1i"], tag="e1:q")
            E("vector", lambda e: e.tensor_scalar(
                e1i[:, :], e1i[:, :], -32768.0, 32767.0, ALU.max, ALU.min),
              reads=["e1i"], writes=["e1i"], tag="e1:cl")
            E("vector", lambda e: e.tensor_scalar(
                e1gate[:, :], e1i[:, :], cells[:, 7:8], None, ALU.mult),
              reads=["e1i", "cells_s2"], writes=["e1gate"], tag="e1:gate")
            E("scalar", lambda e: e.activation(
                e1sig[:, :], e1i[:, :], AF.Sigmoid, scale=cells[:, 7:8]),
              reads=["e1i", "cells_s2"], writes=["e1sig"], tag="e1:sig")
            E("vector", lambda e: e.tensor_scalar(
                e1u[:, :], e1sig[:, :], 65536.0, 65535.0, ALU.mult, ALU.min),
              reads=["e1sig"], writes=["e1u"], tag="e1:u16")
            E("vector", lambda e, par=par: e.scalar_tensor_tensor(
                e1p[:, par, :], e1gate[:, :], 1.0 / 65536.0, e1u[:, :],
                ALU.mult, ALU.mult),
              reads=["e1gate", "e1u"], writes=[("e1p", par)], tag="e1:prod")
            E("vector", lambda e, par=par: e.reduce_max(
                e1mc[:, :], e1p[:, par, :], axis=AX.X, apply_absolute_value=True),
              reads=[("e1p", par)], writes=["e1mc"], tag="e1:mx1")
            E("vector", lambda e: e.tensor_tensor(
                e1mx[:, :], e1mx[:, :], e1mc[:, :], ALU.max),
              reads=["e1mc", "e1mx"], writes=["e1mx"], tag="e1:mx2")
            E("gpsimd", lambda e, mm2=mm2, sup=sup, par=par: e.dma_start(
                out=s_a[mm2 * 128:(mm2 + 1) * 128, sup * 512:(sup + 1) * 512],
                in_=e1p[:, par, :]),
              reads=[("e1p", par)], writes=[("f0s", 1, mm2, sup)], dma=True,
              tag="e1:st")
        E("tensor", lambda e: e.transpose(pb[0][0:1, 0:128], e1mx[:, :], ident[:, :]),
          reads=["e1mx", "ident"], writes=[("pb", 0)], tag="r3:tr")
        E("vector", lambda e: e.reduce_max(gmax[0:1, 16:17], pb[0][0:1, 0:128],
                                           axis=AX.X),
          reads=[("pb", 0)], writes=["r3l"], tag="r3:mx")
        E("gpsimd", lambda e: e.dma_start(out=cc_in[0:1, 0:1], in_=gmax[0:1, 16:17]),
          reads=["r3l"], writes=["cc_in"], war=["cc_in"], dma=True, tag="r3:out")
        E("gpsimd", lambda e: e.collective_compute(
            "AllReduce", ALU.max, replica_groups=[list(range(NCORE))],
            ins=[cc_in[0:1, 0:1].opt()], outs=[cc_out[0:1, 0:1].opt()]),
          reads=["cc_in"], writes=["cc_out"], war=["cc_out"], tag="r3:cc")
        E("gpsimd", lambda e: e.dma_start(out=gmax[0:1, 17:18], in_=cc_out[0:1, 0:1]),
          reads=["cc_out"], writes=["r3m"], dma=True, tag="r3:in")
        E("vector", lambda e: e.tensor_scalar(
            gmax[0:1, 18:19], gmax[0:1, 17:18], 1.0 / 32767.0, 1e-12, ALU.mult,
            ALU.max),
          reads=["r3m"], writes=["s3c"], tag="r3:s3")
        E("vector", lambda e: e.reciprocal(gmax[0:1, 19:20], gmax[0:1, 18:19]),
          reads=["s3c"], writes=["s3r"], tag="r3:rec")
        bcast_cells(gmax[0:1, 19:20], ["s3r"], cells[:, 8:9], "cells_r3", 1)
        bcast_cells(gmax[0:1, 18:19], ["s3c"], cells[:, 9:10], "cells_s3", 1)
        b.barrier()

    with nc.sbuf_tensor("e2p", [128, 2, 512], F32) as e2p, \
         nc.sbuf_tensor("e2u", [128, 2, 512], F32) as e2u, \
         nc.sbuf_tensor("e2i", [128, 512], I32) as e2i, \
         nc.sbuf_tensor("e2up", [128, 512], F32) as e2up, \
         nc.sbuf_tensor("e2o", [128, 2, 512], F32) as e2o, \
         nc.sbuf_tensor("e2mc", [128, 1], F32) as e2mc, \
         nc.sbuf_tensor("e2mx", [128, 1], F32) as e2mx:
        E("vector", lambda e: e.memset(e2mx[:, :], 0.0), writes=["e2mx"], tag="e2:mx0")
        for ti in range(192):
            mm2, sup = ti // 8, ti % 8
            par = ti % 2
            kp2 = ("e2p", par)
            kup = ("e2ul", par)
            E("sync", lambda e, mm2=mm2, sup=sup, par=par: e.dma_start(
                out=e2p[:, par, :], in_=s_a[mm2 * 128:(mm2 + 1) * 128,
                                            sup * 512:(sup + 1) * 512]),
              reads=[("f0s", 1, mm2, sup)], writes=[kp2], war=[kp2], dma=True,
              tag="e2:ldp")
            E("sync", lambda e, mm2=mm2, sup=sup, par=par: e.dma_start(
                out=e2u[:, par, :], in_=s_b[mm2 * 128:(mm2 + 1) * 128,
                                            sup * 512:(sup + 1) * 512]),
              reads=[("f0s", 0, mm2, sup)], writes=[kup], war=[kup], dma=True,
              tag="e2:ldu")
            E("vector", lambda e, par=par: e.tensor_scalar(
                e2i[:, :], e2u[:, par, :], cells[:, 4:5], None, ALU.mult),
              reads=[kup, "cells_ru_rg"], writes=["e2i"], tag="e2:uq")
            E("vector", lambda e: e.tensor_scalar(
                e2i[:, :], e2i[:, :], -32768.0, 32767.0, ALU.max, ALU.min),
              reads=["e2i"], writes=["e2i"], tag="e2:ucl")
            E("vector", lambda e: e.tensor_scalar(
                e2up[:, :], e2i[:, :], cells[:, 6:7], None, ALU.mult),
              reads=["e2i", "cells_s2"], writes=["e2up"], tag="e2:up")
            E("vector", lambda e, par=par: e.tensor_scalar(
                e2i[:, :], e2p[:, par, :], cells[:, 8:9], None, ALU.mult),
              reads=[kp2, "cells_r3"], writes=["e2i"], tag="e2:q3")
            E("vector", lambda e: e.tensor_scalar(
                e2i[:, :], e2i[:, :], -32768.0, 32767.0, ALU.max, ALU.min),
              reads=["e2i"], writes=["e2i"], tag="e2:q3cl")
            E("vector", lambda e, par=par: e.scalar_tensor_tensor(
                e2o[:, par, :], e2i[:, :], cells[:, 9:10], e2up[:, :],
                ALU.mult, ALU.mult),
              reads=["e2i", "cells_s3", "e2up"], writes=[("e2o", par)], tag="e2:p2")
            E("vector", lambda e, par=par: e.reduce_max(
                e2mc[:, :], e2o[:, par, :], axis=AX.X, apply_absolute_value=True),
              reads=[("e2o", par)], writes=["e2mc"], tag="e2:mx1")
            E("vector", lambda e: e.tensor_tensor(
                e2mx[:, :], e2mx[:, :], e2mc[:, :], ALU.max),
              reads=["e2mc", "e2mx"], writes=["e2mx"], tag="e2:mx2")
            E("gpsimd", lambda e, mm2=mm2, sup=sup, par=par: e.dma_start(
                out=s_b[mm2 * 128:(mm2 + 1) * 128, sup * 512:(sup + 1) * 512],
                in_=e2o[:, par, :]),
              reads=[("e2o", par)], writes=[("f0s", 0, mm2, sup)], dma=True,
              tag="e2:st")
        E("tensor", lambda e: e.transpose(pb[0][0:1, 0:128], e2mx[:, :], ident[:, :]),
          reads=["e2mx", "ident"], writes=[("pb", 0)], tag="r4:tr")
        E("vector", lambda e: e.reduce_max(gmax[0:1, 20:21], pb[0][0:1, 0:128],
                                           axis=AX.X),
          reads=[("pb", 0)], writes=["r4l"], tag="r4:mx")
        E("gpsimd", lambda e: e.dma_start(out=cc_in[0:1, 0:1], in_=gmax[0:1, 20:21]),
          reads=["r4l"], writes=["cc_in"], war=["cc_in"], dma=True, tag="r4:out")
        E("gpsimd", lambda e: e.collective_compute(
            "AllReduce", ALU.max, replica_groups=[list(range(NCORE))],
            ins=[cc_in[0:1, 0:1].opt()], outs=[cc_out[0:1, 0:1].opt()]),
          reads=["cc_in"], writes=["cc_out"], war=["cc_out"], tag="r4:cc")
        E("gpsimd", lambda e: e.dma_start(out=gmax[0:1, 21:22], in_=cc_out[0:1, 0:1]),
          reads=["cc_out"], writes=["r4m"], dma=True, tag="r4:in")
        E("vector", lambda e: e.tensor_scalar(
            gmax[0:1, 22:23], gmax[0:1, 21:22], 1.0 / 32767.0, 1e-12, ALU.mult,
            ALU.max),
          reads=["r4m"], writes=["s4c"], tag="r4:s4")
        E("vector", lambda e: e.reciprocal(gmax[0:1, 23:24], gmax[0:1, 22:23]),
          reads=["s4c"], writes=["s4r"], tag="r4:rec")
        bcast_cells(gmax[0:1, 23:24], ["s4r"], cells[:, 10:11], "cells_r4", 1)
        bcast_cells(gmax[0:1, 22:23], ["s4c"], cells[:, 11:12], "cells_s4", 1)
        b.barrier()

    wdscope = contextlib.ExitStack()
    w0dsp_h = wdscope.enter_context(nc.sbuf_tensor("w0dsp_h", [128, 24, D], F16))
    w0dsp_l = wdscope.enter_context(nc.sbuf_tensor("w0dsp_l", [128, 24, D], F16))
    lpbq_prep(w0d, D, w0dsp_h, w0dsp_l, "wd")
    b.barrier()
    with nc.sbuf_tensor("e3p", [128, 2, 512], F32) as e3p, \
         nc.sbuf_tensor("e3i", [128, 512], I32) as e3i, \
         nc.sbuf_tensor("e3sh", [128, 512], I32) as e3sh, \
         nc.sbuf_tensor("e3hi", [128, 2, 512], F16) as e3hi, \
         nc.sbuf_tensor("e3lo", [128, 2, 512], F16) as e3lo, \
         nc.sbuf_tensor("e3o", [128, 512], F32) as e3o:
        for sup in range(8):
            for mm2 in range(24):
                par = mm2 % 2
                kp3 = ("e3p", par)
                E("sync", lambda e, mm2=mm2, sup=sup, par=par: e.dma_start(
                    out=e3p[:, par, :], in_=s_b[mm2 * 128:(mm2 + 1) * 128,
                                                sup * 512:(sup + 1) * 512]),
                  reads=[("f0s", 0, mm2, sup)], writes=[kp3], war=[kp3], dma=True,
                  tag="e3:ld")
                E("vector", lambda e, par=par: e.tensor_scalar(
                    e3i[:, :], e3p[:, par, :], cells[:, 10:11], None, ALU.mult),
                  reads=[kp3, "cells_r4"], writes=["e3i"], tag="e3:q")
                E("vector", lambda e: e.tensor_scalar(
                    e3i[:, :], e3i[:, :], -32768.0, 32767.0, ALU.max, ALU.min),
                  reads=["e3i"], writes=["e3i"], tag="e3:cl")
                E("vector", lambda e: e.tensor_scalar(
                    e3sh[:, :], e3i[:, :], 1.0 / 256.0, None, ALU.mult),
                  reads=["e3i"], writes=["e3sh"], tag="e3:sh")
                E("vector", lambda e, par=par: e.tensor_scalar(
                    e3hi[:, par, :], e3sh[:, :], 256.0, None, ALU.mult),
                  reads=["e3sh"], writes=[("e3hi", par)], tag="e3:hi")
                E("vector", lambda e, par=par: e.scalar_tensor_tensor(
                    e3lo[:, par, :], e3sh[:, :], -256.0, e3i[:, :], ALU.mult,
                    ALU.add),
                  reads=["e3sh", "e3i"], writes=[("e3lo", par)], tag="e3:lo")
                for hc in range(8):
                    for p3, (wa, ob) in enumerate(((0, 0), (1, 0), (0, 1))):
                        ww = w0dsp_h if wa == 0 else w0dsp_l
                        oo = e3hi if ob == 0 else e3lo
                        first = (mm2 == 0 and p3 == 0)
                        lastm = (mm2 == 23 and p3 == 2)
                        E("tensor", lambda e, ww=ww, oo=oo, hc=hc, mm2=mm2, par=par,
                          first=first, lastm=lastm:
                          e.matmul(pb[hc][:, 0:512],
                                   ww[:, mm2, hc * 128:(hc + 1) * 128],
                                   oo[:, par, :], start=first, stop=lastm),
                          reads=[("e3hi", par), ("e3lo", par)],
                          writes=[("pb", hc)] if lastm else (),
                          war=[("pb", hc)] if first else (), tag="e3:mm")
            for hc in range(8):
                E("vector", lambda e, hc=hc: e.tensor_scalar(
                    e3o[:, :], pb[hc][:, 0:512], cells[:, 11:12], None, ALU.mult),
                  reads=[("pb", hc), "cells_s4"], writes=["e3o"], tag="e3:ev")
                E("gpsimd", lambda e, hc=hc, sup=sup: e.dma_start(
                    out=s_ffn0[hc * 128:(hc + 1) * 128, sup * 512:(sup + 1) * 512],
                    in_=e3o[:, :]),
                  reads=["e3o"], writes=[("sff", hc, sup)], dma=True, tag="e3:st")
        b.barrier()
    wdscope.close()

    with nc.sbuf_tensor("p5x", [128, 2, D], F32) as p5x, \
         nc.sbuf_tensor("p5a", [128, 2, 8, 128], F32) as p5a, \
         nc.sbuf_tensor("p5f", [128, 2, 8, 128], F32) as p5f, \
         nc.sbuf_tensor("p5t", [128, 128], F32) as p5t, \
         nc.sbuf_tensor("p5x1", [128, 2, D], F32) as p5x1, \
         nc.sbuf_tensor("p5cols", [128, 3 * NT], F32) as p5cols, \
         nc.sbuf_tensor("p5dump", [128, D], F32) as p5dump:
        for t in range(NT):
            par = t % 2
            kx = ("p5x", par)
            ka = ("p5a", par)
            kf = ("p5f", par)
            kx1 = ("p5x1", par)
            E("sync", lambda e, t=t, par=par: e.dma_start(
                out=p5x[:, par, :], in_=x_in[t * 128:(t + 1) * 128, :]),
              writes=[kx], war=[kx], dma=True, tag="p5:ldx")
            E("sync", lambda e, t=t, par=par: e.dma_start(
                out=p5a[:, par, :, :],
                in_=s_attn0[:, t * 128:(t + 1) * 128]
                .rearrange("(c p) t2 -> p c t2", p=128)),
              writes=[ka], war=[ka], dma=True, tag="p5:lda")
            E("sync", lambda e, t=t, par=par: e.dma_start(
                out=p5f[:, par, :, :],
                in_=s_ffn0[:, t * 128:(t + 1) * 128]
                .rearrange("(c p) t2 -> p c t2", p=128)),
              writes=[kf], war=[kf], dma=True, tag="p5:ldf")
            for c in range(8):
                bank = c % 2
                bank2 = 2 + c % 2
                E("tensor", lambda e, c=c, par=par, bank=bank: e.transpose(
                    pb[bank][:, 0:128], p5a[:, par, c, :], ident[:, :]),
                  reads=[ka, "ident"], writes=[("pb", bank)], tag="p5:tra")
                E("tensor", lambda e, c=c, par=par, bank2=bank2: e.transpose(
                    pb[bank2][:, 0:128], p5f[:, par, c, :], ident[:, :]),
                  reads=[kf, "ident"], writes=[("pb", bank2)], tag="p5:trf")
                E("vector", lambda e, c=c, par=par, bank=bank: e.tensor_tensor(
                    p5t[:, :], pb[bank][:, 0:128],
                    p5x[:, par, c * 128:(c + 1) * 128], ALU.add),
                  reads=[("pb", bank), kx], writes=["p5t"], tag="p5:add1")
                E("vector", lambda e, c=c, par=par, bank2=bank2: e.tensor_tensor(
                    p5x1[:, par, c * 128:(c + 1) * 128], pb[bank2][:, 0:128],
                    p5t[:, :], ALU.add),
                  reads=[("pb", bank2), "p5t"], writes=[kx1], tag="p5:add2")
            E("gpsimd", lambda e, t=t, par=par: e.dma_start(
                out=s_x1[t * 128:(t + 1) * 128, :], in_=p5x1[:, par, :]),
              reads=[kx1], writes=[("sx1", t)], dma=True, tag="p5:st")
            sm = p5cols[:, t:t + 1]
            sq = p5cols[:, NT + t:NT + t + 1]
            E("vector", lambda e, sm=sm, par=par: e.reduce_sum(
                sm, p5x1[:, par, :], axis=AX.X),
              reads=[kx1], writes=[("sm1", t)], tag="p5:sum")
            E("scalar", lambda e, sq=sq, par=par: e.activation(
                p5dump[:, :], p5x1[:, par, :], AF.Square, accum_out=sq),
              reads=[kx1], writes=[("sq1", t), "p5dump"], tag="p5:sq")
            E("vector", lambda e, t=t, sm=sm: e.tensor_scalar_mul(
                stats1_m[:, t:t + 1], sm, 1.0 / D),
              reads=[("sm1", t)], writes=[("m1", t)], tag="p5:m")
            vcol = p5cols[:, 2 * NT + t:2 * NT + t + 1]
            E("vector", lambda e, t=t, vcol=vcol: e.tensor_tensor(
                vcol, stats1_m[:, t:t + 1], stats1_m[:, t:t + 1], ALU.mult),
              reads=[("m1", t)], writes=[("v1", t)], tag="p5:msq")
            E("vector", lambda e, t=t, vcol=vcol, sq=sq: e.scalar_tensor_tensor(
                vcol, sq, 1.0 / D, vcol, ALU.mult, ALU.subtract),
              reads=[("sq1", t), ("v1", t)], writes=[("v1", t)], tag="p5:var")
            E("scalar", lambda e, vcol=vcol: e.activation(
                vcol, vcol, AF.Sqrt, bias=misc[:, 0:1], scale=1.0),
              reads=[("v1", t), "misc"], writes=[("v1b", t)], tag="p5:sqrt")
            E("vector", lambda e, t=t, vcol=vcol: e.reciprocal(
                stats1_r[:, t:t + 1], vcol),
              reads=[("v1b", t)], writes=[("r1", t)], tag="p5:recip")
        b.barrier()


    b.barrier()
    p6scope = contextlib.ExitStack()
    w1sp = {}
    for nm, wsrc1 in (("q", wq1T), ("k", wk1T), ("v", wv1T)):
        w1sp[nm] = (p6scope.enter_context(nc.sbuf_tensor("w1%s_h" % nm, [128, 8, D], F16)),
                    p6scope.enter_context(nc.sbuf_tensor("w1%s_l" % nm, [128, 8, D], F16)))
        hi1, lo1 = w1sp[nm]
        with nc.sbuf_tensor("w1t" + nm, [128, 2, D], F32) as w1t:
            for c in range(8):
                kw = ("w1t" + nm, c % 2)
                E("sync", lambda e, c=c, wsrc1=wsrc1, w1t=w1t: e.dma_start(
                    out=w1t[:, c % 2, :], in_=wsrc1[c * 128:(c + 1) * 128, :]),
                  writes=[kw], war=[kw], dma=True, tag="w1:dma")
                E("vector", lambda e, c=c, hi1=hi1, w1t=w1t: e.tensor_copy(
                    hi1[:, c, :], w1t[:, c % 2, :]),
                  reads=[kw], writes=[("w1h" + nm, c)], tag="w1:hi")
                E("vector", lambda e, c=c, hi1=hi1, lo1=lo1, w1t=w1t: e.tensor_tensor(
                    lo1[:, c, :], w1t[:, c % 2, :], hi1[:, c, :], ALU.subtract),
                  reads=[kw, ("w1h" + nm, c)], writes=[("w1l" + nm, c)], tag="w1:lo")
    b.barrier()

    with nc.sbuf_tensor("p6x", [128, 2, D], F32) as p6x, \
         nc.sbuf_tensor("p6xn", [128, 4, D], F32) as p6xn, \
         nc.sbuf_tensor("p6mx", [128, 4], F32) as p6mx, \
         nc.sbuf_tensor("p6sc", [128, 4], F32) as p6sc, \
         nc.sbuf_tensor("p6xq", [128, 4, D], F16) as p6xq, \
         nc.sbuf_tensor("p6qi", [128, D], I32) as p6qi, \
         nc.sbuf_tensor("xnfm1", [128, 8, 512], F32) as xnfm1, \
         nc.sbuf_tensor("xnfm1_hi", [128, 8, 512], F16) as xnfm1_hi, \
         nc.sbuf_tensor("xnfm1_lo", [128, 8, 512], F16) as xnfm1_lo, \
         nc.sbuf_tensor("xqfm1", [128, 512], F16) as xqfm1, \
         nc.sbuf_tensor("qkst1", [128, 512], F32) as qkst1, \
         nc.sbuf_tensor("sthi1", [128, 2, 512], F16) as sthi1, \
         nc.sbuf_tensor("stlo1", [128, 2, 512], F16) as stlo1:
        for sup in range(8):
            for tt in range(4):
                t = sup * 4 + tt
                kx = ("p6x", t % 2)
                E("sync", lambda e, t=t: e.dma_start(
                    out=p6x[:, t % 2, :], in_=s_x1[t * 128:(t + 1) * 128, :]),
                  reads=[("sx1", t)], writes=[kx], war=[kx], dma=True, tag="p6:dmax")
                kxn = ("p6xn", tt)
                E("vector", lambda e, t=t, tt=tt: e.tensor_scalar(
                    p6xn[:, tt, :], p6x[:, t % 2, :], stats1_m[:, t:t + 1],
                    stats1_r[:, t:t + 1], ALU.subtract, ALU.mult),
                  reads=[kx, ("m1", t), ("r1", t)], writes=[kxn], tag="p6:xn")
                E("vector", lambda e, tt=tt: e.reduce_max(
                    p6mx[:, tt:tt + 1], p6xn[:, tt, :], axis=AX.X,
                    apply_absolute_value=True),
                  reads=[kxn], writes=[("p6m", tt)], tag="p6:mx")
                E("vector", lambda e, tt=tt: e.tensor_scalar(
                    p6mx[:, tt:tt + 1], p6mx[:, tt:tt + 1], 1e-5, None, ALU.max),
                  reads=[("p6m", tt)], writes=[("p6m", tt)], tag="p6:cl")
                E("vector", lambda e, t=t, tt=tt: e.tensor_scalar(
                    rsff[:, t:t + 1], p6mx[:, tt:tt + 1], 1.0 / 127.0, None, ALU.mult),
                  reads=[("p6m", tt)], writes=[("rs", t)], tag="p6:rs")
                E("vector", lambda e, tt=tt: e.reciprocal(
                    p6sc[:, tt:tt + 1], p6mx[:, tt:tt + 1]),
                  reads=[("p6m", tt)], writes=[("p6s", tt)], tag="p6:rec")
                E("vector", lambda e, tt=tt: e.tensor_scalar(
                    p6sc[:, tt:tt + 1], p6sc[:, tt:tt + 1], 127.0, None, ALU.mult),
                  reads=[("p6s", tt)], writes=[("p6s", tt)], tag="p6:sc")
                E("vector", lambda e, tt=tt: e.tensor_scalar(
                    p6qi[:, :], p6xn[:, tt, :], p6sc[:, tt:tt + 1], None, ALU.mult),
                  reads=[kxn, ("p6s", tt)], writes=["p6qi"], tag="p6:qi")
                E("vector", lambda e: e.tensor_scalar(
                    p6qi[:, :], p6qi[:, :], -128.0, 127.0, ALU.max, ALU.min),
                  reads=["p6qi"], writes=["p6qi"], tag="p6:qcl")
                E("vector", lambda e, tt=tt: e.tensor_copy(p6xq[:, tt, :], p6qi[:, :]),
                  reads=["p6qi"], writes=[("p6xq", tt)], tag="p6:qf")
                for c in range(8):
                    bank = c % 2 + 2
                    E("tensor", lambda e, tt=tt, c=c, bank=bank: e.transpose(
                        pb[bank][:, 0:128], p6xn[:, tt, c * 128:(c + 1) * 128],
                        ident[:, :]),
                      reads=[kxn, "ident"], writes=[("pb", bank)], tag="p6:tr")
                    E("vector", lambda e, tt=tt, c=c, bank=bank: e.tensor_copy(
                        xnfm1[:, c, tt * 128:(tt + 1) * 128], pb[bank][:, 0:128]),
                      reads=[("pb", bank)], writes=[("xnfm1", c, tt)], tag="p6:ev")
                    bank2 = c % 2 + 4
                    E("tensor", lambda e, tt=tt, c=c, bank2=bank2: e.transpose(
                        pb[bank2][:, :].bitcast(F16)[:, 0:128],
                        p6xq[:, tt, c * 128:(c + 1) * 128],
                        ident16[:, :]),
                      reads=[("p6xq", tt), "ident16"], writes=[("pb", bank2)],
                      tag="p6:trq")
                    E("vector", lambda e, tt=tt, c=c, bank2=bank2: e.tensor_copy(
                        xqfm1[:, 0:128], pb[bank2][:, :].bitcast(F16)[:, 0:128]),
                      reads=[("pb", bank2)], writes=["xqfm1"], tag="p6:evq")
                    E("gpsimd", lambda e, c=c, sup=sup, tt=tt: e.dma_start(
                        out=s_xqhi[c * 128:(c + 1) * 128,
                                   sup * 512 + tt * 128:sup * 512 + (tt + 1) * 128],
                        in_=xqfm1[:, 0:128]),
                      reads=["xqfm1"], writes=[("sxq1", c, sup, tt)], dma=True,
                      tag="p6:stq")
            for c in range(8):
                rk = [("xnfm1", c, tt) for tt in range(4)]
                E("vector", lambda e, c=c: e.tensor_copy(xnfm1_hi[:, c, :], xnfm1[:, c, :]),
                  reads=rk, writes=[("xn1hi", c)], tag="p6:hi")
                E("vector", lambda e, c=c: e.tensor_tensor(
                    xnfm1_lo[:, c, :], xnfm1[:, c, :], xnfm1_hi[:, c, :], ALU.subtract),
                  reads=rk + [("xn1hi", c)], writes=[("xn1lo", c)], tag="p6:lo")
            xn_r1 = [("xn1hi", c) for c in range(8)] + [("xn1lo", c) for c in range(8)]
            for nm, (dhi, dlo) in (("q", (s_qhi, s_qlo)), ("k", (s_khi, s_klo))):
                hi1, lo1 = w1sp[nm]
                for oc in range(8):
                    bank = oc % 2
                    for c in range(8):
                        for p3, (wa, xb) in enumerate(((0, 0), (0, 1), (1, 0))):
                            ww = hi1 if wa == 0 else lo1
                            act = xnfm1_hi if xb == 0 else xnfm1_lo
                            first = (c == 0 and p3 == 0)
                            lastm = (c == 7 and p3 == 2)
                            E("tensor", lambda e, ww=ww, act=act, c=c, oc=oc, bank=bank,
                              first=first, lastm=lastm:
                              e.matmul(pb[bank][:, 0:512],
                                       ww[:, c, oc * 128:(oc + 1) * 128],
                                       act[:, c, :], start=first, stop=lastm),
                              reads=xn_r1, writes=[("pb", bank)] if lastm else (),
                              war=[("pb", bank)] if first else (), tag="p6:qkmm")
                    E("vector", lambda e, bank=bank: e.tensor_copy(
                        sthi1[:, bank, :], pb[bank][:, 0:512]),
                      reads=[("pb", bank)], writes=[("st1h", bank)], tag="p6:sthi")
                    E("vector", lambda e, bank=bank: e.tensor_tensor(
                        stlo1[:, bank, :], pb[bank][:, 0:512], sthi1[:, bank, :],
                        ALU.subtract),
                      reads=[("pb", bank), ("st1h", bank)], writes=[("st1l", bank)],
                      tag="p6:stlo")
                    E("gpsimd", lambda e, d=dhi, oc=oc, sup=sup, bank=bank: e.dma_start(
                        out=d[oc * 128:(oc + 1) * 128, sup * 512:(sup + 1) * 512],
                        in_=sthi1[:, bank, :]),
                      reads=[("st1h", bank)], writes=[("sqk1", nm, oc, sup, "h")],
                      dma=True, tag="p6:dmah")
                    E("gpsimd", lambda e, d=dlo, oc=oc, sup=sup, bank=bank: e.dma_start(
                        out=d[oc * 128:(oc + 1) * 128, sup * 512:(sup + 1) * 512],
                        in_=stlo1[:, bank, :]),
                      reads=[("st1l", bank)], writes=[("sqk1", nm, oc, sup, "l")],
                      dma=True, tag="p6:dmal")
            hi1, lo1 = w1sp["v"]
            for tt in range(4):
                t = sup * 4 + tt
                for ot in range(2):
                    bank = 6 + ot % 2
                    for c in range(8):
                        for p3, (xb, wa) in enumerate(((0, 0), (0, 1), (1, 0))):
                            act = xnfm1_hi if xb == 0 else xnfm1_lo
                            ww = hi1 if wa == 0 else lo1
                            first = (c == 0 and p3 == 0)
                            lastm = (c == 7 and p3 == 2)
                            E("tensor", lambda e, act=act, ww=ww, c=c, tt=tt, ot=ot,
                              bank=bank, first=first, lastm=lastm:
                              e.matmul(pb[bank][:, 0:512],
                                       act[:, c, tt * 128:(tt + 1) * 128],
                                       ww[:, c, ot * 512:(ot + 1) * 512],
                                       start=first, stop=lastm),
                              reads=xn_r1, writes=[("pb", bank)] if lastm else (),
                              war=[("pb", bank)] if first else (), tag="p6:vmm")
                    E("vector", lambda e, bank=bank: e.tensor_copy(
                        qkst1[:, :], pb[bank][:, 0:512]),
                      reads=[("pb", bank)], writes=["qkst1"], tag="p6:vev")
                    E("gpsimd", lambda e, t=t, ot=ot: e.dma_start(
                        out=s_v[t * 128:(t + 1) * 128, ot * 512:(ot + 1) * 512],
                        in_=qkst1[:, :]),
                      reads=["qkst1"], writes=[("sv1", t, ot)], dma=True,
                      tag="p6:dmav")
        b.barrier()
    p6scope.close()

    with nc.sbuf_tensor("qk41", [65, 4, SEQ], F16) as qk41, \
         nc.sbuf_tensor("vbuf1", [128, 16, HD + 1], F32) as vbuf1, \
         nc.sbuf_tensor("estrip1", [128, 2, SEQ], F32) as estrip1, \
         nc.sbuf_tensor("acol1", [128, 2], F32) as acol1, \
         nc.sbuf_tensor("recrow", [128, 128], F32) as recrow, \
         nc.sbuf_tensor("ofm", [64, SEQ], F32) as ofm, \
         nc.sbuf_tensor("ofm_hi", [64, 2, SEQ], F16) as ofm_hi:
        E("vector", lambda e: e.memset(vbuf1[:, :, 64:65], 1.0),
          writes=["v1one"], tag="a1:vone")
        for sh in range(32):
            s, h = sh // 16, sh % 16
            for slot, src_t in ((0, s_qhi), (1, s_qlo), (2, s_khi), (3, s_klo)):
                E("sync", lambda e, slot=slot, src_t=src_t, s=s, h=h: e.dma_start(
                    out=qk41[0:64, slot, :],
                    in_=src_t[h * 64:(h + 1) * 64, s * SEQ:(s + 1) * SEQ]),
                  writes=[("qk1", slot)], dma=True, tag="a1:ldqk")
            E("vector", lambda e: e.memset(qk41[64:65, 2, :], 1.0),
              writes=[("qk1a", 2)], war=[("qk1a", 2)], tag="a1:aug1")
            E("vector", lambda e: e.memset(qk41[64:65, 3, :], 0.0),
              writes=[("qk1a", 3)], war=[("qk1a", 3)], tag="a1:aug0")
            E("sync", lambda e, s=s, h=h: e.dma_start(
                out=vbuf1[:, :, 0:64],
                in_=s_v[s * SEQ:(s + 1) * SEQ, h * 64:(h + 1) * 64]
                    .rearrange("(j p) c -> p j c", p=128)),
              writes=["vbuf1"], dma=True, tag="a1:ldv")
            for i in range(16):
                par = i % 2
                j0 = (i - 1) * 128 if i > 0 else 0
                w = 256 if i > 0 else 128
                E("tensor", lambda e, i=i, j0=j0, w=w: e.matmul(
                    pb[6][:, 0:w], qk41[0:64, 0, i * 128:(i + 1) * 128],
                    qk41[0:64, 2, j0:j0 + w], start=True, stop=True),
                  reads=[("qk1", 0), ("qk1", 2)], writes=[("pb", 6)], tag="a1:sbm")
                E("vector", lambda e, w=w: e.reduce_max(
                    acol1[:, 0:1], pb[6][:, 0:w], axis=AX.X),
                  reads=[("pb", 6)], writes=["a1c0"], tag="a1:sbr")
                E("vector", lambda e: e.tensor_scalar(
                    acol1[:, 1:2], acol1[:, 0:1], 320.0, -1.0, ALU.add, ALU.mult),
                  reads=["a1c0"], writes=["a1c1"], tag="a1:sba")
                E("tensor", lambda e: e.matmul(
                    pb[6][64:65, 0:128], acol1[:, 1:2], ident[:, :],
                    start=True, stop=True),
                  reads=["a1c1", "ident"], writes=[("pb", 6)], tag="a1:sbt")
                E("vector", lambda e, i=i: e.tensor_copy(
                    qk41[64:65, 0, i * 128:(i + 1) * 128], pb[6][64:65, 0:128]),
                  reads=[("pb", 6)], writes=[("qk1a", 0)], war=[("qk1a", 0)],
                  tag="a1:sbh")
                E("vector", lambda e, i=i: e.tensor_tensor(
                    qk41[64:65, 1, i * 128:(i + 1) * 128], pb[6][64:65, 0:128],
                    qk41[64:65, 0, i * 128:(i + 1) * 128], ALU.subtract),
                  reads=[("pb", 6), ("qk1a", 0)], writes=[("qk1a", 1)],
                  war=[("qk1a", 1)], tag="a1:sbl")
                for j in range(i + 1):
                    bank = j % 2
                    for p3, (aa, bb2) in enumerate(((0, 2), (1, 2), (0, 3))):
                        E("tensor", lambda e, j=j, i=i, bank=bank, p3=p3, aa=aa, bb2=bb2:
                          e.matmul(pb[bank][:, 0:128],
                                   qk41[:, bb2, j * 128:(j + 1) * 128],
                                   qk41[:, aa, i * 128:(i + 1) * 128],
                                   start=(p3 == 0), stop=(p3 == 2)),
                          reads=[("qk1", aa), ("qk1", bb2), ("qk1a", aa), ("qk1a", bb2)],
                          writes=[("pb", bank)] if p3 == 2 else (),
                          war=[("pb", bank)] if p3 == 0 else (), tag="a1:qk")
                    if j == i:
                        E("vector", lambda e, j=j, par=par, bank=bank: e.tensor_tensor(
                            estrip1[:, par, j * 128:(j + 1) * 128], pb[bank][:, 0:128],
                            maskbias[:, :], ALU.add),
                          reads=[("pb", bank), "maskbias"], writes=[("em1", par)],
                          tag="a1:mask")
                        E("scalar", lambda e, j=j, par=par: e.activation(
                            estrip1[:, par, j * 128:(j + 1) * 128],
                            estrip1[:, par, j * 128:(j + 1) * 128],
                            AF.Exp, bias=misc[:, 2:3], scale=0.125),
                          reads=[("em1", par), "misc"], writes=[("es1", par, j)],
                          tag="a1:expd")
                    else:
                        E("scalar", lambda e, j=j, par=par, bank=bank: e.activation(
                            estrip1[:, par, j * 128:(j + 1) * 128], pb[bank][:, 0:128],
                            AF.Exp, bias=misc[:, 2:3], scale=0.125),
                          reads=[("pb", bank), "misc"], writes=[("es1", par, j)],
                          tag="a1:exp")
                for j in range(i + 1):
                    E("tensor", lambda e, j=j, par=par, i=i: e.matmul(
                        pb[2][0:65, 0:128], vbuf1[:, j, :],
                        estrip1[:, par, j * 128:(j + 1) * 128],
                        start=(j == 0), stop=(j == i)),
                      reads=[("es1", par, j), "vbuf1", "v1one"],
                      writes=[("pb", 2)] if j == i else (),
                      war=[("pb", 2)] if j == 0 else (), tag="a1:pv")
                E("vector", lambda e: e.reciprocal(recrow[64:65, 0:128],
                                                   pb[2][64:65, 0:128]),
                  reads=[("pb", 2)], writes=["recr"], tag="a1:rec")
                E("tensor", lambda e: e.matmul(
                    pb[3][0:64, 0:128], ones_all[64:65, 0:64], recrow[64:65, 0:128],
                    start=True, stop=True),
                  reads=["recr", "ones_all"], writes=[("pb", 3)], tag="a1:rbc")
                E("scalar", lambda e: e.copy(recrow[0:64, 0:128], pb[3][0:64, 0:128]),
                  reads=[("pb", 3)], writes=["recb"], tag="a1:rbe")
                E("vector", lambda e, i=i: e.tensor_tensor(
                    ofm[:, i * 128:(i + 1) * 128], pb[2][0:64, 0:128],
                    recrow[0:64, 0:128], ALU.mult),
                  reads=[("pb", 2), "recb"], writes=[("ofm", i)], tag="a1:oev")
                E("vector", lambda e, i=i: e.tensor_copy(
                    ofm_hi[:, 0, i * 128:(i + 1) * 128],
                    ofm[:, i * 128:(i + 1) * 128]),
                  reads=[("ofm", i)], writes=[("ofh", i)], tag="a1:ohl")
                E("vector", lambda e, i=i: e.tensor_tensor(
                    ofm_hi[:, 1, i * 128:(i + 1) * 128],
                    ofm[:, i * 128:(i + 1) * 128],
                    ofm_hi[:, 0, i * 128:(i + 1) * 128], ALU.subtract),
                  reads=[("ofm", i), ("ofh", i)], writes=[("ofl", i)], tag="a1:oll")
            E("gpsimd", lambda e, s=s, h=h: e.dma_start(
                out=s_afmh[h * 64:(h + 1) * 64, s * SEQ:(s + 1) * SEQ],
                in_=ofm_hi[:, 0, :]),
              reads=[("ofh", i) for i in range(16)], writes=[("safm1", sh, 0)],
              dma=True, tag="a1:sth")
            E("gpsimd", lambda e, s=s, h=h: e.dma_start(
                out=s_afml[h * 64:(h + 1) * 64, s * SEQ:(s + 1) * SEQ],
                in_=ofm_hi[:, 1, :]),
              reads=[("ofl", i) for i in range(16)], writes=[("safm1", sh, 1)],
              dma=True, tag="a1:stl")
        b.barrier()

    with nc.sbuf_tensor("wosp", [128, 2, 8, D], F16) as wosp, \
         nc.sbuf_tensor("wot0", [128, 2, D], F32) as wot0, \
         nc.sbuf_tensor("aft1", [128, 2, 8, 512], F16) as aft1, \
         nc.sbuf_tensor("aost1", [128, 512], F32) as aost1:
        for c in range(8):
            kw = ("wot", c % 2)
            E("sync", lambda e, c=c: e.dma_start(
                out=wot0[:, c % 2, :], in_=wo1[c * 128:(c + 1) * 128, :]),
              writes=[kw], war=[kw], dma=True, tag="ap1:ldw")
            E("vector", lambda e, c=c: e.tensor_copy(wosp[:, 0, c, :], wot0[:, c % 2, :]),
              reads=[kw], writes=[("woh", c)], tag="ap1:hi")
            E("vector", lambda e, c=c: e.tensor_tensor(
                wosp[:, 1, c, :], wot0[:, c % 2, :], wosp[:, 0, c, :], ALU.subtract),
              reads=[kw, ("woh", c)], writes=[("wol", c)], tag="ap1:lo")
        for s2 in range(2):
            for tt in range(4):
                col0 = s2 * SEQ + tt * 512
                for sl, srct in ((0, s_afmh), (1, s_afml)):
                    E("sync", lambda e, sl=sl, srct=srct, col0=col0: e.dma_start(
                        out=aft1[:, sl, :, :],
                        in_=srct[:, col0:col0 + 512].rearrange("(c p) t -> p c t", p=128)),
                      reads=[("safm1", sh2, sl) for sh2 in range(32)],
                      writes=[("aft1", sl)], war=[("aft1", sl)], dma=True,
                      tag="ap1:lda")
                for co in range(8):
                    bank = co % 2
                    for c in range(8):
                        for p3, (wa, ab) in enumerate(((0, 0), (0, 1), (1, 0))):
                            first = (c == 0 and p3 == 0)
                            lastm = (c == 7 and p3 == 2)
                            E("tensor", lambda e, co=co, c=c, wa=wa, ab=ab, bank=bank,
                              first=first, lastm=lastm:
                              e.matmul(pb[bank][:, 0:512],
                                       wosp[:, wa, c, co * 128:(co + 1) * 128],
                                       aft1[:, ab, c, :], start=first, stop=lastm),
                              reads=[("woh", c), ("wol", c), ("aft1", 0), ("aft1", 1)],
                              writes=[("pb", bank)] if lastm else (),
                              war=[("pb", bank)] if first else (), tag="ap1:mm")
                    E("vector", lambda e, bank=bank: e.tensor_copy(
                        aost1[:, :], pb[bank][:, 0:512]),
                      reads=[("pb", bank)], writes=["aost1"], tag="ap1:ev")
                    E("gpsimd", lambda e, co=co, col0=col0: e.dma_start(
                        out=s_attn0[co * 128:(co + 1) * 128, col0:col0 + 512],
                        in_=aost1[:, :]),
                      reads=["aost1"], writes=[("sat1", co, col0)], dma=True,
                      tag="ap1:st")
        b.barrier()

    def w1_quant(wsrc1, wq_f16, kp, ncol, cc0):
        nrc2 = int(wsrc1.shape[0]) // 128
        with nc.sbuf_tensor("wq_t" + kp, [128, 2, ncol], F32) as wqt, \
             nc.sbuf_tensor("wq_s" + kp, [128, 33], F32) as wqs, \
             nc.sbuf_tensor("wq_i" + kp, [128, ncol], I32) as wqi:
            for rc in range(nrc2):
                kw = (kp, "w", rc % 2)
                E("sync", lambda e, rc=rc: e.dma_start(
                    out=wqt[:, rc % 2, :], in_=wsrc1[rc * 128:(rc + 1) * 128, :]),
                  writes=[kw], war=[kw], dma=True, tag="f1:lw")
                E("vector", lambda e, rc=rc: e.reduce_sum(
                    wqs[:, rc:rc + 1], wqt[:, rc % 2, :], axis=AX.X,
                    apply_absolute_value=True),
                  reads=[kw], writes=[(kp, "s", rc)], tag="f1:asum")
            E("vector", lambda e, nrc2=nrc2: e.reduce_sum(
                wqs[:, nrc2:nrc2 + 1], wqs[:, 0:nrc2], axis=AX.X),
              reads=[(kp, "s", rc2) for rc2 in range(nrc2)], writes=[(kp, "s2")],
              tag="f1:asum2")
            E("tensor", lambda e, nrc2=nrc2: e.transpose(
                pb[6][0:1, 0:128], wqs[:, nrc2:nrc2 + 1], ident[:, :]),
              reads=[(kp, "s2"), "ident"], writes=[("pb", 6)], tag="f1:str")
            E("vector", lambda e: e.reduce_sum(gmax[0:1, 24:25], pb[6][0:1, 0:128],
                                               axis=AX.X),
              reads=[("pb", 6)], writes=[(kp, "tot")], tag="f1:tot")
            E("vector", lambda e, nrc2=nrc2, ncol=ncol: e.tensor_scalar(
                gmax[0:1, 25:26], gmax[0:1, 24:25], 1.0 / (nrc2 * 128 * ncol), 1e-5,
                ALU.mult, ALU.max),
              reads=[(kp, "tot")], writes=[(kp, "mean")], tag="f1:mean")
            E("vector", lambda e: e.reciprocal(gmax[0:1, 26:27], gmax[0:1, 25:26]),
              reads=[(kp, "mean")], writes=[(kp, "sw")], tag="f1:sw")
            bcast_cells(gmax[0:1, 25:27], [(kp, "mean"), (kp, "sw")],
                        cells[:, cc0:cc0 + 2], (kp, "swc"), 2)
            for rc in range(nrc2):
                kw = (kp, "w", rc % 2)
                E("sync", lambda e, rc=rc: e.dma_start(
                    out=wqt[:, rc % 2, :], in_=wsrc1[rc * 128:(rc + 1) * 128, :]),
                  writes=[kw], war=[kw], dma=True, tag="f1:lw2")
                E("vector", lambda e, rc=rc: e.tensor_scalar(
                    wqi[:, :], wqt[:, rc % 2, :], cells[:, cc0 + 1:cc0 + 2], -1.0,
                    ALU.mult, ALU.max),
                  reads=[kw, (kp, "swc")], writes=[(kp, "i")], tag="f1:q1")
                E("vector", lambda e: e.tensor_scalar(
                    wqi[:, :], wqi[:, :], 1.0, None, ALU.min),
                  reads=[(kp, "i")], writes=[(kp, "i")], tag="f1:q2")
                E("vector", lambda e, rc=rc: e.tensor_copy(
                    wq_f16[:, rc, :], wqi[:, :]),
                  reads=[(kp, "i")], writes=[(kp, "f", rc)], tag="f1:qf")

    b.barrier()
    f1scope = contextlib.ExitStack()
    w1g_f = f1scope.enter_context(nc.sbuf_tensor("w1g_f", [128, 8, DFF1], F16))
    w1u_f = f1scope.enter_context(nc.sbuf_tensor("w1u_f", [128, 8, DFF1], F16))
    w1_quant(w1gT, w1g_f, "f1g", DFF1, 12)
    w1_quant(w1uT, w1u_f, "f1u", DFF1, 14)
    b.barrier()
    if "w1g" in dbg_out:
        E("sync", lambda e: e.dma_start(
            out=dbg_out["w1g"][:, :],
            in_=w1g_f[:, :, :].rearrange("p c o -> p (c o)")),
          writes=["dbgw1g"], dma=True, tag="dbg:w1g")
        E("sync", lambda e: e.dma_start(out=dbg_out["gm2"][:, :], in_=gmax[0:1, 0:32]),
          writes=["dbggm2"], dma=True, tag="dbg:gm2")
        b.barrier()

    dram_scr("s_guard1", [256, 1024], F32)
    s_oq = dram_scr("s_oq", [8, 128, 32, 512], F16)
    dram_scr("s_guard2", [256, 1024], F32)
    with nc.sbuf_tensor("f1xq", [128, 8, 128], F16) as f1xq, \
         nc.sbuf_tensor("f1gt", [128, 512], F32) as f1gt, \
         nc.sbuf_tensor("f1gu", [128, DFF1], F32) as f1gu, \
         nc.sbuf_tensor("f1up", [128, 512], F32) as f1up, \
         nc.sbuf_tensor("f1mc", [128, 1], F32) as f1mc, \
         nc.sbuf_tensor("f1mx", [128, 1], F32) as f1mx, \
         nc.sbuf_tensor("f1sc", [128, 4], F32) as f1sc, \
         nc.sbuf_tensor("f1qi", [128, 512], I32) as f1qi, \
         nc.sbuf_tensor("f1of", [128, 512], F16) as f1of, \
         nc.sbuf_tensor("f1ost", [128, 4, 128], F16) as f1ost:
        for t in range(NT):
            E("sync", lambda e, t=t: e.dma_start(
                out=f1xq[:, :, :],
                in_=s_xqhi[:, t * 128:(t + 1) * 128]
                    .rearrange("(c p) t2 -> p c t2", p=128)),
              reads=[("sxq1", c2, t // 4, t % 4) for c2 in range(8)],
              writes=["f1xq"], war=["f1xq"], dma=True, tag="f1:ldxq")
            E("vector", lambda e, t=t: e.tensor_tensor(
                f1sc[:, 0:1], rsff[:, t:t + 1], cells[:, 12:13], ALU.mult),
              reads=[("rs", t), ("f1g", "swc")], writes=["f1s0"], tag="f1:s0")
            E("vector", lambda e, t=t: e.tensor_tensor(
                f1sc[:, 1:2], rsff[:, t:t + 1], cells[:, 14:15], ALU.mult),
              reads=[("rs", t), ("f1u", "swc")], writes=["f1s1"], tag="f1:s1")
            E("vector", lambda e: e.memset(f1mx[:, :], 0.0),
              writes=["f1mx"], war=["f1mx"], tag="f1:mx0")
            for it in range(8):
                for wi2, wf in ((0, w1g_f), (1, w1u_f)):
                    bank = wi2
                    for c in range(8):
                        E("tensor", lambda e, wf=wf, c=c, it=it, bank=bank:
                          e.matmul(pb[bank][:, 0:512], f1xq[:, c, :],
                                   wf[:, c, it * 512:(it + 1) * 512],
                                   start=(c == 0), stop=(c == 7)),
                          reads=["f1xq"],
                          writes=[("pb", bank)] if c == 7 else (),
                          war=[("pb", bank)] if c == 0 else (), tag="f1:mm")
                E("scalar", lambda e: e.activation(
                    f1gt[:, :], pb[0][:, 0:512], AF.Silu, scale=f1sc[:, 0:1]),
                  reads=[("pb", 0), "f1s0"], writes=["f1gt"], tag="f1:silu")
                E("vector", lambda e: e.tensor_scalar(
                    f1up[:, :], pb[1][:, 0:512], f1sc[:, 1:2], None, ALU.mult),
                  reads=[("pb", 1), "f1s1"], writes=["f1up"], tag="f1:up")
                E("vector", lambda e, it=it: e.tensor_tensor(
                    f1gu[:, it * 512:(it + 1) * 512], f1gt[:, :], f1up[:, :],
                    ALU.mult),
                  reads=["f1gt", "f1up"], writes=[("f1gu", it)], tag="f1:gu")
                E("vector", lambda e, it=it: e.reduce_max(
                    f1mc[:, :], f1gu[:, it * 512:(it + 1) * 512], axis=AX.X,
                    apply_absolute_value=True),
                  reads=[("f1gu", it)], writes=["f1mc"], tag="f1:mx1")
                E("vector", lambda e: e.tensor_tensor(
                    f1mx[:, :], f1mx[:, :], f1mc[:, :], ALU.max),
                  reads=["f1mc", "f1mx"], writes=["f1mx"], tag="f1:mx2")
            E("vector", lambda e: e.tensor_scalar(
                f1sc[:, 2:3], f1mx[:, :], 1e-5, None, ALU.max),
              reads=["f1mx"], writes=["f1cl"], tag="f1:cl")
            E("vector", lambda e, t=t: e.tensor_scalar(
                rso[:, t:t + 1], f1sc[:, 2:3], 1.0 / 127.0, None, ALU.mult),
              reads=["f1cl"], writes=[("rso", t)], tag="f1:rso")
            E("vector", lambda e: e.reciprocal(f1sc[:, 3:4], f1sc[:, 2:3]),
              reads=["f1cl"], writes=["f1r"], tag="f1:rec")
            E("vector", lambda e: e.tensor_scalar(
                f1sc[:, 3:4], f1sc[:, 3:4], 127.0, None, ALU.mult),
              reads=["f1r"], writes=["f1r"], tag="f1:sc127")
            for it in range(8):
                E("vector", lambda e, it=it: e.tensor_scalar(
                    f1qi[:, :], f1gu[:, it * 512:(it + 1) * 512], f1sc[:, 3:4],
                    None, ALU.mult),
                  reads=[("f1gu", it), "f1r"], writes=["f1qi"], tag="f1:oq")
                E("vector", lambda e: e.tensor_scalar(
                    f1qi[:, :], f1qi[:, :], -128.0, 127.0, ALU.max, ALU.min),
                  reads=["f1qi"], writes=["f1qi"], tag="f1:ocl")
                E("vector", lambda e: e.tensor_copy(f1of[:, :], f1qi[:, :]),
                  reads=["f1qi"], writes=["f1of"], tag="f1:of")
                for sub in range(4):
                    bank = 2 + sub % 2
                    E("tensor", lambda e, sub=sub, bank=bank: e.transpose(
                        pb[bank][:, :].bitcast(F16)[:, 0:128],
                        f1of[:, sub * 128:(sub + 1) * 128],
                        ident16[:, :]),
                      reads=["f1of", "ident16"], writes=[("pb", bank)],
                      tag="f1:otr")
                    E("vector", lambda e, sub=sub, bank=bank: e.tensor_copy(
                        f1ost[:, sub, :], pb[bank][:, :].bitcast(F16)[:, 0:128]),
                      reads=[("pb", bank)], writes=[("f1os", sub)], tag="f1:oev")
                E("sync", lambda e, it=it, t=t: e.dma_start(
                    out=s_oq[t // 4, :, it * 4:(it + 1) * 4,
                             (t % 4) * 128:(t % 4 + 1) * 128],
                    in_=f1ost[:, :, :]),
                  reads=[("f1os", sub2) for sub2 in range(4)],
                  writes=[("soq", it, t)], dma=True, tag="f1:ost")
        b.barrier()
    f1scope.close()

    f1bscope = contextlib.ExitStack()
    w1d_f = f1bscope.enter_context(nc.sbuf_tensor("w1d_f", [128, 32, D], F16))
    rsbc = f1bscope.enter_context(nc.sbuf_tensor("rsbc", [128, NTOK], F32))
    w1_quant(w1dT, w1d_f, "f1d", D, 16)
    b.barrier()
    with nc.sbuf_tensor("rsr", [1, 128], F32) as rsr:
        for t in range(NT):
            E("tensor", lambda e, t=t: e.transpose(
                pb[6][0:1, 0:128], rso[:, t:t + 1], ident[:, :]),
              reads=[("rso", t), "ident"], writes=[("pb", 6)], tag="f1:rstr")
            E("vector", lambda e: e.tensor_scalar(
                rsr[:, :], pb[6][0:1, 0:128], cells[0:1, 16:17], None, ALU.mult),
              reads=[("pb", 6), ("f1d", "swc")], writes=["rsr"], tag="f1:rsev1")
            E("tensor", lambda e: e.matmul(
                pb[7][:, 0:128], ones_row[:, :], rsr[0:1, :], start=True, stop=True),
              reads=["rsr", "ones_row"], writes=[("pb", 7)], tag="f1:rsmm")
            E("vector", lambda e, t=t: e.tensor_copy(
                rsbc[:, t * 128:(t + 1) * 128], pb[7][:, 0:128]),
              reads=[("pb", 7)], writes=[("rsbc", t // 4)], tag="f1:rsev2")
    with nc.sbuf_tensor("f1pad", [128, 2048], F32) as f1pad, \
         nc.sbuf_tensor("f1oq", [128, 32, 512], F16) as f1oq, \
         nc.sbuf_tensor("f1fo", [128, 512], F32) as f1fo, \
         nc.sbuf_tensor("f1fo2", [128, 512], F32) as f1fo2:
        for sup in range(8):
            for qt in range(4):
                E("sync", lambda e, sup=sup, qt=qt: e.dma_start(
                    out=f1oq[:, qt * 8:(qt + 1) * 8, :],
                    in_=s_oq[sup, :, qt * 8:(qt + 1) * 8, :]),
                  reads=[("soq", it2, t2) for it2 in range(qt * 2, qt * 2 + 2)
                         for t2 in range(sup * 4, sup * 4 + 4)],
                  writes=[("f1oq", qt)], war=[("f1oq", qt)], dma=True,
                  tag="f1:ldoq")
            b.barrier()
            if "oqs" in dbg_out:
                E("gpsimd", lambda e, sup=sup: e.dma_start(
                    out=dbg_out["oqs"][sup, :, :, :], in_=f1oq[:, :, :]),
                  reads=[("f1oq", 0)], writes=[("dbgoqs", sup)], dma=True,
                  tag="dbg:oqs")
                b.barrier()
            for hc in range(8):
                bank = hc % 2
                for kc in range(32):
                    E("tensor", lambda e, kc=kc, hc=hc, bank=bank: e.matmul(
                        pb[bank][:, 0:512], w1d_f[:, kc, hc * 128:(hc + 1) * 128],
                        f1oq[:, kc, :], start=(kc == 0), stop=(kc == 31)),
                      reads=[("f1oq", kc // 8)],
                      writes=[("pb", bank)] if kc == 31 else (),
                      war=[("pb", bank)] if kc == 0 else (), tag="f1:dmm")
                if "fraw" in dbg_out:
                    E("scalar", lambda e, hc=hc, sup=sup, bank=bank: e.copy(
                        f1fo2[:, :], pb[bank][:, 0:512]),
                      reads=[("pb", bank)], writes=["f1fo2"], tag="f1:devr")
                    E("gpsimd", lambda e, hc=hc, sup=sup: e.dma_start(
                        out=dbg_out["fraw"][hc * 128:(hc + 1) * 128,
                                            sup * 512:(sup + 1) * 512],
                        in_=f1fo2[:, :]),
                      reads=["f1fo2"], writes=[("dbgfr", hc, sup)], dma=True,
                      tag="f1:devrst")
                E("vector", lambda e, hc=hc, sup=sup, bank=bank: e.scalar_tensor_tensor(
                    f1fo[:, :], pb[bank][:, 0:512], 1.0,
                    rsbc[:, sup * 512:(sup + 1) * 512], ALU.mult, ALU.mult),
                  reads=[("pb", bank), ("rsbc", sup)], writes=["f1fo"], tag="f1:dev")
                E("gpsimd", lambda e, hc=hc, sup=sup: e.dma_start(
                    out=s_ffn0[hc * 128:(hc + 1) * 128, sup * 512:(sup + 1) * 512],
                    in_=f1fo[:, :]),
                  reads=["f1fo"], writes=[("sff1", hc, sup)], dma=True, tag="f1:dst")
            b.barrier()
        b.barrier()
    f1bscope.close()

    with nc.sbuf_tensor("p9x", [128, 2, D], F32) as p9x, \
         nc.sbuf_tensor("p9a", [128, 2, 8, 128], F32) as p9a, \
         nc.sbuf_tensor("p9f", [128, 2, 8, 128], F32) as p9f, \
         nc.sbuf_tensor("p9t", [128, 128], F32) as p9t, \
         nc.sbuf_tensor("p9y", [128, 2, D], F32) as p9y:
        for t in range(NT):
            par = t % 2
            kx = ("p9x", par)
            ka = ("p9a", par)
            kf = ("p9f", par)
            ky = ("p9y", par)
            E("sync", lambda e, t=t, par=par: e.dma_start(
                out=p9x[:, par, :], in_=s_x1[t * 128:(t + 1) * 128, :]),
              reads=[("sx1", t)], writes=[kx], war=[kx], dma=True, tag="p9:ldx")
            E("sync", lambda e, t=t, par=par: e.dma_start(
                out=p9a[:, par, :, :],
                in_=s_attn0[:, t * 128:(t + 1) * 128]
                .rearrange("(c p) t2 -> p c t2", p=128)),
              reads=[("sat1", co, (t // 4) * 512) for co in range(8)],
              writes=[ka], war=[ka], dma=True, tag="p9:lda")
            E("sync", lambda e, t=t, par=par: e.dma_start(
                out=p9f[:, par, :, :],
                in_=s_ffn0[:, t * 128:(t + 1) * 128]
                .rearrange("(c p) t2 -> p c t2", p=128)),
              reads=[("sff1", hc2, t // 4) for hc2 in range(8)],
              writes=[kf], war=[kf], dma=True, tag="p9:ldf")
            for c in range(8):
                bank = c % 2
                bank2 = 2 + c % 2
                E("tensor", lambda e, c=c, par=par, bank=bank: e.transpose(
                    pb[bank][:, 0:128], p9a[:, par, c, :], ident[:, :]),
                  reads=[ka, "ident"], writes=[("pb", bank)], tag="p9:tra")
                E("tensor", lambda e, c=c, par=par, bank2=bank2: e.transpose(
                    pb[bank2][:, 0:128], p9f[:, par, c, :], ident[:, :]),
                  reads=[kf, "ident"], writes=[("pb", bank2)], tag="p9:trf")
                E("vector", lambda e, c=c, par=par, bank=bank: e.tensor_tensor(
                    p9t[:, :], pb[bank][:, 0:128],
                    p9x[:, par, c * 128:(c + 1) * 128], ALU.add),
                  reads=[("pb", bank), kx], writes=["p9t"], tag="p9:add1")
                E("vector", lambda e, c=c, par=par, bank2=bank2: e.tensor_tensor(
                    p9y[:, par, c * 128:(c + 1) * 128], pb[bank2][:, 0:128],
                    p9t[:, :], ALU.add),
                  reads=[("pb", bank2), "p9t"], writes=[ky], tag="p9:add2")
            E("sync", lambda e, t=t, par=par: e.dma_start(
                out=y_out[t * 128:(t + 1) * 128, :], in_=p9y[:, par, :]),
              reads=[ky], writes=[("y", t)], dma=True, tag="p9:sty")
        b.barrier()

    dbg_keys = [("y", t) for t in range(NT)]
    if "cols" in dbg_out:
        with nc.sbuf_tensor("dbgcols", [128, 4 * NT], F32) as dbgc:
            E("vector", lambda e: e.tensor_copy(dbgc[:, 0:NT], rsff[:, :]), tag="dbg")
            E("vector", lambda e: e.tensor_copy(dbgc[:, NT:2 * NT], rso[:, :]), tag="dbg")
            E("vector", lambda e: e.tensor_copy(dbgc[:, 2 * NT:3 * NT], stats1_m[:, :]), tag="dbg")
            E("vector", lambda e: e.tensor_copy(dbgc[:, 3 * NT:4 * NT], stats1_r[:, :]), tag="dbg")
            E("sync", lambda e: e.dma_start(out=dbg_out["cols"][:, :], in_=dbgc[:, :]),
              writes=["dbgcols"], dma=True, tag="dbg")
            dbg_keys.append("dbgcols")
    srcmap = {"xqhi": s_xqhi, "xqlo": s_xqlo, "qhi": s_qhi, "qlo": s_qlo,
              "khi": s_khi, "klo": s_klo, "v": s_v, "attn0": s_attn0,
              "ffn0": s_ffn0, "x1": s_x1, "sa": s_a, "sb": s_b}
    for name in dbg_out:
        if name in srcmap:
            E("sync", lambda e, d=dbg_out[name], s=srcmap[name]:
              e.dma_start(out=d[:, :], in_=s[:, :]),
              writes=["dbg" + name], dma=True, tag="tail:dbg")
            dbg_keys.append("dbg" + name)
    E("sync", None, reads=dbg_keys, tag="tail:wait")

    b.finalize()
    st.close()
    return nc, dbg_out


def kernel(**inputs):
    shards, com = _prep_inputs(inputs)
    nc, _ = build_nc()
    in_maps = []
    for c in range(NCORE):
        m = {"x": np.ascontiguousarray(shards[c])}
        m.update(com)
        in_maps.append(m)
    res = run_bass_kernel_spmd(nc, in_maps, core_ids=list(range(NCORE)))
    out = np.concatenate([r["y"] for r in res.results], axis=0)
    return out.reshape(16, SEQ, D).astype(np.float32)
